# version 07
# baseline (speedup 1.0000x reference)
import numpy as np
import ml_dtypes
from contextlib import ExitStack

import concourse.bass as bass
import concourse.mybir as mybir
from concourse.bass_utils import run_bass_kernel_spmd

F32 = mybir.dt.float32
BF16 = mybir.dt.bfloat16
FP8 = mybir.dt.float8e4
I16 = mybir.dt.int16
AF = mybir.ActivationFunctionType
ALU = mybir.AluOpType
AX = mybir.AxisListType
BF = ml_dtypes.bfloat16

B, L, D, HID = 4, 8192, 1024, 4096
N1 = N2 = 128
NFFT = N1 * N2
J1 = 64
CHQ, CHH = 256, 128
TOK, TCH = 4096, 128
NCHUNK = TOK // TCH
MAGIC = float(np.float32(1.5 * 2 ** 23))
PHASE = 4
RG_QUAD = [[0, 1, 2, 3], [4, 5, 6, 7]]
ENGS = ("sync", "scalar", "vector", "tensor", "gpsimd")


def _tables():
    w = np.exp(-2j * np.pi / NFFT)
    w1 = np.exp(-2j * np.pi / N1)
    j1 = np.arange(J1); k1 = np.arange(N1); j2 = np.arange(N2); k2 = np.arange(N2)
    D1 = w1 ** (j1[:, None] * k1[None, :])
    D2 = w1 ** (j2[:, None] * k2[None, :])
    T = w ** (j2[:, None] * k1[None, :])
    cD1 = np.conj(w1 ** (k1[:, None] * j1[None, :]))
    t = {
        "tD1re": D1.real, "tD1im": D1.imag, "tD1imN": -D1.imag,
        "tD2re": D2.real, "tD2im": D2.imag, "tD2imN": -D2.imag,
        "tTre": T.real, "tTim": T.imag,
        "tTI1": cD1.real, "tTI2": cD1.imag, "tTI2N": -cD1.imag,
    }
    out = {kk: np.ascontiguousarray(v.astype(BF)) for kk, v in t.items()}
    out["ident_bf"] = np.eye(128, dtype=BF)
    out["ident_f32"] = np.eye(128, dtype=np.float32)
    return out


def shard_inputs(u, k, skip, w1, cw, cb, w2):
    tabs = _tables()
    cw2 = np.ascontiguousarray(np.asarray(cw, dtype=np.float32).reshape(HID, 3))
    cw_cols = np.ascontiguousarray(cw2.reshape(32, 128, 3).transpose(1, 0, 2))
    cb_cols = np.ascontiguousarray(
        np.asarray(cb, dtype=np.float32).reshape(32, 128).T)
    u = np.asarray(u, dtype=np.float32)
    k = np.asarray(k, dtype=np.float32)
    skip = np.asarray(skip, dtype=np.float32)
    w1 = np.ascontiguousarray(np.asarray(w1, dtype=np.float32))
    w2 = np.ascontiguousarray(np.asarray(w2, dtype=np.float32))
    maps = []
    for r in range(8):
        Q, dq = r // 4, r % 4
        bq, th = r // 2, r % 2
        c0 = CHQ * dq
        m = {
            "u_ffn": np.ascontiguousarray(u[bq, th * TOK:(th + 1) * TOK, :]),
            "u_halo": np.ascontiguousarray(u[bq, TOK - 1:TOK + 1, :]),
            "halo_mask": np.array([[1.0 if th == 1 else 0.0,
                                    1.0 if th == 0 else 0.0]], dtype=np.float32),
            "u_conv": np.ascontiguousarray(u[2 * Q:2 * Q + 2, :, c0:c0 + CHQ]),
            "k_fold": np.ascontiguousarray(
                k[c0:c0 + CHQ, :].reshape(CHQ, J1, N2).transpose(1, 0, 2)),
            "skip_s": np.ascontiguousarray(skip[c0:c0 + CHQ].reshape(1, CHQ)),
            "w1": w1, "w2": w2, "cw_cols": cw_cols, "cb_cols": cb_cols,
        }
        m.update(tabs)
        maps.append(m)
    return maps


class Sem:
    def __init__(self, stk, nc, name):
        self.h = stk.enter_context(nc.semaphore(name))
        self.v = 0
        self.name = name


class Emit:
    def __init__(self):
        self.prog = {e: [] for e in ENGS}

    def op(self, eng, fn, waits=(), inc=None):
        ev = None
        hinc = None
        if inc is not None:
            sem, amt = inc
            sem.v += amt
            ev = (sem, sem.v)
            hinc = (sem.h, amt)
        ws = []
        for wv in waits:
            if wv is None:
                continue
            s, v = wv
            ws.append((s.h, v))
        self.prog[eng].append((ws, fn, hinc))
        return ev

    def replay(self, block):
        def make(name):
            items = self.prog[name]

            def body(eng):
                for waits, fn, inc in items:
                    for sem, val in waits:
                        eng.wait_ge(sem, val)
                    if fn is not None:
                        ins = fn(eng)
                        if inc is not None:
                            ins.then_inc(inc[0], inc[1])
            return body

        for name in ENGS:
            getattr(block, name)(make(name))


class Builder:
    def build(self):
        nc = bass.Bass("TRN2", target_bir_lowering=False)
        self.nc = nc
        em = Emit()
        self.em = em

        u_ffn = nc.dram_tensor("u_ffn", [TOK, D], F32, kind="ExternalInput").ap()
        u_halo = nc.dram_tensor("u_halo", [2, D], F32, kind="ExternalInput").ap()
        halo_mask = nc.dram_tensor("halo_mask", [1, 2], F32, kind="ExternalInput").ap()
        u_conv = nc.dram_tensor("u_conv", [2, L, CHQ], F32, kind="ExternalInput").ap()
        k_fold = nc.dram_tensor("k_fold", [J1, CHQ, N2], F32, kind="ExternalInput").ap()
        skip_s = nc.dram_tensor("skip_s", [1, CHQ], F32, kind="ExternalInput").ap()
        w1_d = nc.dram_tensor("w1", [HID, D], F32, kind="ExternalInput").ap()
        w2_d = nc.dram_tensor("w2", [D, HID], F32, kind="ExternalInput").ap()
        cwc_d = nc.dram_tensor("cw_cols", [128, 32, 3], F32, kind="ExternalInput").ap()
        cbc_d = nc.dram_tensor("cb_cols", [128, 32], F32, kind="ExternalInput").ap()
        tabd = {}
        for name in ("tD1re", "tD1im", "tD1imN"):
            tabd[name] = nc.dram_tensor(name, [J1, N1], BF16, kind="ExternalInput").ap()
        for name in ("tD2re", "tD2im", "tD2imN", "tTre", "tTim"):
            tabd[name] = nc.dram_tensor(name, [N2, N2], BF16, kind="ExternalInput").ap()
        for name in ("tTI1", "tTI2", "tTI2N"):
            tabd[name] = nc.dram_tensor(name, [N1, J1], BF16, kind="ExternalInput").ap()
        ident_bf_d = nc.dram_tensor("ident_bf", [128, 128], BF16, kind="ExternalInput").ap()
        ident_f_d = nc.dram_tensor("ident_f32", [128, 128], F32, kind="ExternalInput").ap()
        o_out = nc.dram_tensor("o_out", [TOK, D], F32, kind="ExternalOutput").ap()

        cin_stats = nc.dram_tensor("cin_stats", [J1, 128], F32, kind="Internal").ap()
        cag_stats = nc.dram_tensor("cag_stats", [4, J1, 128], F32, kind="Internal").ap()
        cin_s = nc.dram_tensor("cin_s", [2, 2, 128, J1, CHH], BF16,
                               kind="Internal").ap()
        cag_s = nc.dram_tensor("cag_s", [2, 4, 2, 128, J1, CHH], BF16,
                               kind="Internal").ap()
        d_scr1 = nc.dram_tensor("d_scr1", [128, 32], F32, kind="Internal").ap()
        d_scr2 = nc.dram_tensor("d_scr2", [128, 8], F32, kind="Internal").ap()
        d_row = nc.dram_tensor("d_row", [1, 16], F32, kind="Internal").ap()
        d_row2 = nc.dram_tensor("d_row2", [2, 2], F32, kind="Internal").ap()

        stk = ExitStack()
        with stk:
            SBT = lambda name, shape, dt=F32: stk.enter_context(
                nc.sbuf_tensor(name, shape, dt))
            utile = [SBT(f"utile{i}", [128, D]) for i in range(2)]
            xnf = SBT("xnf", [128, D])
            ftmp = SBT("ftmp", [128, TCH])
            m1f = xnf
            m1bb = [SBT(f"m1b{i}", [128, D], BF16) for i in range(2)]
            stats_sb = SBT("stats_sb", [128, J1])
            amax1c_sb = SBT("amax1c_sb", [128, 32])
            s1_sb = SBT("s1_sb", [128, 32])
            tb = {}
            for name, shape in (
                ("tD1re", [J1, N1]), ("tD1im", [J1, N1]), ("tD1imN", [J1, N1]),
                ("tD2re", [N2, N2]), ("tD2im", [N2, N2]), ("tD2imN", [N2, N2]),
                ("tTre", [N2, N2]), ("tTim", [N2, N2]),
                ("tTI1", [N1, J1]), ("tTI2", [N1, J1]), ("tTI2N", [N1, J1]),
            ):
                tb[name] = SBT("s" + name, shape, BF16)
            ident_bf = SBT("sident_bf", [128, 128], BF16)
            ident_f = SBT("sident_f", [128, 128])
            w1qT = SBT("w1qT", [128, 8 * HID], FP8)
            w2qT = SBT("w2qT", [128, 32 * D], FP8)
            scal_bc = SBT("scal_bc", [128, 4])
            cwcols = SBT("cwcols", [128, 96])
            cbcols = SBT("cbcols", [128, 32])
            iomask = SBT("iomask", [128, 128], I16)
            diagmask = SBT("diagmask", [128, 128], BF16)
            onesbf = SBT("onesbf", [1, 128], BF16)
            epscol = SBT("epscol", [128, 1])
            halocol = SBT("halocol", [128, 16], BF16)
            halosc = SBT("halosc", [2, 8])
            hrow = SBT("hrow", [1, 2])
            hmask = SBT("hmask", [2, 1])
            smallf = SBT("smallf", [128, 8])
            srow = SBT("srow", [2, 256])
            statsT = [SBT(f"statsT{i}", [128, J1]) for i in range(4)]
            statg = SBT("statg", [J1, 128])
            skipbc = SBT("skipbc", [128, CHQ])
            haloxn = SBT("haloxn", [2, D])
            halom = SBT("halom", [2, D], BF16)
            halotmp = SBT("halotmp", [2, D])
            ctmp = SBT("ctmp", [N2, 8 * N1])

            PSB = lambda name, dt=F32, n=512: stk.enter_context(
                nc.psum_tensor(name, [128, n], dt))
            psb = [PSB(f"psb{i}") for i in range(7)]
            psTb = PSB("psTb", BF16, 1024)

            qS = Sem(stk, nc, "qS")
            qP = Sem(stk, nc, "qP")
            qG = Sem(stk, nc, "qG")
            qU = [Sem(stk, nc, "qU0"), Sem(stk, nc, "qU1")]
            qT2 = Sem(stk, nc, "qT2")
            qX1 = Sem(stk, nc, "qX1")
            qX23 = [Sem(stk, nc, "qX2"), Sem(stk, nc, "qX3")]
            qM = [Sem(stk, nc, "qM0"), Sem(stk, nc, "qM1")]
            qO = [Sem(stk, nc, "qO0"), Sem(stk, nc, "qO1")]
            qK = Sem(stk, nc, "qK")
            qL = [Sem(stk, nc, "qL0"), Sem(stk, nc, "qL1")]
            tS = Sem(stk, nc, "tS")
            vS = Sem(stk, nc, "vS")
            sS = Sem(stk, nc, "sS")
            gS = Sem(stk, nc, "gS")
            cS = Sem(stk, nc, "cS")

            def dma(eng, q, out, in_, waits=()):
                return em.op(eng, lambda e, o=out, i=in_: e.dma_start(out=o, in_=i),
                             waits=waits, inc=(q, 16))

            def vop(fn, waits=()):
                return em.op("vector", fn, waits=waits, inc=(vS, 1))

            def sop(fn, waits=()):
                return em.op("scalar", fn, waits=waits, inc=(sS, 1))

            def top(fn, waits=(), inc=True):
                return em.op("tensor", fn, waits=waits,
                             inc=(tS, 1) if inc else None)

            def gop(fn, waits=()):
                return em.op("gpsimd", fn, waits=waits, inc=(gS, 1))

            def vdrain():
                em.op("vector", lambda e: e.drain())

            def gdrain():
                em.op("gpsimd", lambda e: e.drain())

            def cmul_psum(ps, out_re, out_im, par, waits, conj=False):
                o = par * 4 * N1
                v1 = vop(lambda e: e.tensor_tensor(
                    ctmp[:, o:o + N1], ps[:, 0:128], tb["tTre"][:], ALU.mult),
                    waits=waits)
                vop(lambda e: e.tensor_tensor(
                    ctmp[:, o + N1:o + 2 * N1], ps[:, 128:256], tb["tTim"][:],
                    ALU.mult))
                vop(lambda e: e.tensor_tensor(
                    ctmp[:, o + 2 * N1:o + 3 * N1], ps[:, 0:128], tb["tTim"][:],
                    ALU.mult))
                v4 = vop(lambda e: e.tensor_tensor(
                    ctmp[:, o + 3 * N1:o + 4 * N1], ps[:, 128:256], tb["tTre"][:],
                    ALU.mult))
                vdrain()
                if not conj:
                    gop(lambda e: e.tensor_tensor(
                        out_re, ctmp[:, o:o + N1], ctmp[:, o + N1:o + 2 * N1],
                        ALU.subtract), waits=[v4])
                    ev = gop(lambda e: e.tensor_tensor(
                        out_im, ctmp[:, o + 2 * N1:o + 3 * N1],
                        ctmp[:, o + 3 * N1:o + 4 * N1], ALU.add))
                else:
                    gop(lambda e: e.tensor_tensor(
                        out_re, ctmp[:, o:o + N1], ctmp[:, o + N1:o + 2 * N1],
                        ALU.add), waits=[v4])
                    ev = gop(lambda e: e.tensor_tensor(
                        out_im, ctmp[:, o + 3 * N1:o + 4 * N1],
                        ctmp[:, o + 2 * N1:o + 3 * N1], ALU.subtract))
                gdrain()
                return v4, ev

            for name in tb:
                dma("sync", qS, tb[name][:], tabd[name])
            dma("sync", qS, ident_bf[:], ident_bf_d)
            dma("sync", qS, ident_f[:], ident_f_d)
            dma("sync", qS, cwcols[:], cwc_d.rearrange("p a b -> p (a b)"))
            dma("sync", qS, cbcols[:], cbc_d)
            dma("sync", qS, skipbc[:], skip_s.broadcast_to((128, CHQ)))
            dma("sync", qS, hmask[:], halo_mask.rearrange("o t -> t o"))
            ev_tab = dma("sync", qS, haloxn[:], u_halo)
            ev_ones = vop(lambda e: e.memset(onesbf[:], 1.0), waits=[ev_tab])
            ev_eps = vop(lambda e: e.memset(epscol[:], 1e-5))
            ev_iota = gop(lambda e: e.iota(iomask[:], pattern=[[1, 128]], base=0,
                                           channel_multiplier=-1))
            ev_dmask = vop(lambda e: e.tensor_scalar(
                diagmask[:], iomask[:], 0.0, None, ALU.is_equal), waits=[ev_iota])
            vdrain()

            if PHASE < 1:
                allq = [qS, qP, qG, qT2, qX1, qK] + qU + qX23 + qM + qO + qL
                em.op("sync", None, waits=[(q, q.v) for q in allq] + [(cS, cS.v)])
                block = stk.enter_context(nc.Block())
                em.replay(block)
                return nc
            wstk = ExitStack()
            WSBT = lambda name, shape, dt=F32: wstk.enter_context(
                nc.sbuf_tensor(name, shape, dt))
            wt = [WSBT(f"wt{i}", [128, HID]) for i in range(2)]
            wabs = WSBT("wabs", [128, HID])
            wq_b = WSBT("wq_b", [128, HID], BF16)
            absacc = WSBT("absacc", [128, 48])

            specs = [(w1_d, 32, 0, D), (w2_d, 8, 32, HID)]
            ev_abs = []
            for wd, ntile, coff, width in specs:
                wr = wd.rearrange("(n p) d -> n p d", p=128)
                for i in range(ntile):
                    waits = [ev_abs[-2]] if len(ev_abs) >= 2 else [ev_tab]
                    ev_l = dma("sync", qU[(coff + i) % 2],
                               wt[(coff + i) % 2][:, 0:width], wr[i],
                               waits=waits)
                    ev_a = sop(lambda e, j=(coff + i) % 2, c=coff + i, w=width:
                               e.activation(wabs[:, 0:w], wt[j][:, 0:w], AF.Abs,
                                            accum_out=absacc[:, c:c + 1]),
                               waits=[ev_l])
                    em.op("scalar", lambda e: e.drain())
                    ev_abs.append(ev_a)
            if PHASE < 1.2:
                allq = [qS, qP, qG, qT2, qX1, qK] + qU + qX23 + qM + qO + qL
                em.op("sync", None, waits=[(q, q.v) for q in allq] + [(cS, cS.v)])
                wstk.close()
                block = stk.enter_context(nc.Block())
                em.replay(block)
                return nc
            ev_c1 = vop(lambda e: e.tensor_reduce(
                smallf[:, 0:1], absacc[:, 0:32], AX.X, ALU.add),
                waits=[ev_abs[-1]])
            ev_c2 = vop(lambda e: e.tensor_reduce(
                smallf[:, 1:2], absacc[:, 32:40], AX.X, ALU.add))
            vdrain()
            ev_ct = top(lambda e: e.transpose(
                psb[6][0:2, 0:128], smallf[:, 0:2], ident_f[:]),
                waits=[ev_c2])
            ev_cr = vop(lambda e: e.tensor_copy(
                halotmp[0:2, 0:128], psb[6][0:2, 0:128]), waits=[ev_ct])
            vdrain()
            ev_s1 = vop(lambda e: e.tensor_reduce(
                smallf[0:2, 0:1], halotmp[0:2, 0:128], AX.X, ALU.add))
            vdrain()
            vop(lambda e: e.tensor_scalar(
                smallf[0:2, 2:3], smallf[0:2, 0:1], 1.0 / (HID * D), 1e-5,
                ALU.mult, ALU.max))
            vdrain()
            vop(lambda e: e.reciprocal(smallf[0:2, 3:4], smallf[0:2, 2:3]))
            vdrain()
            ev_pk = vop(lambda e: e.tensor_copy(srow[0:2, 0:2],
                                                smallf[0:2, 2:4]))
            ev_sd = dma("scalar", qX1, d_row2, srow[0:2, 0:2], waits=[ev_pk])
            ev_bc = dma("scalar", qX1, scal_bc[:],
                        d_row2.rearrange("a b -> (a b)").unsqueeze(0)
                        .broadcast_to((128, 4)), waits=[ev_sd])

            if PHASE < 1.5:
                allq = [qS, qP, qG, qT2, qX1, qK] + qU + qX23 + qM + qO + qL
                em.op("sync", None, waits=[(q, q.v) for q in allq] + [(cS, cS.v)])
                wstk.close()
                block = stk.enter_context(nc.Block())
                em.replay(block)
                return nc
            ev_q = []
            ev_half = [None, None]
            ev_wq_last = ev_bc
            for wi, (wd, ntile, coff, width) in enumerate(specs):
                wr = wd.rearrange("(n p) d -> n p d", p=128)
                for i in range(ntile):
                    waits = [ev_q[-2]] if len(ev_q) >= 2 else [ev_bc]
                    ev_l = dma("sync", qU[i % 2], wt[i % 2][:, 0:width], wr[i],
                               waits=waits)
                    vop(lambda e, j=i % 2, w=width, c=1 + 2 * wi:
                        e.tensor_scalar(wabs[:, 0:w], wt[j][:, 0:w],
                                        scal_bc[:, c:c + 1], MAGIC,
                                        ALU.mult, ALU.add), waits=[ev_l, ev_bc])
                    vdrain()
                    vop(lambda e, w=width: e.tensor_scalar(
                        wabs[:, 0:w], wabs[:, 0:w], MAGIC, None, ALU.subtract))
                    vdrain()
                    ev_t3 = vop(lambda e, w=width: e.tensor_scalar(
                        wq_b[:, 0:w], wabs[:, 0:w], 1.0, -1.0, ALU.min, ALU.max))
                    vdrain()
                    ev_q.append(ev_t3)
                    for cc in range(width // 128 if PHASE >= 1.8 else 0):
                        h = cc % 2
                        if h == 0:
                            pq = psTb[:, 0:128]
                            ev_tp = top(lambda e, c=cc, d=pq: e.transpose(
                                d, wq_b[:, c * 128:(c + 1) * 128], ident_bf[:]),
                                waits=[ev_t3, ev_half[h]])
                        else:
                            pq = psb[5][:, 0:128]
                            ev_tp = top(lambda e, c=cc, d=pq: e.matmul(
                                d, wq_b[:, c * 128:(c + 1) * 128], ident_bf[:],
                                start=True, stop=True),
                                waits=[ev_t3, ev_half[h]])
                        if wi == 0:
                            fn = lambda e, c=cc, ii=i, d=pq: e.tensor_copy(
                                w1qT[:, c * HID + ii * 128:c * HID + ii * 128 + 128],
                                d)
                        else:
                            fn = lambda e, c=cc, ii=i, d=pq: e.tensor_copy(
                                w2qT[:, c * D + ii * 128:c * D + ii * 128 + 128],
                                d)
                        ev_e = vop(fn, waits=[ev_tp])
                        ev_half[h] = ev_e
                        ev_wq_last = ev_e
            ev_wprep = ev_wq_last
            wstk.close()
            if PHASE < 2:
                pass

            if PHASE < 2:
                allq = [qS, qP, qG, qT2, qX1, qK] + qU + qX23 + qM + qO + qL
                em.op("sync", None, waits=[(q, q.v) for q in allq] + [(cS, cS.v)])
                block = stk.enter_context(nc.Block())
                em.replay(block)
                return nc
            ev_ln = []
            ev_sql = []
            ufr = u_ffn.rearrange("(n p) d -> n p d", p=128)
            for i in range(32):
                waits = [ev_wprep] if i < 2 else [ev_ln[i - 2], ev_sql[i - 2]]
                ev_l = dma("sync", qU[i % 2], utile[i % 2][:, 0:D], ufr[i],
                           waits=waits)
                ev_sq = sop(lambda e, j=i % 2: e.activation(
                    xnf[:, 0:D], utile[j][:, 0:D], AF.Square,
                    accum_out=smallf[:, 0:1]),
                    waits=[ev_l] + ([ev_sql[i - 1], ev_ln[i - 1]] if i >= 1
                                    else [ev_wprep]))
                em.op("scalar", lambda e: e.drain())
                ev_sql.append(ev_sq)
                vop(lambda e, j=i % 2: e.tensor_reduce(
                    smallf[:, 1:2], utile[j][:, 0:D], AX.X, ALU.add),
                    waits=[ev_l, ev_sq])
                vop(lambda e, j=i % 2: e.tensor_reduce(
                    smallf[:, 2:3], utile[j][:, 0:D], AX.X, ALU.max))
                vop(lambda e, j=i % 2: e.tensor_reduce(
                    smallf[:, 3:4], utile[j][:, 0:D], AX.X, ALU.min))
                vdrain()
                vop(lambda e, c=i: e.tensor_scalar(
                    stats_sb[:, c:c + 1], smallf[:, 1:2], 1.0 / D, None, ALU.mult))
                vop(lambda e: e.tensor_scalar(
                    smallf[:, 4:5], smallf[:, 0:1], 1.0 / D, None, ALU.mult))
                vdrain()
                vop(lambda e, c=i: e.tensor_tensor(
                    smallf[:, 5:6], stats_sb[:, c:c + 1], stats_sb[:, c:c + 1],
                    ALU.mult))
                vdrain()
                ev_v2 = vop(lambda e: e.tensor_tensor(
                    smallf[:, 4:5], smallf[:, 4:5], smallf[:, 5:6], ALU.subtract))
                ev_sqx = sop(lambda e: e.activation(
                    smallf[:, 5:6], smallf[:, 4:5], AF.Sqrt,
                    bias=epscol[:, 0:1]), waits=[ev_v2, ev_eps])
                ev_rs = vop(lambda e, c=i: e.reciprocal(
                    stats_sb[:, 32 + c:33 + c], smallf[:, 5:6]), waits=[ev_sqx])
                vdrain()
                vop(lambda e, c=i: e.tensor_tensor(
                    smallf[:, 6:7], smallf[:, 2:3], stats_sb[:, c:c + 1],
                    ALU.subtract))
                vop(lambda e, c=i: e.tensor_tensor(
                    smallf[:, 7:8], stats_sb[:, c:c + 1], smallf[:, 3:4],
                    ALU.subtract))
                vdrain()
                vop(lambda e: e.tensor_tensor(
                    smallf[:, 6:7], smallf[:, 6:7], smallf[:, 7:8], ALU.max))
                vdrain()
                vop(lambda e, c=i: e.tensor_tensor(
                    smallf[:, 6:7], smallf[:, 6:7], stats_sb[:, 32 + c:33 + c],
                    ALU.mult))
                vdrain()
                vop(lambda e, c=i: e.tensor_scalar(
                    amax1c_sb[:, c:c + 1], smallf[:, 6:7], 1e-5, None, ALU.max))
                vdrain()
                vop(lambda e, c=i: e.reciprocal(
                    smallf[:, 7:8], amax1c_sb[:, c:c + 1]))
                vdrain()
                ev_f = vop(lambda e, c=i: e.tensor_scalar(
                    s1_sb[:, c:c + 1], smallf[:, 7:8], 127.0, None, ALU.mult))
                vdrain()
                ev_ln.append(ev_f)

            ev_st_tp = top(lambda e: e.transpose(
                psb[6][0:64, 0:128], stats_sb[:, 0:J1], ident_f[:]),
                waits=[ev_ln[-1]])
            ev_st_ev = vop(lambda e: e.tensor_copy(
                statg[:, 0:128], psb[6][0:J1, 0:128]), waits=[ev_st_tp])
            ev_st_sp = dma("scalar", qX1, cin_stats, statg[:], waits=[ev_st_ev])
            ev_st_cc = em.op("gpsimd", lambda e: e.collective_compute(
                "AllGather", ALU.bypass, replica_groups=RG_QUAD,
                ins=[cin_stats], outs=[cag_stats]),
                waits=[ev_st_sp], inc=(cS, 1))

            ev_h_sq = sop(lambda e: e.activation(
                halotmp[:], haloxn[:], AF.Square, accum_out=halosc[:, 0:1]),
                waits=[ev_tab, ev_ln[-1], ev_sql[-1]])
            em.op("scalar", lambda e: e.drain())
            vop(lambda e: e.tensor_reduce(
                halosc[:, 1:2], haloxn[:], AX.X, ALU.add),
                waits=[ev_tab, ev_ln[-1], ev_h_sq])
            vop(lambda e: e.tensor_reduce(halosc[:, 2:3], haloxn[:], AX.X, ALU.max))
            vop(lambda e: e.tensor_reduce(halosc[:, 3:4], haloxn[:], AX.X, ALU.min))
            vdrain()
            vop(lambda e: e.tensor_scalar(
                halosc[:, 0:2], halosc[:, 0:2], 1.0 / D, None, ALU.mult))
            vdrain()
            vop(lambda e: e.tensor_tensor(
                halotmp[:, 0:1], halosc[:, 1:2], halosc[:, 1:2], ALU.mult))
            vdrain()
            ev_hv = vop(lambda e: e.tensor_tensor(
                halosc[:, 0:1], halosc[:, 0:1], halotmp[:, 0:1], ALU.subtract))
            ev_hs = sop(lambda e: e.activation(
                halotmp[:, 0:1], halosc[:, 0:1], AF.Sqrt,
                bias=epscol[0:2, 0:1]), waits=[ev_hv, ev_eps])
            vop(lambda e: e.reciprocal(halosc[:, 0:1], halotmp[:, 0:1]),
                waits=[ev_hs])
            vdrain()
            vop(lambda e: e.tensor_scalar(
                haloxn[:], haloxn[:], halosc[:, 1:2], halosc[:, 0:1],
                ALU.subtract, ALU.mult))
            vdrain()
            vop(lambda e: e.tensor_tensor(
                halotmp[:, 1:2], halosc[:, 2:3], halosc[:, 1:2], ALU.subtract))
            vop(lambda e: e.tensor_tensor(
                halotmp[:, 2:3], halosc[:, 1:2], halosc[:, 3:4], ALU.subtract))
            vdrain()
            vop(lambda e: e.tensor_tensor(
                halotmp[:, 1:2], halotmp[:, 1:2], halotmp[:, 2:3], ALU.max))
            vdrain()
            vop(lambda e: e.tensor_tensor(
                halotmp[:, 1:2], halotmp[:, 1:2], halosc[:, 0:1], ALU.mult))
            vdrain()
            vop(lambda e: e.tensor_scalar(
                halosc[:, 2:3], halotmp[:, 1:2], 1e-5, None, ALU.max))
            vdrain()
            vop(lambda e: e.reciprocal(halotmp[:, 1:2], halosc[:, 2:3]))
            vdrain()
            vop(lambda e: e.tensor_scalar(
                halotmp[:, 1:2], halotmp[:, 1:2], 127.0, None, ALU.mult))
            vdrain()
            vop(lambda e: e.tensor_scalar(
                haloxn[:], haloxn[:], halotmp[:, 1:2], MAGIC,
                ALU.mult, ALU.add))
            vdrain()
            vop(lambda e: e.tensor_scalar(halom[:], haloxn[:], MAGIC, None,
                                          ALU.subtract))
            vdrain()
            ev_hm = vop(lambda e: e.tensor_scalar(
                halom[:], halom[:], hmask[:, 0:1], None, ALU.mult))
            vdrain()
            ev_hh = vop(lambda e: e.tensor_scalar(
                halosc[:, 3:4], halosc[:, 2:3], scal_bc[0:2, 0:1], 1.0 / 127.0,
                ALU.mult, ALU.mult), waits=[ev_bc])
            vdrain()
            ev_hrow = dma("sync", qT2, hrow[0:1, 0:2], halosc[0:2, 3:4],
                          waits=[ev_hh])
            ev_hcol = None
            for cc in range(8):
                po = 512 + (cc % 2) * 2
                ev_tp = top(lambda e, c=cc, p=po: e.transpose(
                    psTb[:, p:p + 2], halom[:, c * 128:(c + 1) * 128],
                    ident_bf[0:2, 0:2]), waits=[ev_hm, ev_hcol])
                ev_hcol = vop(lambda e, c=cc, p=po: e.tensor_copy(
                    halocol[:, c * 2:c * 2 + 2], psTb[:, p:p + 2]), waits=[ev_tp])

            if PHASE < 3:
                allq = [qS, qP, qG, qT2, qX1, qK] + qU + qX23 + qM + qO + qL
                em.op("sync", None, waits=[(q, q.v) for q in allq] + [(cS, cS.v)])
                block = stk.enter_context(nc.Block())
                em.replay(block)
                return nc
            ev_statsT = None
            for blk in range(4):
                ev_lg = dma("sync", qT2, statg[:], cag_stats[blk],
                            waits=[ev_st_cc, ev_statsT, ev_hrow])
                ev_tp = top(lambda e: e.transpose(
                    psb[6][:, 0:64], statg[:, 0:128], ident_f[0:J1, 0:J1]),
                    waits=[ev_lg, ev_statsT])
                ev_statsT = vop(lambda e, b=blk: e.tensor_copy(
                    statsT[b][:], psb[6][:, 0:64]), waits=[ev_tp])

            cstk = ExitStack()
            CSBT = lambda name, shape, dt=F32: cstk.enter_context(
                nc.sbuf_tensor(name, shape, dt))
            xnb = [CSBT(f"xnb{i}", [128, J1 * CHH], BF16) for i in range(2)]
            ssb = [CSBT(f"ssb{i}", [128, J1 * CHH], BF16) for i in range(2)]
            kbf = CSBT("kbf", [J1, 64 * N2], BF16)
            cXre = CSBT("cXre", [J1, N2 * 2], BF16)
            cXim = CSBT("cXim", [J1, N2 * 2], BF16)
            cBre = CSBT("cBre", [N2, N1 * 2], BF16)
            cBim = CSBT("cBim", [N2, N1 * 2], BF16)
            cKhre = CSBT("cKhre", [N2, N1 * 2], BF16)
            cKhim = CSBT("cKhim", [N2, N1 * 2], BF16)
            cVre = CSBT("cVre", [N2, N1 * 2], BF16)
            cVim = CSBT("cVim", [N2, N1 * 2], BF16)
            cVreN = CSBT("cVreN", [N2, N1 * 2], BF16)
            cWpre = CSBT("cWpre", [N2, N1 * 2], BF16)
            cWpim = CSBT("cWpim", [N2, N1 * 2], BF16)
            cWtre = CSBT("cWtre", [N1, N2 * 2], BF16)
            cWtim = CSBT("cWtim", [N1, N2 * 2], BF16)
            xnb_v = [x.rearrange("p (a c) -> p a c", c=CHH) for x in xnb]
            ssb_v = [x.rearrange("p (a c) -> p a c", c=CHH) for x in ssb]

            ev_gather = []
            ev_ssb_free = None
            ev_k_last = None
            ev_xnb_last = None
            ev_ep_prev = None
            cc_pending = []
            for half in range(2):
                c0 = half * CHH
                ev_fill = []
                for bl in range(2):
                    for tau in range(J1):
                        idx = bl * J1 + tau
                        waits = [ev_fill[idx - 2]] if idx >= 2 else \
                            [ev_statsT, ev_ep_prev]
                        ev_l = dma("sync", qU[idx % 2], utile[idx % 2][:, 0:CHH],
                                   u_conv[bl].rearrange("(n p) c -> n p c", p=128)
                                   [tau, :, c0:c0 + CHH], waits=waits)
                        blk = 2 * bl + (tau // 32)
                        tl = tau % 32
                        eng = vop if idx % 2 == 0 else gop
                        ev_x = eng(lambda e, j=idx % 2, b=bl, t=tau, bk=blk, tt=tl:
                                   e.tensor_scalar(
                                       xnb_v[b][:, t, :], utile[j][:, 0:CHH],
                                       statsT[bk][:, tt:tt + 1],
                                       statsT[bk][:, 32 + tt:33 + tt],
                                       ALU.subtract, ALU.mult), waits=[ev_l])
                        ev_fill.append(ev_x)
                ev_xnb = (ev_fill[-1], ev_fill[-2])
                ev_xnb_last = ev_xnb
                if cc_pending:
                    ev_cc0 = em.op("gpsimd", lambda e: e.collective_compute(
                        "AllGather", ALU.bypass, replica_groups=RG_QUAD,
                        ins=[cin_s[0]], outs=[cag_s[0]]),
                        waits=[cc_pending.pop()], inc=(cS, 1))
                    ev_gather.append(ev_cc0)
                ev_kb_q = []

                P = [dict(psA=None, psF=None, psW=None, B_free=None, V_free=None,
                          Wp_free=None, Kh_free=None) for _ in range(2)]
                ev_psTb = ev_hcol
                ev_ep_last = None
                kcell = {"ev": None, "last_k1": ev_k_last}

                def load_kq(sub):
                    kcell["ev"] = dma(
                        "gpsimd", qK, kbf[:],
                        k_fold[:, c0 + sub * 64:c0 + sub * 64 + 64, :]
                        .rearrange("a c b -> a (c b)"),
                        waits=[ev_st_cc, kcell["last_k1"]])

                def stages_for(c, half=half, c0=c0, P=P, ev_xnb=ev_xnb):
                    par = c % 2
                    S = P[par]
                    pA, pF, pW = psb[par], psb[2 + par], psb[4 + par]
                    o1 = par * N1
                    o2 = par * N2
                    ev_kb = kcell["ev"]
                    klhs = kbf[0:J1, (c % 64) * 128:((c % 64) + 1) * 128]
                    st = {}

                    def s_k1():
                        top(lambda e: e.matmul(pA[:, 0:128], klhs, tb["tD1re"][:],
                                               start=True, stop=True),
                            waits=[ev_kb, S["psA"]], inc=False)
                        st["k1"] = top(lambda e: e.matmul(
                            pA[:, 128:256], klhs, tb["tD1im"][:],
                            start=True, stop=True))
                        kcell["last_k1"] = st["k1"]

                    def s_ktw():
                        ev_free, ev_res = cmul_psum(
                            pA, cBre[:, o1:o1 + N1], cBim[:, o1:o1 + N1], par,
                            waits=[st["k1"], S["B_free"]])
                        S["psA"] = ev_free
                        st["ktw"] = ev_res

                    def s_k2():
                        top(lambda e: e.matmul(pF[:, 0:128], tb["tD2re"][:],
                                               cBre[:, o1:o1 + N1],
                                               start=True, stop=False),
                            waits=[st["ktw"], S["psF"]], inc=False)
                        top(lambda e: e.matmul(pF[:, 0:128], tb["tD2imN"][:],
                                               cBim[:, o1:o1 + N1],
                                               start=False, stop=True), inc=False)
                        top(lambda e: e.matmul(pF[:, 128:256], tb["tD2im"][:],
                                               cBre[:, o1:o1 + N1],
                                               start=True, stop=False), inc=False)
                        st["k2"] = top(lambda e: e.matmul(
                            pF[:, 128:256], tb["tD2re"][:], cBim[:, o1:o1 + N1],
                            start=False, stop=True))
                        st["B_free"] = st["k2"]

                    def s_kev():
                        sop(lambda e: e.activation(
                            cKhre[:, o1:o1 + N1], pF[:, 0:128], AF.Copy,
                            scale=1.0 / NFFT), waits=[st["k2"], S["Kh_free"]])
                        ev = sop(lambda e: e.activation(
                            cKhim[:, o1:o1 + N1], pF[:, 128:256], AF.Copy,
                            scale=1.0 / NFFT))
                        st["kev"] = ev
                        S["psF"] = ev

                    def s_xtr():
                        nonlocal ev_psTb
                        po = par * 256
                        top(lambda e: e.transpose(
                            psTb[0:J1, po:po + 128], xnb_v[0][:, :, c],
                            ident_bf[:]),
                            waits=[ev_xnb[0], ev_xnb[1], ev_psTb], inc=False)
                        ev_t = top(lambda e: e.transpose(
                            psTb[0:J1, po + 128:po + 256], xnb_v[1][:, :, c],
                            ident_bf[:]))
                        sop(lambda e: e.activation(
                            cXre[0:J1, o2:o2 + N2], psTb[0:J1, po:po + 128],
                            AF.Copy), waits=[ev_t])
                        ev = sop(lambda e: e.activation(
                            cXim[0:J1, o2:o2 + N2],
                            psTb[0:J1, po + 128:po + 256], AF.Copy))
                        st["xtr"] = ev
                        ev_psTb = ev

                    def s_u1():
                        xre = cXre[0:J1, o2:o2 + N2]
                        xim = cXim[0:J1, o2:o2 + N2]
                        top(lambda e: e.matmul(pA[:, 0:128], xre, tb["tD1re"][:],
                                               start=True, stop=False),
                            waits=[st["xtr"], S["psA"]], inc=False)
                        top(lambda e: e.matmul(pA[:, 0:128], xim, tb["tD1imN"][:],
                                               start=False, stop=True), inc=False)
                        top(lambda e: e.matmul(pA[:, 128:256], xre, tb["tD1im"][:],
                                               start=True, stop=False), inc=False)
                        st["u1"] = top(lambda e: e.matmul(
                            pA[:, 128:256], xim, tb["tD1re"][:],
                            start=False, stop=True))

                    def s_utw():
                        ev_free, ev_res = cmul_psum(
                            pA, cBre[:, o1:o1 + N1], cBim[:, o1:o1 + N1], par,
                            waits=[st["u1"], st["B_free"]])
                        S["psA"] = ev_free
                        st["utw"] = ev_res

                    def s_u2():
                        top(lambda e: e.matmul(pF[:, 0:128], tb["tD2re"][:],
                                               cBre[:, o1:o1 + N1],
                                               start=True, stop=False),
                            waits=[st["utw"], S["psF"]], inc=False)
                        top(lambda e: e.matmul(pF[:, 0:128], tb["tD2imN"][:],
                                               cBim[:, o1:o1 + N1],
                                               start=False, stop=True), inc=False)
                        top(lambda e: e.matmul(pF[:, 128:256], tb["tD2im"][:],
                                               cBre[:, o1:o1 + N1],
                                               start=True, stop=False), inc=False)
                        st["u2"] = top(lambda e: e.matmul(
                            pF[:, 128:256], tb["tD2re"][:], cBim[:, o1:o1 + N1],
                            start=False, stop=True))
                        st["uB_free"] = st["u2"]

                    def s_pw():
                        o = par * 4 * N1
                        vop(lambda e: e.tensor_tensor(
                            ctmp[:, o:o + N1], pF[:, 0:128],
                            cKhre[:, o1:o1 + N1], ALU.mult),
                            waits=[st["u2"], st["kev"], S["V_free"]])
                        vop(lambda e: e.tensor_tensor(
                            ctmp[:, o + N1:o + 2 * N1], pF[:, 128:256],
                            cKhim[:, o1:o1 + N1], ALU.mult))
                        vop(lambda e: e.tensor_tensor(
                            ctmp[:, o + 2 * N1:o + 3 * N1], pF[:, 0:128],
                            cKhim[:, o1:o1 + N1], ALU.mult))
                        v4 = vop(lambda e: e.tensor_tensor(
                            ctmp[:, o + 3 * N1:o + 4 * N1], pF[:, 128:256],
                            cKhre[:, o1:o1 + N1], ALU.mult))
                        S["psF"] = v4
                        S["Kh_free"] = v4
                        vdrain()
                        gop(lambda e: e.tensor_tensor(
                            cVre[:, o1:o1 + N1], ctmp[:, o:o + N1],
                            ctmp[:, o + N1:o + 2 * N1], ALU.subtract),
                            waits=[v4])
                        gop(lambda e: e.tensor_tensor(
                            cVreN[:, o1:o1 + N1], ctmp[:, o + N1:o + 2 * N1],
                            ctmp[:, o:o + N1], ALU.subtract))
                        ev = gop(lambda e: e.tensor_tensor(
                            cVim[:, o1:o1 + N1], ctmp[:, o + 2 * N1:o + 3 * N1],
                            ctmp[:, o + 3 * N1:o + 4 * N1], ALU.add))
                        gdrain()
                        st["pw"] = ev

                    def s_iA():
                        top(lambda e: e.matmul(pW[:, 0:128], tb["tD2re"][:],
                                               cVre[:, o1:o1 + N1],
                                               start=True, stop=False),
                            waits=[st["pw"], S["psW"]], inc=False)
                        top(lambda e: e.matmul(pW[:, 0:128], tb["tD2im"][:],
                                               cVim[:, o1:o1 + N1],
                                               start=False, stop=True), inc=False)
                        top(lambda e: e.matmul(pW[:, 128:256], tb["tD2re"][:],
                                               cVim[:, o1:o1 + N1],
                                               start=True, stop=False), inc=False)
                        st["iA"] = top(lambda e: e.matmul(
                            pW[:, 128:256], tb["tD2im"][:], cVreN[:, o1:o1 + N1],
                            start=False, stop=True))
                        S["V_free"] = st["iA"]

                    def s_itw():
                        ev_free, ev_res = cmul_psum(
                            pW, cWpre[:, o1:o1 + N1], cWpim[:, o1:o1 + N1], par,
                            waits=[st["iA"], S["Wp_free"]], conj=True)
                        st["W_mm_free"] = ev_free
                        st["itw"] = ev_res

                    def s_wtr():
                        nonlocal ev_psTb
                        po = par * 256
                        top(lambda e: e.transpose(
                            psTb[:, po:po + 128], cWpre[:, o1:o1 + N1],
                            ident_bf[:]), waits=[st["itw"], ev_psTb], inc=False)
                        ev_t = top(lambda e: e.transpose(
                            psTb[:, po + 128:po + 256], cWpim[:, o1:o1 + N1],
                            ident_bf[:]))
                        sop(lambda e: e.activation(
                            cWtre[:, o2:o2 + N2], psTb[:, po:po + 128],
                            AF.Copy), waits=[ev_t])
                        ev = sop(lambda e: e.activation(
                            cWtim[:, o2:o2 + N2], psTb[:, po + 128:po + 256],
                            AF.Copy))
                        st["wtr"] = ev
                        S["Wp_free"] = ev
                        ev_psTb = ev

                    def s_iB():
                        top(lambda e: e.matmul(pW[:, 0:64],
                                               cWtre[:, o2:o2 + N2], tb["tTI1"][:],
                                               start=True, stop=False),
                            waits=[st["wtr"], st["W_mm_free"]], inc=False)
                        top(lambda e: e.matmul(pW[:, 0:64],
                                               cWtim[:, o2:o2 + N2], tb["tTI2N"][:],
                                               start=False, stop=True), inc=False)
                        top(lambda e: e.matmul(pW[:, 64:128],
                                               cWtim[:, o2:o2 + N2], tb["tTI1"][:],
                                               start=True, stop=False), inc=False)
                        st["iB"] = top(lambda e: e.matmul(
                            pW[:, 64:128], cWtre[:, o2:o2 + N2], tb["tTI2"][:],
                            start=False, stop=True))

                    def s_ep():
                        gc = c0 + c
                        vop(lambda e: e.scalar_tensor_tensor(
                            ssb_v[0][:, :, c], xnb_v[0][:, :, c],
                            skipbc[:, gc:gc + 1], pW[:, 0:64],
                            ALU.mult, ALU.add),
                            waits=[st["iB"], ev_ssb_free])
                        ev = vop(lambda e: e.scalar_tensor_tensor(
                            ssb_v[1][:, :, c], xnb_v[1][:, :, c],
                            skipbc[:, gc:gc + 1], pW[:, 64:128],
                            ALU.mult, ALU.add))
                        S["psW"] = ev
                        st["ep"] = ev

                    return st, [s_k1, s_ktw, s_k2, s_kev, s_xtr, s_u1, s_utw,
                                s_u2, s_pw, s_iA, s_itw, s_wtr, s_iB, s_ep]

                last_st = None
                for cp in range(CHH // 2):
                    if cp % 32 == 0:
                        load_kq(cp // 32)
                    sta, fa = stages_for(2 * cp)
                    stb, fb = stages_for(2 * cp + 1)
                    for f1, f2 in zip(fa, fb):
                        f1()
                        f2()
                    last_st = stb
                ev_ep_last = last_st["ep"]
                ev_ep_prev = ev_ep_last
                ev_k_last = kcell["last_k1"]

                ev_sp0 = dma("scalar", qX23[half],
                             cin_s[half, 0].rearrange("p a c -> p (a c)"),
                             ssb[0][:], waits=[ev_ep_last])
                ev_sp1 = dma("scalar", qX23[half],
                             cin_s[half, 1].rearrange("p a c -> p (a c)"),
                             ssb[1][:], waits=[ev_ep_last])
                ev_ssb_free = ev_sp1
                if half == 0:
                    cc_pending.append(ev_sp1)
                else:
                    ev_cc = em.op("gpsimd", lambda e: e.collective_compute(
                        "AllGather", ALU.bypass, replica_groups=RG_QUAD,
                        ins=[cin_s[1]], outs=[cag_s[1]]),
                        waits=[ev_sp1], inc=(cS, 1))
                    ev_gather.append(ev_cc)

            ev_s_ready = ev_gather[-1]
            cstk.close()

            if PHASE < 4:
                allq = [qS, qP, qG, qT2, qX1, qK] + qU + qX23 + qM + qO + qL
                em.op("sync", None, waits=[(q, q.v) for q in allq] + [(cS, cS.v)])
                block = stk.enter_context(nc.Block())
                em.replay(block)
                return nc
            diag = SBT("diag", [128, 96 * 128], BF16)
            hbuf = [SBT(f"hbuf{i}", [128, 32 * (TCH + 2)], BF16) for i in range(3)]
            m1T = [SBT(f"m1T{i}", [128, 8 * (TCH + 2)], BF16) for i in range(3)]
            hs_c = SBT("hs_c", [128, 32 * TCH], BF16)
            m2_c = SBT("m2_c", [128, 32 * TCH], BF16)
            m2i_t = [SBT(f"m2i_t{i}", [128, TCH]) for i in range(2)]
            gtmp = SBT("gtmp", [128, TCH], BF16)
            sred = SBT("sred", [128, TCH], BF16)
            gtr = SBT("gtr", [128, TCH], BF16)
            gtok = SBT("gtok", [128, D], BF16)
            stok = [SBT(f"stok{i}", [128, D], BF16) for i in range(2)]
            gs_t = SBT("gs_t", [128, D], BF16)
            ucmb = [SBT(f"ucmb{i}", [128, D]) for i in range(2)]
            otile = [SBT(f"otile{i}", [128, D]) for i in range(2)]
            bcrow = SBT("bcrow", [1, 512])
            bcrow_b = SBT("bcrow_b", [1, 512], BF16)
            colsc = SBT("colsc", [128, 8])
            hscalebc = SBT("hscalebc", [128, TCH + 2])
            s2bc = SBT("s2bc", [128, TCH])
            s2rbc = SBT("s2rbc", [128, TCH])

            hbuf_v = [x.rearrange("p (a c) -> p a c", c=TCH + 2) for x in hbuf]
            m1T_v = [x.rearrange("p (a c) -> p a c", c=TCH + 2) for x in m1T]
            hs_v = hs_c.rearrange("p (a c) -> p a c", c=TCH)
            m2_v = m2_c.rearrange("p (a c) -> p a c", c=TCH)

            ev_diag = ev_dmask
            for hb in range(32):
                for tap in range(3):
                    ev_diag = vop(lambda e, h=hb, t=tap: e.tensor_scalar(
                        diag[:, (h * 3 + t) * 128:(h * 3 + t + 1) * 128],
                        diagmask[:], cwcols[:, h * 3 + t:h * 3 + t + 1], None,
                        ALU.mult), waits=[ev_dmask, ev_tab, ev_s_ready])

            sload_val = [(qL[kk % 2], (kk // 2 + 1) * 128) for kk in range(NCHUNK)]

            ev_m1 = [None] * (NCHUNK + 2)
            ev_bl1 = [None] * (NCHUNK + 2)
            ev_conv3 = [None] * (NCHUNK + 2)
            ev_comb = [None] * (NCHUNK + 4)
            ev_sload = [None] * (NCHUNK + 2)
            ev_uc = [None] * (NCHUNK + 2)
            ev_st = [None] * (NCHUNK + 4)
            ev_mm_m1 = [None] * (NCHUNK + 2)
            ev_xn_ts = [None] * (NCHUNK + 2)
            ev_sil_last = [None] * (NCHUNK + 2)
            ev_fe_last = [None] * (NCHUNK + 2)
            ev_gt_free = [None] * (NCHUNK + 2)

            ev_hbc_prev = vop(lambda e: e.memset(bcrow[0:1, 0:512], 0.0),
                              waits=[ev_s_ready])
            ev_bc_free = ev_statsT
            ev_pTb_m1 = last_st["wtr"]
            ev_gs_prev = None
            ev_sred_free = None

            for it in range(NCHUNK + 2):
                k = it
                kb = it - 1
                kc = it - 2

                if 0 <= kb < NCHUNK:
                    if kb == 0:
                        ncols, off = TCH + 1, 0
                    elif kb == NCHUNK - 1:
                        ncols, off = TCH + 1, 1
                    else:
                        ncols, off = TCH, 1
                    ev_c = vop(lambda e, c=kb: e.tensor_scalar(
                        colsc[:, 0:1], amax1c_sb[:, c:c + 1], scal_bc[:, 0:1],
                        1.0 / 127.0, ALU.mult, ALU.mult),
                        waits=[ev_bc, ev_hbc_prev,
                               ev_s_ready if kb == 0 else None])
                    vdrain()
                    ev_t = top(lambda e: e.transpose(
                        psb[6][0:1, 256:384], colsc[:, 0:1], ident_f[:]),
                        waits=[ev_c, ev_bc_free])
                    ev_r = vop(lambda e: e.tensor_copy(
                        bcrow[0:1, 1:129], psb[6][0:1, 256:384]), waits=[ev_t])
                    if kb == 0:
                        ev_r = vop(lambda e: e.tensor_copy(
                            bcrow[0:1, 0:1], hrow[0:1, 0:1]), waits=[ev_hrow])
                    if kb == NCHUNK - 1:
                        ev_r = vop(lambda e: e.tensor_copy(
                            bcrow[0:1, 129:130], hrow[0:1, 1:2]),
                            waits=[ev_hrow])
                    vdrain()
                    vop(lambda e: e.tensor_copy(bcrow_b[0:1, 0:130],
                                                bcrow[0:1, 0:130]))
                    vdrain()
                    vop(lambda e: e.tensor_tensor(
                        bcrow[0:1, 256:386], bcrow[0:1, 0:130],
                        bcrow_b[0:1, 0:130], ALU.subtract))
                    vdrain()
                    ev_lo = vop(lambda e: e.tensor_copy(
                        bcrow_b[0:1, 256:386], bcrow[0:1, 256:386]))
                    vdrain()
                    top(lambda e: e.matmul(psb[6][:, 0:130], onesbf[:],
                                           bcrow_b[0:1, 0:130],
                                           start=True, stop=False),
                        waits=[ev_lo, ev_ones], inc=False)
                    ev_bcm = top(lambda e: e.matmul(
                        psb[6][:, 0:130], onesbf[:], bcrow_b[0:1, 256:386],
                        start=False, stop=True))
                    ev_hsb = vop(lambda e: e.tensor_copy(
                        hscalebc[:, 0:130], psb[6][:, 0:130]), waits=[ev_bcm])
                    vdrain()
                    ev_bc_free = ev_hsb
                    ev_hbc_prev = ev_hsb

                    ev_hevac = [None] * 32
                    ev_mm_last = None
                    for hb in range(32):
                        pH = psb[hb % 2]
                        mw = [ev_m1[kb], ev_hsb]
                        if hb >= 2:
                            mw.append(ev_hevac[hb - 2])
                        elif kb >= 1:
                            mw.append(ev_bl1[kb - 1])
                        for cc in range(8):
                            lastmm = (cc == 7)
                            ev_mm = top(lambda e, h=hb, c=cc, n=ncols, of=off,
                                        p=pH, kk=kb: e.matmul(
                                p[:, 0:n],
                                w1qT[:, c * HID + h * 128:c * HID + h * 128 + 128],
                                m1T_v[kk % 3][:, c, of:of + n],
                                start=(c == 0), stop=(c == 7)),
                                waits=mw if cc == 0 else (), inc=lastmm)
                        ev_mm_last = ev_mm
                        ww = [ev_mm]
                        if kb >= 3:
                            ww.append(ev_conv3[kb - 3])
                        ev_e = vop(lambda e, h=hb, n=ncols, of=off, p=pH, kk=kb:
                                   e.tensor_tensor(
                                       hbuf_v[kk % 3][:, h, of:of + n],
                                       p[:, 0:n], hscalebc[:, of:of + n],
                                       ALU.mult), waits=ww)
                        if kb > 0:
                            ev_e = vop(lambda e, h=hb, of=off, p=pH, kk=kb:
                                       e.tensor_tensor(
                                           hbuf_v[(kk - 1) % 3][:, h,
                                                                TCH + 1:TCH + 2],
                                           p[:, 0:1], hscalebc[:, of:of + 1],
                                           ALU.mult))
                        if kb < NCHUNK - 1:
                            ev_e = vop(lambda e, h=hb, n=ncols, of=off, p=pH,
                                       kk=kb: e.tensor_tensor(
                                           hbuf_v[(kk + 1) % 3][:, h, 0:1],
                                           p[:, n - 1:n],
                                           hscalebc[:, of + n - 1:of + n],
                                           ALU.mult))
                        ev_hevac[hb] = ev_e
                    ev_bl1[kb] = ev_hevac[31]
                    ev_mm_m1[kb] = ev_mm_last

                if 0 <= kc < NCHUNK:
                    ev_sil = [None] * 32
                    ev_c3last = None
                    for hb in range(32):
                        pC = psb[2 + hb % 2]
                        ww = [ev_bl1[kc], ev_diag]
                        if kc + 1 < NCHUNK:
                            ww.append(ev_bl1[kc + 1])
                        if hb >= 2:
                            ww.append(ev_sil[hb - 2])
                        elif kc >= 1:
                            ww.append(ev_sil_last[kc - 1])
                        for tap in range(3):
                            ev_c3 = top(lambda e, h=hb, t=tap, kk=kc, p=pC:
                                        e.matmul(
                                p[:, 0:TCH],
                                diag[:, (h * 3 + t) * 128:(h * 3 + t + 1) * 128],
                                hbuf_v[kk % 3][:, h, t:t + TCH],
                                start=(t == 0), stop=(t == 2)),
                                waits=ww if tap == 0 else (), inc=(tap == 2))
                        ev_c3last = ev_c3
                        ev_sil[hb] = sop(lambda e, h=hb, p=pC: e.activation(
                            hs_v[:, h, :], p[:, 0:TCH], AF.Silu,
                            bias=cbcols[:, h:h + 1]), waits=[ev_c3])
                    ev_conv3[kc] = ev_c3last
                    ev_sil_last[kc] = ev_sil[31]

                    ev_red = vop(lambda e: e.tensor_reduce(
                        sred[:], hs_c.rearrange("p (h t) -> p t h", t=TCH),
                        AX.X, ALU.max, apply_absolute_value=True),
                        waits=[ev_sil[31], ev_sred_free])
                    vdrain()
                    ev_tp2a = top(lambda e: e.transpose(
                        psTb[:, 768:896], sred[:], ident_bf[:]),
                        waits=[ev_red, ev_gt_free[kc - 1], ev_pTb_m1])
                    ev_sred_free = ev_tp2a
                    ev_te = sop(lambda e: e.activation(
                        gtr[:, 0:TCH], psTb[:, 768:896], AF.Copy),
                        waits=[ev_tp2a])
                    vop(lambda e: e.tensor_reduce(
                        colsc[:, 1:2], gtr[:, 0:TCH], AX.X, ALU.max,
                        apply_absolute_value=True), waits=[ev_te])
                    vdrain()
                    vop(lambda e: e.tensor_scalar(
                        colsc[:, 2:3], colsc[:, 1:2], 1e-5, None, ALU.max))
                    vdrain()
                    vop(lambda e: e.reciprocal(colsc[:, 3:4], colsc[:, 2:3]))
                    vop(lambda e: e.tensor_scalar(
                        colsc[:, 4:5], colsc[:, 2:3], scal_bc[:, 2:3],
                        1.0 / 127.0, ALU.mult, ALU.mult))
                    vdrain()
                    ev_s2c = vop(lambda e: e.tensor_scalar(
                        colsc[:, 3:4], colsc[:, 3:4], 127.0, None, ALU.mult))
                    vdrain()
                    ev_bcs = []
                    for ci, dst in ((3, s2bc), (4, s2rbc)):
                        ev_t2 = top(lambda e, c=ci: e.transpose(
                            psb[6][0:1, 256:384], colsc[:, c:c + 1], ident_f[:]),
                            waits=[ev_s2c, ev_bc_free])
                        ev_r2 = vop(lambda e: e.tensor_copy(
                            bcrow[0:1, 0:128], psb[6][0:1, 256:384]),
                            waits=[ev_t2])
                        vdrain()
                        vop(lambda e: e.tensor_copy(
                            bcrow_b[0:1, 0:128], bcrow[0:1, 0:128]))
                        vdrain()
                        vop(lambda e: e.tensor_tensor(
                            bcrow[0:1, 256:384], bcrow[0:1, 0:128],
                            bcrow_b[0:1, 0:128], ALU.subtract))
                        vdrain()
                        ev_lo2 = vop(lambda e: e.tensor_copy(
                            bcrow_b[0:1, 256:384], bcrow[0:1, 256:384]))
                        vdrain()
                        top(lambda e: e.matmul(psb[6][:, 0:128], onesbf[:],
                                               bcrow_b[0:1, 0:128],
                                               start=True, stop=False),
                            waits=[ev_lo2, ev_ones], inc=False)
                        ev_m2m = top(lambda e: e.matmul(
                            psb[6][:, 0:128], onesbf[:], bcrow_b[0:1, 256:384],
                            start=False, stop=True))
                        ev_b = vop(lambda e, d=dst: e.tensor_copy(
                            d[:, 0:TCH], psb[6][:, 0:128]), waits=[ev_m2m])
                        vdrain()
                        ev_bc_free = ev_b
                        ev_bcs.append(ev_b)

                    ev_m2done = [None] * 32
                    for hp2 in range(16):
                        for hb in (2 * hp2, 2 * hp2 + 1):
                            gop(lambda e, h=hb: e.tensor_tensor(
                                m2i_t[h % 2][:, 0:TCH], hs_v[:, h, :],
                                s2bc[:, 0:TCH], ALU.mult),
                                waits=[ev_bcs[0], ev_sil[hb]] +
                                      ([ev_fe_last[kc - 1]]
                                       if hp2 == 0 and kc >= 1 else []))
                        gdrain()
                        for hb in (2 * hp2, 2 * hp2 + 1):
                            ev_m2done[hb] = gop(lambda e, h=hb: e.tensor_scalar(
                                m2_v[:, h, :], m2i_t[h % 2][:, 0:TCH],
                                MAGIC, MAGIC, ALU.add, ALU.subtract))
                        gdrain()

                    ev_g = [None] * 8
                    ev_gtoke = [None] * 8
                    ev_gt_prev = None
                    for oc in range(8):
                        pB = psb[4 + oc % 2]
                        ww2 = [ev_m2done[31]]
                        if oc >= 2:
                            ww2.append(ev_g[oc - 2])
                        for hb in range(32):
                            ev_f2 = top(lambda e, o=oc, h=hb, p=pB: e.matmul(
                                p[:, 0:TCH],
                                w2qT[:, h * D + o * 128:h * D + o * 128 + 128],
                                m2_v[:, h, :], start=(h == 0), stop=(h == 31)),
                                waits=ww2 if hb == 0 else (), inc=(hb == 31))
                        ev_fe = vop(lambda e, o=oc, p=pB: e.tensor_tensor(
                            ftmp[:, 0:TCH], p[:, 0:TCH], s2rbc[:, 0:TCH],
                            ALU.mult), waits=[ev_f2, ev_bcs[1]])
                        ev_g[oc] = ev_fe
                        ev_sg = sop(lambda e: e.activation(
                            gtmp[:, 0:TCH], ftmp[:, 0:TCH], AF.Silu),
                            waits=[ev_fe,
                                   ev_gt_prev if oc >= 1 else None])
                        em.op("scalar", lambda e: e.drain())
                        po2 = 768 + (oc % 2) * 128
                        ev_gt = ev_gt_prev = top(lambda e, p=po2: e.transpose(
                            psTb[:, p:p + 128], gtmp[:, 0:TCH], ident_bf[:]),
                            waits=[ev_sg, ev_te if oc < 2 else None,
                                   ev_gtoke[oc - 2] if oc >= 2 else None,
                                   ev_gt_free[kc - 1] if oc < 2 else None])
                        ev_gtoke[oc] = sop(lambda e, o=oc, p=po2: e.activation(
                            gtok[:, o * 128:(o + 1) * 128], psTb[:, p:p + 128],
                            AF.Copy), waits=[ev_gt, ev_gs_prev])
                    ev_fe_last[kc] = ev_g[7]
                    ev_gt_free[kc] = ev_gtoke[7]

                    ev_gs = vop(lambda e, kk=kc: e.tensor_tensor(
                        gs_t[:], gtok[:], stok[kk % 2][:], ALU.mult),
                        waits=[ev_gtoke[7], ev_sload[kc]])
                    vdrain()
                    ev_gs_prev = ev_gs
                    ev_o = vop(lambda e, kk=kc: e.tensor_tensor(
                        otile[kk % 2][:], gs_t[:], ucmb[kk % 2][:], ALU.add),
                        waits=[ev_uc[kc], ev_gs,
                               ev_st[kc - 2] if kc >= 2 else None])
                    vdrain()
                    ev_comb[kc] = ev_o
                    ev_st[kc] = dma("scalar", qO[kc % 2],
                                    o_out[kc * TCH:(kc + 1) * TCH, :],
                                    otile[kc % 2][:], waits=[ev_o])

                if k < NCHUNK:
                    lw = [ev_xn_ts[k - 2]] if k >= 2 else \
                        [ev_xnb_last[0], ev_xnb_last[1], ev_ln[-1], ev_sql[-1]]
                    ev_l = dma("scalar", qU[k % 2], utile[k % 2][:, 0:D],
                               ufr[k], waits=lw)
                    gop(lambda e, j=k % 2, c=k: e.tensor_scalar(
                        xnf[:], utile[j][:, 0:D], stats_sb[:, c:c + 1],
                        stats_sb[:, 32 + c:33 + c], ALU.subtract, ALU.mult),
                        waits=[ev_l])
                    gdrain()
                    ev_xn_ts[k] = gop(lambda e, c=k: e.tensor_scalar(
                        m1f[:], xnf[:], s1_sb[:, c:c + 1], MAGIC,
                        ALU.mult, ALU.add))
                    gdrain()
                    m1b = m1bb[k % 2]
                    ev_m1b = gop(lambda e, d=m1b: e.tensor_scalar(
                        d[:], m1f[:], MAGIC, None, ALU.subtract))
                    gdrain()
                    ev_last = None
                    for cc in range(8):
                        po = 512 + (cc % 2) * 128
                        xw = [ev_gt_free[k - 2]] if (cc == 0 and k >= 2) else []
                        ev_tp = top(lambda e, c=cc, p=po, b=m1b: e.transpose(
                            psTb[:, p:p + 128], b[:, c * 128:(c + 1) * 128],
                            ident_bf[:]), waits=[ev_m1b, ev_pTb_m1] + xw)
                        ev_e = sop(lambda e, c=cc, p=po, kk=k: e.activation(
                            m1T_v[kk % 3][:, c, 1:1 + TCH], psTb[:, p:p + 128],
                            AF.Copy), waits=[ev_tp, ev_mm_m1[k - 3],
                                             ev_s_ready if k < 3 else None])
                        ev_pTb_m1 = ev_e
                        ev_last = ev_e
                    if k == 0:
                        for cc in range(8):
                            ev_last = sop(lambda e, c=cc: e.activation(
                                m1T_v[0][:, c, 0:1], halocol[:, c * 2:c * 2 + 1],
                                AF.Copy), waits=[ev_hcol])
                    if k == NCHUNK - 1:
                        for cc in range(8):
                            ev_last = sop(lambda e, c=cc: e.activation(
                                m1T_v[(NCHUNK - 1) % 3][:, c, TCH + 1:TCH + 2],
                                halocol[:, c * 2 + 1:c * 2 + 2], AF.Copy),
                                waits=[ev_hcol])
                    ev_m1[k] = ev_last
                    ev_sload[k] = sload_val[k]
                    ev_uc[k] = dma("scalar", qM[k % 2], ucmb[k % 2][:], ufr[k],
                                   waits=[ev_comb[k - 2],
                                          ev_s_ready if k < 2 else None])

            comb_vals = [ev[1] for ev in ev_comb[:NCHUNK]]
            sready_val = ev_s_ready[1]

            def all_sloads(e):
                r0 = e.partition_id()
                ins = None
                for rr in nc.Switch(engines=[nc.sync], index=[r0], n=8):
                    j = rr % 4
                    bl, th = j >> 1, j & 1
                    e.wait_ge(cS.h, sready_val)
                    for kappa in range(NCHUNK):
                        if kappa >= 2:
                            e.wait_ge(vS.h, comb_vals[kappa - 2])
                        j1v = th * 32 + kappa
                        for hh in range(2):
                            for q in range(4):
                                ins = e.dma_start(
                                    out=stok[kappa % 2][:, (q * 2 + hh) * CHH:
                                                        (q * 2 + hh + 1) * CHH],
                                    in_=cag_s[hh, q, bl, :, j1v, :])
                                ins.then_inc(qL[kappa % 2].h, 16)
                return None

            em.op("sync", all_sloads)
            qL[0].v += (NCHUNK // 2) * 128
            qL[1].v += (NCHUNK // 2) * 128

            allq = [qS, qP, qG, qT2, qX1, qK] + qU + qX23 + qM + qO + qL
            em.op("sync", None, waits=[(q, q.v) for q in allq] + [(cS, cS.v)])

            block = stk.enter_context(nc.Block())
            em.replay(block)
        return nc


_NC_CACHE = {}


def kernel(L0_attn_u, L0_attn_k, L0_attn_skip, L0_ffn_w1, L0_ffn_conv_w,
           L0_ffn_conv_b, L0_ffn_w2):
    in_maps = shard_inputs(L0_attn_u, L0_attn_k, L0_attn_skip, L0_ffn_w1,
                           L0_ffn_conv_w, L0_ffn_conv_b, L0_ffn_w2)
    if "nc" not in _NC_CACHE:
        _NC_CACHE["nc"] = Builder().build()
    nc = _NC_CACHE["nc"]
    res = run_bass_kernel_spmd(nc, in_maps, core_ids=list(range(8)))
    out = np.empty((B, L, D), dtype=np.float32)
    for r in range(8):
        bq, th = r // 2, r % 2
        out[bq, th * TOK:(th + 1) * TOK, :] = res.results[r]["o_out"]
    return out


# version 08
# speedup vs baseline: 1.2387x; 1.2387x over previous
import numpy as np
import ml_dtypes
from contextlib import ExitStack

import concourse.bass as bass
import concourse.mybir as mybir
from concourse.bass_utils import run_bass_kernel_spmd

F32 = mybir.dt.float32
BF16 = mybir.dt.bfloat16
FP8 = mybir.dt.float8e4
I16 = mybir.dt.int16
AF = mybir.ActivationFunctionType
ALU = mybir.AluOpType
AX = mybir.AxisListType
BF = ml_dtypes.bfloat16

B, L, D, HID = 4, 8192, 1024, 4096
N1 = N2 = 128
NFFT = N1 * N2
J1 = 64
CHQ, CHH = 256, 128
TOK, TCH = 4096, 128
NCHUNK = TOK // TCH
MAGIC = float(np.float32(1.5 * 2 ** 23))
PHASE = 4
RG_QUAD = [[0, 1, 2, 3], [4, 5, 6, 7]]
ENGS = ("sync", "scalar", "vector", "tensor", "gpsimd")


def _tables():
    w = np.exp(-2j * np.pi / NFFT)
    w1 = np.exp(-2j * np.pi / N1)
    j1 = np.arange(J1); k1 = np.arange(N1); j2 = np.arange(N2); k2 = np.arange(N2)
    D1 = w1 ** (j1[:, None] * k1[None, :])
    D2 = w1 ** (j2[:, None] * k2[None, :])
    T = w ** (j2[:, None] * k1[None, :])
    cD1 = np.conj(w1 ** (k1[:, None] * j1[None, :]))
    t = {
        "tD1re": D1.real, "tD1im": D1.imag, "tD1imN": -D1.imag,
        "tD2re": D2.real, "tD2im": D2.imag, "tD2imN": -D2.imag,
        "tTre": T.real, "tTim": T.imag,
        "tTI1": cD1.real, "tTI2": cD1.imag, "tTI2N": -cD1.imag,
    }
    out = {kk: np.ascontiguousarray(v.astype(BF)) for kk, v in t.items()}
    out["ident_bf"] = np.eye(128, dtype=BF)
    out["ident_f32"] = np.eye(128, dtype=np.float32)
    return out


def shard_inputs(u, k, skip, w1, cw, cb, w2):
    tabs = _tables()
    cw2 = np.ascontiguousarray(np.asarray(cw, dtype=np.float32).reshape(HID, 3))
    cw_cols = np.ascontiguousarray(cw2.reshape(32, 128, 3).transpose(1, 0, 2))
    cb_cols = np.ascontiguousarray(
        np.asarray(cb, dtype=np.float32).reshape(32, 128).T)
    u = np.asarray(u, dtype=np.float32)
    k = np.asarray(k, dtype=np.float32)
    skip = np.asarray(skip, dtype=np.float32)
    w1 = np.ascontiguousarray(np.asarray(w1, dtype=np.float32))
    w2 = np.ascontiguousarray(np.asarray(w2, dtype=np.float32))
    maps = []
    for r in range(8):
        Q, dq = r // 4, r % 4
        bq, th = r // 2, r % 2
        c0 = CHQ * dq
        m = {
            "u_ffn": np.ascontiguousarray(u[bq, th * TOK:(th + 1) * TOK, :]),
            "u_halo": np.ascontiguousarray(u[bq, TOK - 1:TOK + 1, :]),
            "halo_mask": np.array([[1.0 if th == 1 else 0.0,
                                    1.0 if th == 0 else 0.0]], dtype=np.float32),
            "u_conv": np.ascontiguousarray(u[2 * Q:2 * Q + 2, :, c0:c0 + CHQ]),
            "k_fold": np.ascontiguousarray(
                k[c0:c0 + CHQ, :].reshape(CHQ, J1, N2).transpose(1, 0, 2)),
            "skip_s": np.ascontiguousarray(skip[c0:c0 + CHQ].reshape(1, CHQ)),
            "w1": w1, "w2": w2, "cw_cols": cw_cols, "cb_cols": cb_cols,
        }
        m.update(tabs)
        maps.append(m)
    return maps


class Sem:
    def __init__(self, stk, nc, name):
        self.h = stk.enter_context(nc.semaphore(name))
        self.v = 0
        self.name = name


class Emit:
    def __init__(self):
        self.prog = {e: [] for e in ENGS}

    def op(self, eng, fn, waits=(), inc=None):
        ev = None
        hinc = None
        if inc is not None:
            sem, amt = inc
            sem.v += amt
            ev = (sem, sem.v)
            hinc = (sem.h, amt)
        ws = []
        for wv in waits:
            if wv is None:
                continue
            s, v = wv
            ws.append((s.h, v))
        self.prog[eng].append((ws, fn, hinc))
        return ev

    def replay(self, block):
        def make(name):
            items = self.prog[name]

            def body(eng):
                for waits, fn, inc in items:
                    for sem, val in waits:
                        eng.wait_ge(sem, val)
                    if fn is not None:
                        ins = fn(eng)
                        if inc is not None:
                            ins.then_inc(inc[0], inc[1])
            return body

        for name in ENGS:
            getattr(block, name)(make(name))


class Builder:
    def build(self):
        nc = bass.Bass("TRN2", target_bir_lowering=False)
        self.nc = nc
        em = Emit()
        self.em = em

        u_ffn = nc.dram_tensor("u_ffn", [TOK, D], F32, kind="ExternalInput").ap()
        u_halo = nc.dram_tensor("u_halo", [2, D], F32, kind="ExternalInput").ap()
        halo_mask = nc.dram_tensor("halo_mask", [1, 2], F32, kind="ExternalInput").ap()
        u_conv = nc.dram_tensor("u_conv", [2, L, CHQ], F32, kind="ExternalInput").ap()
        k_fold = nc.dram_tensor("k_fold", [J1, CHQ, N2], F32, kind="ExternalInput").ap()
        skip_s = nc.dram_tensor("skip_s", [1, CHQ], F32, kind="ExternalInput").ap()
        w1_d = nc.dram_tensor("w1", [HID, D], F32, kind="ExternalInput").ap()
        w2_d = nc.dram_tensor("w2", [D, HID], F32, kind="ExternalInput").ap()
        cwc_d = nc.dram_tensor("cw_cols", [128, 32, 3], F32, kind="ExternalInput").ap()
        cbc_d = nc.dram_tensor("cb_cols", [128, 32], F32, kind="ExternalInput").ap()
        tabd = {}
        for name in ("tD1re", "tD1im", "tD1imN"):
            tabd[name] = nc.dram_tensor(name, [J1, N1], BF16, kind="ExternalInput").ap()
        for name in ("tD2re", "tD2im", "tD2imN", "tTre", "tTim"):
            tabd[name] = nc.dram_tensor(name, [N2, N2], BF16, kind="ExternalInput").ap()
        for name in ("tTI1", "tTI2", "tTI2N"):
            tabd[name] = nc.dram_tensor(name, [N1, J1], BF16, kind="ExternalInput").ap()
        ident_bf_d = nc.dram_tensor("ident_bf", [128, 128], BF16, kind="ExternalInput").ap()
        ident_f_d = nc.dram_tensor("ident_f32", [128, 128], F32, kind="ExternalInput").ap()
        o_out = nc.dram_tensor("o_out", [TOK, D], F32, kind="ExternalOutput").ap()

        cin_stats = nc.dram_tensor("cin_stats", [J1, 128], F32, kind="Internal").ap()
        cag_stats = nc.dram_tensor("cag_stats", [4, J1, 128], F32, kind="Internal").ap()
        cin_s = nc.dram_tensor("cin_s", [2, 2, 128, J1, CHH], BF16,
                               kind="Internal").ap()
        cag_s = nc.dram_tensor("cag_s", [2, 4, 2, 128, J1, CHH], BF16,
                               kind="Internal").ap()
        d_scr1 = nc.dram_tensor("d_scr1", [128, 32], F32, kind="Internal").ap()
        d_scr2 = nc.dram_tensor("d_scr2", [128, 8], F32, kind="Internal").ap()
        d_row = nc.dram_tensor("d_row", [1, 16], F32, kind="Internal").ap()
        d_row2 = nc.dram_tensor("d_row2", [2, 2], F32, kind="Internal").ap()

        stk = ExitStack()
        with stk:
            SBT = lambda name, shape, dt=F32: stk.enter_context(
                nc.sbuf_tensor(name, shape, dt))
            utile = [SBT(f"utile{i}", [128, D]) for i in range(2)]
            xnf = SBT("xnf", [128, D])
            ftmp = SBT("ftmp", [128, TCH])
            m1f = xnf
            m1bb = [SBT(f"m1b{i}", [128, D], BF16) for i in range(2)]
            stats_sb = SBT("stats_sb", [128, J1])
            amax1c_sb = SBT("amax1c_sb", [128, 32])
            s1_sb = SBT("s1_sb", [128, 32])
            tb = {}
            for name, shape in (
                ("tD1re", [J1, N1]), ("tD1im", [J1, N1]), ("tD1imN", [J1, N1]),
                ("tD2re", [N2, N2]), ("tD2im", [N2, N2]), ("tD2imN", [N2, N2]),
                ("tTre", [N2, N2]), ("tTim", [N2, N2]),
                ("tTI1", [N1, J1]), ("tTI2", [N1, J1]), ("tTI2N", [N1, J1]),
            ):
                tb[name] = SBT("s" + name, shape, BF16)
            ident_bf = SBT("sident_bf", [128, 128], BF16)
            ident_f = SBT("sident_f", [128, 128])
            w1qT = SBT("w1qT", [128, 8 * HID], FP8)
            w2qT = SBT("w2qT", [128, 32 * D], FP8)
            scal_bc = SBT("scal_bc", [128, 4])
            cwcols = SBT("cwcols", [128, 96])
            cbcols = SBT("cbcols", [128, 32])
            iomask = SBT("iomask", [128, 128], I16)
            diagmask = SBT("diagmask", [128, 128], BF16)
            onesbf = SBT("onesbf", [1, 128], BF16)
            epscol = SBT("epscol", [128, 1])
            halocol = SBT("halocol", [128, 16], BF16)
            halosc = SBT("halosc", [2, 8])
            hrow = SBT("hrow", [1, 2])
            hmask = SBT("hmask", [2, 1])
            smallf = SBT("smallf", [128, 8])
            srow = SBT("srow", [2, 256])
            statsT = [SBT(f"statsT{i}", [128, J1]) for i in range(4)]
            statg = SBT("statg", [J1, 128])
            skipbc = SBT("skipbc", [128, CHQ])
            haloxn = SBT("haloxn", [2, D])
            halom = SBT("halom", [2, D], BF16)
            halotmp = SBT("halotmp", [2, D])
            ctmp = SBT("ctmp", [N2, 8 * N1])

            PSB = lambda name, dt=F32, n=512: stk.enter_context(
                nc.psum_tensor(name, [128, n], dt))
            psb = [PSB(f"psb{i}") for i in range(7)]
            psTb = PSB("psTb", BF16, 1024)

            qS = Sem(stk, nc, "qS")
            qP = Sem(stk, nc, "qP")
            qG = Sem(stk, nc, "qG")
            qU = [Sem(stk, nc, "qU0"), Sem(stk, nc, "qU1")]
            qT2 = Sem(stk, nc, "qT2")
            qX1 = Sem(stk, nc, "qX1")
            qX23 = [Sem(stk, nc, "qX2"), Sem(stk, nc, "qX3")]
            qM = [Sem(stk, nc, "qM0"), Sem(stk, nc, "qM1")]
            qO = [Sem(stk, nc, "qO0"), Sem(stk, nc, "qO1")]
            qK = Sem(stk, nc, "qK")
            qL = [Sem(stk, nc, "qL0"), Sem(stk, nc, "qL1")]
            tS = Sem(stk, nc, "tS")
            vS = Sem(stk, nc, "vS")
            sS = Sem(stk, nc, "sS")
            gS = Sem(stk, nc, "gS")
            cS = Sem(stk, nc, "cS")

            def dma(eng, q, out, in_, waits=()):
                return em.op(eng, lambda e, o=out, i=in_: e.dma_start(out=o, in_=i),
                             waits=waits, inc=(q, 16))

            def vop(fn, waits=()):
                return em.op("vector", fn, waits=waits, inc=(vS, 1))

            def sop(fn, waits=()):
                return em.op("scalar", fn, waits=waits, inc=(sS, 1))

            def top(fn, waits=(), inc=True):
                return em.op("tensor", fn, waits=waits,
                             inc=(tS, 1) if inc else None)

            def gop(fn, waits=()):
                return em.op("gpsimd", fn, waits=waits, inc=(gS, 1))

            def vdrain():
                em.op("vector", lambda e: e.drain())

            def gdrain():
                em.op("gpsimd", lambda e: e.drain())

            def cmul_psum(ps, out_re, out_im, par, waits, conj=False):
                o = par * 4 * N1
                v1 = vop(lambda e: e.tensor_tensor(
                    ctmp[:, o:o + N1], ps[:, 0:128], tb["tTre"][:], ALU.mult),
                    waits=waits)
                vop(lambda e: e.tensor_tensor(
                    ctmp[:, o + N1:o + 2 * N1], ps[:, 128:256], tb["tTim"][:],
                    ALU.mult))
                vop(lambda e: e.tensor_tensor(
                    ctmp[:, o + 2 * N1:o + 3 * N1], ps[:, 0:128], tb["tTim"][:],
                    ALU.mult))
                v4 = vop(lambda e: e.tensor_tensor(
                    ctmp[:, o + 3 * N1:o + 4 * N1], ps[:, 128:256], tb["tTre"][:],
                    ALU.mult))
                vdrain()
                if not conj:
                    gop(lambda e: e.tensor_tensor(
                        out_re, ctmp[:, o:o + N1], ctmp[:, o + N1:o + 2 * N1],
                        ALU.subtract), waits=[v4])
                    ev = gop(lambda e: e.tensor_tensor(
                        out_im, ctmp[:, o + 2 * N1:o + 3 * N1],
                        ctmp[:, o + 3 * N1:o + 4 * N1], ALU.add))
                else:
                    gop(lambda e: e.tensor_tensor(
                        out_re, ctmp[:, o:o + N1], ctmp[:, o + N1:o + 2 * N1],
                        ALU.add), waits=[v4])
                    ev = gop(lambda e: e.tensor_tensor(
                        out_im, ctmp[:, o + 3 * N1:o + 4 * N1],
                        ctmp[:, o + 2 * N1:o + 3 * N1], ALU.subtract))
                gdrain()
                return v4, ev

            for name in tb:
                dma("sync", qS, tb[name][:], tabd[name])
            dma("sync", qS, ident_bf[:], ident_bf_d)
            dma("sync", qS, ident_f[:], ident_f_d)
            dma("sync", qS, cwcols[:], cwc_d.rearrange("p a b -> p (a b)"))
            dma("sync", qS, cbcols[:], cbc_d)
            dma("sync", qS, skipbc[:], skip_s.broadcast_to((128, CHQ)))
            dma("sync", qS, hmask[:], halo_mask.rearrange("o t -> t o"))
            ev_tab = dma("sync", qS, haloxn[:], u_halo)
            ev_ones = vop(lambda e: e.memset(onesbf[:], 1.0), waits=[ev_tab])
            ev_eps = vop(lambda e: e.memset(epscol[:], 1e-5))
            ev_iota = gop(lambda e: e.iota(iomask[:], pattern=[[1, 128]], base=0,
                                           channel_multiplier=-1))
            ev_dmask = vop(lambda e: e.tensor_scalar(
                diagmask[:], iomask[:], 0.0, None, ALU.is_equal), waits=[ev_iota])
            vdrain()

            if PHASE < 1:
                allq = [qS, qP, qG, qT2, qX1, qK] + qU + qX23 + qM + qO + qL
                em.op("sync", None, waits=[(q, q.v) for q in allq] + [(cS, cS.v)])
                block = stk.enter_context(nc.Block())
                em.replay(block)
                return nc
            wstk = ExitStack()
            WSBT = lambda name, shape, dt=F32: wstk.enter_context(
                nc.sbuf_tensor(name, shape, dt))
            wt = [WSBT(f"wt{i}", [128, HID]) for i in range(2)]
            wabs = WSBT("wabs", [128, HID])
            wq_b = WSBT("wq_b", [128, HID], BF16)
            absacc = WSBT("absacc", [128, 48])

            specs = [(w1_d, 32, 0, D), (w2_d, 8, 32, HID)]
            ev_abs = []
            for wd, ntile, coff, width in specs:
                wr = wd.rearrange("(n p) d -> n p d", p=128)
                for i in range(ntile):
                    waits = [ev_abs[-2]] if len(ev_abs) >= 2 else [ev_tab]
                    ev_l = dma("sync", qU[(coff + i) % 2],
                               wt[(coff + i) % 2][:, 0:width], wr[i],
                               waits=waits)
                    ev_a = sop(lambda e, j=(coff + i) % 2, c=coff + i, w=width:
                               e.activation(wabs[:, 0:w], wt[j][:, 0:w], AF.Abs,
                                            accum_out=absacc[:, c:c + 1]),
                               waits=[ev_l])
                    em.op("scalar", lambda e: e.drain())
                    ev_abs.append(ev_a)
            if PHASE < 1.2:
                allq = [qS, qP, qG, qT2, qX1, qK] + qU + qX23 + qM + qO + qL
                em.op("sync", None, waits=[(q, q.v) for q in allq] + [(cS, cS.v)])
                wstk.close()
                block = stk.enter_context(nc.Block())
                em.replay(block)
                return nc
            ev_c1 = vop(lambda e: e.tensor_reduce(
                smallf[:, 0:1], absacc[:, 0:32], AX.X, ALU.add),
                waits=[ev_abs[-1]])
            ev_c2 = vop(lambda e: e.tensor_reduce(
                smallf[:, 1:2], absacc[:, 32:40], AX.X, ALU.add))
            vdrain()
            ev_ct = top(lambda e: e.transpose(
                psb[6][0:2, 0:128], smallf[:, 0:2], ident_f[:]),
                waits=[ev_c2])
            ev_cr = vop(lambda e: e.tensor_copy(
                halotmp[0:2, 0:128], psb[6][0:2, 0:128]), waits=[ev_ct])
            vdrain()
            ev_s1 = vop(lambda e: e.tensor_reduce(
                smallf[0:2, 0:1], halotmp[0:2, 0:128], AX.X, ALU.add))
            vdrain()
            vop(lambda e: e.tensor_scalar(
                smallf[0:2, 2:3], smallf[0:2, 0:1], 1.0 / (HID * D), 1e-5,
                ALU.mult, ALU.max))
            vdrain()
            vop(lambda e: e.reciprocal(smallf[0:2, 3:4], smallf[0:2, 2:3]))
            vdrain()
            ev_pk = vop(lambda e: e.tensor_copy(srow[0:2, 0:2],
                                                smallf[0:2, 2:4]))
            ev_sd = dma("scalar", qX1, d_row2, srow[0:2, 0:2], waits=[ev_pk])
            ev_bc = dma("scalar", qX1, scal_bc[:],
                        d_row2.rearrange("a b -> (a b)").unsqueeze(0)
                        .broadcast_to((128, 4)), waits=[ev_sd])

            if PHASE < 1.5:
                allq = [qS, qP, qG, qT2, qX1, qK] + qU + qX23 + qM + qO + qL
                em.op("sync", None, waits=[(q, q.v) for q in allq] + [(cS, cS.v)])
                wstk.close()
                block = stk.enter_context(nc.Block())
                em.replay(block)
                return nc
            ev_q = []
            ev_half = [None, None]
            ev_wq_last = ev_bc
            for wi, (wd, ntile, coff, width) in enumerate(specs):
                wr = wd.rearrange("(n p) d -> n p d", p=128)
                for i in range(ntile):
                    waits = [ev_q[-2]] if len(ev_q) >= 2 else [ev_bc]
                    ev_l = dma("sync", qU[i % 2], wt[i % 2][:, 0:width], wr[i],
                               waits=waits)
                    vop(lambda e, j=i % 2, w=width, c=1 + 2 * wi:
                        e.tensor_scalar(wabs[:, 0:w], wt[j][:, 0:w],
                                        scal_bc[:, c:c + 1], MAGIC,
                                        ALU.mult, ALU.add), waits=[ev_l, ev_bc])
                    vdrain()
                    vop(lambda e, w=width: e.tensor_scalar(
                        wabs[:, 0:w], wabs[:, 0:w], MAGIC, None, ALU.subtract))
                    vdrain()
                    ev_t3 = vop(lambda e, w=width: e.tensor_scalar(
                        wq_b[:, 0:w], wabs[:, 0:w], 1.0, -1.0, ALU.min, ALU.max))
                    vdrain()
                    ev_q.append(ev_t3)
                    for cc in range(width // 128 if PHASE >= 1.8 else 0):
                        h = cc % 2
                        if h == 0:
                            pq = psTb[:, 0:128]
                            ev_tp = top(lambda e, c=cc, d=pq: e.transpose(
                                d, wq_b[:, c * 128:(c + 1) * 128], ident_bf[:]),
                                waits=[ev_t3, ev_half[h]])
                        else:
                            pq = psb[5][:, 0:128]
                            ev_tp = top(lambda e, c=cc, d=pq: e.matmul(
                                d, wq_b[:, c * 128:(c + 1) * 128], ident_bf[:],
                                start=True, stop=True),
                                waits=[ev_t3, ev_half[h]])
                        if wi == 0:
                            fn = lambda e, c=cc, ii=i, d=pq: e.tensor_copy(
                                w1qT[:, c * HID + ii * 128:c * HID + ii * 128 + 128],
                                d)
                        else:
                            fn = lambda e, c=cc, ii=i, d=pq: e.tensor_copy(
                                w2qT[:, c * D + ii * 128:c * D + ii * 128 + 128],
                                d)
                        ev_e = vop(fn, waits=[ev_tp])
                        ev_half[h] = ev_e
                        ev_wq_last = ev_e
            ev_wprep = ev_wq_last
            wstk.close()
            if PHASE < 2:
                pass

            if PHASE < 2:
                allq = [qS, qP, qG, qT2, qX1, qK] + qU + qX23 + qM + qO + qL
                em.op("sync", None, waits=[(q, q.v) for q in allq] + [(cS, cS.v)])
                block = stk.enter_context(nc.Block())
                em.replay(block)
                return nc
            ev_ln = []
            ev_sql = []
            ufr = u_ffn.rearrange("(n p) d -> n p d", p=128)
            for i in range(32):
                waits = [ev_wprep] if i < 2 else [ev_ln[i - 2], ev_sql[i - 2]]
                ev_l = dma("sync", qU[i % 2], utile[i % 2][:, 0:D], ufr[i],
                           waits=waits)
                ev_sq = sop(lambda e, j=i % 2: e.activation(
                    xnf[:, 0:D], utile[j][:, 0:D], AF.Square,
                    accum_out=smallf[:, 0:1]),
                    waits=[ev_l] + ([ev_sql[i - 1], ev_ln[i - 1]] if i >= 1
                                    else [ev_wprep]))
                em.op("scalar", lambda e: e.drain())
                ev_sql.append(ev_sq)
                vop(lambda e, j=i % 2: e.tensor_reduce(
                    smallf[:, 1:2], utile[j][:, 0:D], AX.X, ALU.add),
                    waits=[ev_l, ev_sq])
                vop(lambda e, j=i % 2: e.tensor_reduce(
                    smallf[:, 2:3], utile[j][:, 0:D], AX.X, ALU.max))
                vop(lambda e, j=i % 2: e.tensor_reduce(
                    smallf[:, 3:4], utile[j][:, 0:D], AX.X, ALU.min))
                vdrain()
                vop(lambda e, c=i: e.tensor_scalar(
                    stats_sb[:, c:c + 1], smallf[:, 1:2], 1.0 / D, None, ALU.mult))
                vop(lambda e: e.tensor_scalar(
                    smallf[:, 4:5], smallf[:, 0:1], 1.0 / D, None, ALU.mult))
                vdrain()
                vop(lambda e, c=i: e.tensor_tensor(
                    smallf[:, 5:6], stats_sb[:, c:c + 1], stats_sb[:, c:c + 1],
                    ALU.mult))
                vdrain()
                ev_v2 = vop(lambda e: e.tensor_tensor(
                    smallf[:, 4:5], smallf[:, 4:5], smallf[:, 5:6], ALU.subtract))
                ev_sqx = sop(lambda e: e.activation(
                    smallf[:, 5:6], smallf[:, 4:5], AF.Sqrt,
                    bias=epscol[:, 0:1]), waits=[ev_v2, ev_eps])
                ev_rs = vop(lambda e, c=i: e.reciprocal(
                    stats_sb[:, 32 + c:33 + c], smallf[:, 5:6]), waits=[ev_sqx])
                vdrain()
                vop(lambda e, c=i: e.tensor_tensor(
                    smallf[:, 6:7], smallf[:, 2:3], stats_sb[:, c:c + 1],
                    ALU.subtract))
                vop(lambda e, c=i: e.tensor_tensor(
                    smallf[:, 7:8], stats_sb[:, c:c + 1], smallf[:, 3:4],
                    ALU.subtract))
                vdrain()
                vop(lambda e: e.tensor_tensor(
                    smallf[:, 6:7], smallf[:, 6:7], smallf[:, 7:8], ALU.max))
                vdrain()
                vop(lambda e, c=i: e.tensor_tensor(
                    smallf[:, 6:7], smallf[:, 6:7], stats_sb[:, 32 + c:33 + c],
                    ALU.mult))
                vdrain()
                vop(lambda e, c=i: e.tensor_scalar(
                    amax1c_sb[:, c:c + 1], smallf[:, 6:7], 1e-5, None, ALU.max))
                vdrain()
                vop(lambda e, c=i: e.reciprocal(
                    smallf[:, 7:8], amax1c_sb[:, c:c + 1]))
                vdrain()
                ev_f = vop(lambda e, c=i: e.tensor_scalar(
                    s1_sb[:, c:c + 1], smallf[:, 7:8], 127.0, None, ALU.mult))
                vdrain()
                ev_ln.append(ev_f)

            ev_st_tp = top(lambda e: e.transpose(
                psb[6][0:64, 0:128], stats_sb[:, 0:J1], ident_f[:]),
                waits=[ev_ln[-1]])
            ev_st_ev = vop(lambda e: e.tensor_copy(
                statg[:, 0:128], psb[6][0:J1, 0:128]), waits=[ev_st_tp])
            ev_st_sp = dma("scalar", qX1, cin_stats, statg[:], waits=[ev_st_ev])
            ev_st_cc = em.op("gpsimd", lambda e: e.collective_compute(
                "AllGather", ALU.bypass, replica_groups=RG_QUAD,
                ins=[cin_stats], outs=[cag_stats]),
                waits=[ev_st_sp], inc=(cS, 1))

            ev_h_sq = sop(lambda e: e.activation(
                halotmp[:], haloxn[:], AF.Square, accum_out=halosc[:, 0:1]),
                waits=[ev_tab, ev_ln[-1], ev_sql[-1]])
            em.op("scalar", lambda e: e.drain())
            vop(lambda e: e.tensor_reduce(
                halosc[:, 1:2], haloxn[:], AX.X, ALU.add),
                waits=[ev_tab, ev_ln[-1], ev_h_sq])
            vop(lambda e: e.tensor_reduce(halosc[:, 2:3], haloxn[:], AX.X, ALU.max))
            vop(lambda e: e.tensor_reduce(halosc[:, 3:4], haloxn[:], AX.X, ALU.min))
            vdrain()
            vop(lambda e: e.tensor_scalar(
                halosc[:, 0:2], halosc[:, 0:2], 1.0 / D, None, ALU.mult))
            vdrain()
            vop(lambda e: e.tensor_tensor(
                halotmp[:, 0:1], halosc[:, 1:2], halosc[:, 1:2], ALU.mult))
            vdrain()
            ev_hv = vop(lambda e: e.tensor_tensor(
                halosc[:, 0:1], halosc[:, 0:1], halotmp[:, 0:1], ALU.subtract))
            ev_hs = sop(lambda e: e.activation(
                halotmp[:, 0:1], halosc[:, 0:1], AF.Sqrt,
                bias=epscol[0:2, 0:1]), waits=[ev_hv, ev_eps])
            vop(lambda e: e.reciprocal(halosc[:, 0:1], halotmp[:, 0:1]),
                waits=[ev_hs])
            vdrain()
            vop(lambda e: e.tensor_scalar(
                haloxn[:], haloxn[:], halosc[:, 1:2], halosc[:, 0:1],
                ALU.subtract, ALU.mult))
            vdrain()
            vop(lambda e: e.tensor_tensor(
                halotmp[:, 1:2], halosc[:, 2:3], halosc[:, 1:2], ALU.subtract))
            vop(lambda e: e.tensor_tensor(
                halotmp[:, 2:3], halosc[:, 1:2], halosc[:, 3:4], ALU.subtract))
            vdrain()
            vop(lambda e: e.tensor_tensor(
                halotmp[:, 1:2], halotmp[:, 1:2], halotmp[:, 2:3], ALU.max))
            vdrain()
            vop(lambda e: e.tensor_tensor(
                halotmp[:, 1:2], halotmp[:, 1:2], halosc[:, 0:1], ALU.mult))
            vdrain()
            vop(lambda e: e.tensor_scalar(
                halosc[:, 2:3], halotmp[:, 1:2], 1e-5, None, ALU.max))
            vdrain()
            vop(lambda e: e.reciprocal(halotmp[:, 1:2], halosc[:, 2:3]))
            vdrain()
            vop(lambda e: e.tensor_scalar(
                halotmp[:, 1:2], halotmp[:, 1:2], 127.0, None, ALU.mult))
            vdrain()
            vop(lambda e: e.tensor_scalar(
                haloxn[:], haloxn[:], halotmp[:, 1:2], MAGIC,
                ALU.mult, ALU.add))
            vdrain()
            vop(lambda e: e.tensor_scalar(halom[:], haloxn[:], MAGIC, None,
                                          ALU.subtract))
            vdrain()
            ev_hm = vop(lambda e: e.tensor_scalar(
                halom[:], halom[:], hmask[:, 0:1], None, ALU.mult))
            vdrain()
            ev_hh = vop(lambda e: e.tensor_scalar(
                halosc[:, 3:4], halosc[:, 2:3], scal_bc[0:2, 0:1], 1.0 / 127.0,
                ALU.mult, ALU.mult), waits=[ev_bc])
            vdrain()
            ev_hrow = dma("sync", qT2, hrow[0:1, 0:2], halosc[0:2, 3:4],
                          waits=[ev_hh])
            ev_hcol = None
            for cc in range(8):
                po = 512 + (cc % 2) * 2
                ev_tp = top(lambda e, c=cc, p=po: e.transpose(
                    psTb[:, p:p + 2], halom[:, c * 128:(c + 1) * 128],
                    ident_bf[0:2, 0:2]), waits=[ev_hm, ev_hcol])
                ev_hcol = vop(lambda e, c=cc, p=po: e.tensor_copy(
                    halocol[:, c * 2:c * 2 + 2], psTb[:, p:p + 2]), waits=[ev_tp])

            if PHASE < 3:
                allq = [qS, qP, qG, qT2, qX1, qK] + qU + qX23 + qM + qO + qL
                em.op("sync", None, waits=[(q, q.v) for q in allq] + [(cS, cS.v)])
                block = stk.enter_context(nc.Block())
                em.replay(block)
                return nc
            ev_statsT = None
            for blk in range(4):
                ev_lg = dma("sync", qT2, statg[:], cag_stats[blk],
                            waits=[ev_st_cc, ev_statsT, ev_hrow])
                ev_tp = top(lambda e: e.transpose(
                    psb[6][:, 0:64], statg[:, 0:128], ident_f[0:J1, 0:J1]),
                    waits=[ev_lg, ev_statsT])
                ev_statsT = vop(lambda e, b=blk: e.tensor_copy(
                    statsT[b][:], psb[6][:, 0:64]), waits=[ev_tp])

            cstk = ExitStack()
            CSBT = lambda name, shape, dt=F32: cstk.enter_context(
                nc.sbuf_tensor(name, shape, dt))
            xnb = [CSBT(f"xnb{i}", [128, J1 * CHH], BF16) for i in range(2)]
            ssb = [CSBT(f"ssb{i}", [128, J1 * CHH], BF16) for i in range(2)]
            kbf = CSBT("kbf", [J1, 64 * N2], BF16)
            cXre = CSBT("cXre", [J1, N2 * 2], BF16)
            cXim = CSBT("cXim", [J1, N2 * 2], BF16)
            cBre = CSBT("cBre", [N2, N1 * 2], BF16)
            cBim = CSBT("cBim", [N2, N1 * 2], BF16)
            cKhre = CSBT("cKhre", [N2, N1 * 2], BF16)
            cKhim = CSBT("cKhim", [N2, N1 * 2], BF16)
            cVre = CSBT("cVre", [N2, N1 * 2], BF16)
            cVim = CSBT("cVim", [N2, N1 * 2], BF16)
            cVreN = CSBT("cVreN", [N2, N1 * 2], BF16)
            cWpre = CSBT("cWpre", [N2, N1 * 2], BF16)
            cWpim = CSBT("cWpim", [N2, N1 * 2], BF16)
            cWtre = CSBT("cWtre", [N1, N2 * 2], BF16)
            cWtim = CSBT("cWtim", [N1, N2 * 2], BF16)
            xnb_v = [x.rearrange("p (a c) -> p a c", c=CHH) for x in xnb]
            ssb_v = [x.rearrange("p (a c) -> p a c", c=CHH) for x in ssb]

            ev_gather = []
            ev_ssb_free = None
            ev_k_last = None
            ev_xnb_last = None
            ev_ep_prev = None
            cc_pending = []
            for half in range(2):
                c0 = half * CHH
                ev_fill = []
                for bl in range(2):
                    for tau in range(J1):
                        idx = bl * J1 + tau
                        waits = [ev_fill[idx - 2]] if idx >= 2 else \
                            [ev_statsT, ev_ep_prev]
                        ev_l = dma("sync", qU[idx % 2], utile[idx % 2][:, 0:CHH],
                                   u_conv[bl].rearrange("(n p) c -> n p c", p=128)
                                   [tau, :, c0:c0 + CHH], waits=waits)
                        blk = 2 * bl + (tau // 32)
                        tl = tau % 32
                        eng = vop if idx % 2 == 0 else gop
                        ev_x = eng(lambda e, j=idx % 2, b=bl, t=tau, bk=blk, tt=tl:
                                   e.tensor_scalar(
                                       xnb_v[b][:, t, :], utile[j][:, 0:CHH],
                                       statsT[bk][:, tt:tt + 1],
                                       statsT[bk][:, 32 + tt:33 + tt],
                                       ALU.subtract, ALU.mult), waits=[ev_l])
                        ev_fill.append(ev_x)
                ev_xnb = (ev_fill[-1], ev_fill[-2])
                ev_xnb_last = ev_xnb
                if cc_pending:
                    ev_cc0 = em.op("gpsimd", lambda e: e.collective_compute(
                        "AllGather", ALU.bypass, replica_groups=RG_QUAD,
                        ins=[cin_s[0]], outs=[cag_s[0]]),
                        waits=[cc_pending.pop()], inc=(cS, 1))
                    ev_gather.append(ev_cc0)
                ev_kb_q = []

                P = [dict(psA=None, psF=None, psW=None, B_free=None, V_free=None,
                          Wp_free=None, Kh_free=None) for _ in range(2)]
                ev_psTb = ev_hcol
                ev_ep_last = None
                kcell = {"ev": None, "last_k1": ev_k_last}

                def load_kq(sub):
                    kcell["ev"] = dma(
                        "gpsimd", qK, kbf[:],
                        k_fold[:, c0 + sub * 64:c0 + sub * 64 + 64, :]
                        .rearrange("a c b -> a (c b)"),
                        waits=[ev_st_cc, kcell["last_k1"]])

                def stages_for(c, half=half, c0=c0, P=P, ev_xnb=ev_xnb):
                    par = c % 2
                    S = P[par]
                    pA, pF, pW = psb[par], psb[2 + par], psb[4 + par]
                    o1 = par * N1
                    o2 = par * N2
                    ev_kb = kcell["ev"]
                    klhs = kbf[0:J1, (c % 64) * 128:((c % 64) + 1) * 128]
                    st = {}

                    def s_k1():
                        top(lambda e: e.matmul(pA[:, 0:128], klhs, tb["tD1re"][:],
                                               start=True, stop=True),
                            waits=[ev_kb, S["psA"]], inc=False)
                        st["k1"] = top(lambda e: e.matmul(
                            pA[:, 128:256], klhs, tb["tD1im"][:],
                            start=True, stop=True))
                        kcell["last_k1"] = st["k1"]

                    def s_ktw():
                        ev_free, ev_res = cmul_psum(
                            pA, cBre[:, o1:o1 + N1], cBim[:, o1:o1 + N1], par,
                            waits=[st["k1"], S["B_free"]])
                        S["psA"] = ev_free
                        st["ktw"] = ev_res

                    def s_k2():
                        top(lambda e: e.matmul(pF[:, 0:128], tb["tD2re"][:],
                                               cBre[:, o1:o1 + N1],
                                               start=True, stop=False),
                            waits=[st["ktw"], S["psF"]], inc=False)
                        top(lambda e: e.matmul(pF[:, 0:128], tb["tD2imN"][:],
                                               cBim[:, o1:o1 + N1],
                                               start=False, stop=True), inc=False)
                        top(lambda e: e.matmul(pF[:, 128:256], tb["tD2im"][:],
                                               cBre[:, o1:o1 + N1],
                                               start=True, stop=False), inc=False)
                        st["k2"] = top(lambda e: e.matmul(
                            pF[:, 128:256], tb["tD2re"][:], cBim[:, o1:o1 + N1],
                            start=False, stop=True))
                        st["B_free"] = st["k2"]

                    def s_kev():
                        sop(lambda e: e.activation(
                            cKhre[:, o1:o1 + N1], pF[:, 0:128], AF.Copy,
                            scale=1.0 / NFFT), waits=[st["k2"], S["Kh_free"]])
                        ev = sop(lambda e: e.activation(
                            cKhim[:, o1:o1 + N1], pF[:, 128:256], AF.Copy,
                            scale=1.0 / NFFT))
                        st["kev"] = ev
                        S["psF"] = ev

                    def s_xtr():
                        nonlocal ev_psTb
                        po = par * 256
                        top(lambda e: e.transpose(
                            psTb[0:J1, po:po + 128], xnb_v[0][:, :, c],
                            ident_bf[:]),
                            waits=[ev_xnb[0], ev_xnb[1], ev_psTb], inc=False)
                        ev_t = top(lambda e: e.transpose(
                            psTb[0:J1, po + 128:po + 256], xnb_v[1][:, :, c],
                            ident_bf[:]))
                        sop(lambda e: e.activation(
                            cXre[0:J1, o2:o2 + N2], psTb[0:J1, po:po + 128],
                            AF.Copy), waits=[ev_t])
                        ev = sop(lambda e: e.activation(
                            cXim[0:J1, o2:o2 + N2],
                            psTb[0:J1, po + 128:po + 256], AF.Copy))
                        st["xtr"] = ev
                        ev_psTb = ev

                    def s_u1():
                        xre = cXre[0:J1, o2:o2 + N2]
                        xim = cXim[0:J1, o2:o2 + N2]
                        top(lambda e: e.matmul(pA[:, 0:128], xre, tb["tD1re"][:],
                                               start=True, stop=False),
                            waits=[st["xtr"], S["psA"]], inc=False)
                        top(lambda e: e.matmul(pA[:, 0:128], xim, tb["tD1imN"][:],
                                               start=False, stop=True), inc=False)
                        top(lambda e: e.matmul(pA[:, 128:256], xre, tb["tD1im"][:],
                                               start=True, stop=False), inc=False)
                        st["u1"] = top(lambda e: e.matmul(
                            pA[:, 128:256], xim, tb["tD1re"][:],
                            start=False, stop=True))

                    def s_utw():
                        ev_free, ev_res = cmul_psum(
                            pA, cBre[:, o1:o1 + N1], cBim[:, o1:o1 + N1], par,
                            waits=[st["u1"], st["B_free"]])
                        S["psA"] = ev_free
                        st["utw"] = ev_res

                    def s_u2():
                        top(lambda e: e.matmul(pF[:, 0:128], tb["tD2re"][:],
                                               cBre[:, o1:o1 + N1],
                                               start=True, stop=False),
                            waits=[st["utw"], S["psF"]], inc=False)
                        top(lambda e: e.matmul(pF[:, 0:128], tb["tD2imN"][:],
                                               cBim[:, o1:o1 + N1],
                                               start=False, stop=True), inc=False)
                        top(lambda e: e.matmul(pF[:, 128:256], tb["tD2im"][:],
                                               cBre[:, o1:o1 + N1],
                                               start=True, stop=False), inc=False)
                        st["u2"] = top(lambda e: e.matmul(
                            pF[:, 128:256], tb["tD2re"][:], cBim[:, o1:o1 + N1],
                            start=False, stop=True))
                        st["uB_free"] = st["u2"]

                    def s_pw():
                        o = par * 4 * N1
                        vop(lambda e: e.tensor_tensor(
                            ctmp[:, o:o + N1], pF[:, 0:128],
                            cKhre[:, o1:o1 + N1], ALU.mult),
                            waits=[st["u2"], st["kev"], S["V_free"]])
                        vop(lambda e: e.tensor_tensor(
                            ctmp[:, o + N1:o + 2 * N1], pF[:, 128:256],
                            cKhim[:, o1:o1 + N1], ALU.mult))
                        vop(lambda e: e.tensor_tensor(
                            ctmp[:, o + 2 * N1:o + 3 * N1], pF[:, 0:128],
                            cKhim[:, o1:o1 + N1], ALU.mult))
                        v4 = vop(lambda e: e.tensor_tensor(
                            ctmp[:, o + 3 * N1:o + 4 * N1], pF[:, 128:256],
                            cKhre[:, o1:o1 + N1], ALU.mult))
                        S["psF"] = v4
                        S["Kh_free"] = v4
                        vdrain()
                        gop(lambda e: e.tensor_tensor(
                            cVre[:, o1:o1 + N1], ctmp[:, o:o + N1],
                            ctmp[:, o + N1:o + 2 * N1], ALU.subtract),
                            waits=[v4])
                        gop(lambda e: e.tensor_tensor(
                            cVreN[:, o1:o1 + N1], ctmp[:, o + N1:o + 2 * N1],
                            ctmp[:, o:o + N1], ALU.subtract))
                        ev = gop(lambda e: e.tensor_tensor(
                            cVim[:, o1:o1 + N1], ctmp[:, o + 2 * N1:o + 3 * N1],
                            ctmp[:, o + 3 * N1:o + 4 * N1], ALU.add))
                        gdrain()
                        st["pw"] = ev

                    def s_iA():
                        top(lambda e: e.matmul(pW[:, 0:128], tb["tD2re"][:],
                                               cVre[:, o1:o1 + N1],
                                               start=True, stop=False),
                            waits=[st["pw"], S["psW"]], inc=False)
                        top(lambda e: e.matmul(pW[:, 0:128], tb["tD2im"][:],
                                               cVim[:, o1:o1 + N1],
                                               start=False, stop=True), inc=False)
                        top(lambda e: e.matmul(pW[:, 128:256], tb["tD2re"][:],
                                               cVim[:, o1:o1 + N1],
                                               start=True, stop=False), inc=False)
                        st["iA"] = top(lambda e: e.matmul(
                            pW[:, 128:256], tb["tD2im"][:], cVreN[:, o1:o1 + N1],
                            start=False, stop=True))
                        S["V_free"] = st["iA"]

                    def s_itw():
                        ev_free, ev_res = cmul_psum(
                            pW, cWpre[:, o1:o1 + N1], cWpim[:, o1:o1 + N1], par,
                            waits=[st["iA"], S["Wp_free"]], conj=True)
                        st["W_mm_free"] = ev_free
                        st["itw"] = ev_res

                    def s_wtr():
                        nonlocal ev_psTb
                        po = par * 256
                        top(lambda e: e.transpose(
                            psTb[:, po:po + 128], cWpre[:, o1:o1 + N1],
                            ident_bf[:]), waits=[st["itw"], ev_psTb], inc=False)
                        ev_t = top(lambda e: e.transpose(
                            psTb[:, po + 128:po + 256], cWpim[:, o1:o1 + N1],
                            ident_bf[:]))
                        sop(lambda e: e.activation(
                            cWtre[:, o2:o2 + N2], psTb[:, po:po + 128],
                            AF.Copy), waits=[ev_t])
                        ev = sop(lambda e: e.activation(
                            cWtim[:, o2:o2 + N2], psTb[:, po + 128:po + 256],
                            AF.Copy))
                        st["wtr"] = ev
                        S["Wp_free"] = ev
                        ev_psTb = ev

                    def s_iB():
                        top(lambda e: e.matmul(pW[:, 0:64],
                                               cWtre[:, o2:o2 + N2], tb["tTI1"][:],
                                               start=True, stop=False),
                            waits=[st["wtr"], st["W_mm_free"]], inc=False)
                        top(lambda e: e.matmul(pW[:, 0:64],
                                               cWtim[:, o2:o2 + N2], tb["tTI2N"][:],
                                               start=False, stop=True), inc=False)
                        top(lambda e: e.matmul(pW[:, 64:128],
                                               cWtim[:, o2:o2 + N2], tb["tTI1"][:],
                                               start=True, stop=False), inc=False)
                        st["iB"] = top(lambda e: e.matmul(
                            pW[:, 64:128], cWtre[:, o2:o2 + N2], tb["tTI2"][:],
                            start=False, stop=True))

                    def s_ep():
                        gc = c0 + c
                        vop(lambda e: e.scalar_tensor_tensor(
                            ssb_v[0][:, :, c], xnb_v[0][:, :, c],
                            skipbc[:, gc:gc + 1], pW[:, 0:64],
                            ALU.mult, ALU.add),
                            waits=[st["iB"], ev_ssb_free])
                        ev = vop(lambda e: e.scalar_tensor_tensor(
                            ssb_v[1][:, :, c], xnb_v[1][:, :, c],
                            skipbc[:, gc:gc + 1], pW[:, 64:128],
                            ALU.mult, ALU.add))
                        S["psW"] = ev
                        st["ep"] = ev

                    return st, [s_k1, s_ktw, s_k2, s_kev, s_xtr, s_u1, s_utw,
                                s_u2, s_pw, s_iA, s_itw, s_wtr, s_iB, s_ep]

                last_st = None
                for cp in range(CHH // 2):
                    if cp % 32 == 0:
                        load_kq(cp // 32)
                    sta, fa = stages_for(2 * cp)
                    stb, fb = stages_for(2 * cp + 1)
                    for f1, f2 in zip(fa, fb):
                        f1()
                        f2()
                    last_st = stb
                ev_ep_last = last_st["ep"]
                ev_ep_prev = ev_ep_last
                ev_k_last = kcell["last_k1"]

                ev_sp0 = dma("scalar", qX23[half],
                             cin_s[half, 0].rearrange("p a c -> p (a c)"),
                             ssb[0][:], waits=[ev_ep_last])
                ev_sp1 = dma("scalar", qX23[half],
                             cin_s[half, 1].rearrange("p a c -> p (a c)"),
                             ssb[1][:], waits=[ev_ep_last])
                ev_ssb_free = ev_sp1
                if half == 0:
                    cc_pending.append(ev_sp1)
                else:
                    ev_cc = em.op("gpsimd", lambda e: e.collective_compute(
                        "AllGather", ALU.bypass, replica_groups=RG_QUAD,
                        ins=[cin_s[1]], outs=[cag_s[1]]),
                        waits=[ev_sp1], inc=(cS, 1))
                    ev_gather.append(ev_cc)

            ev_s_ready = ev_gather[-1]
            cstk.close()

            if PHASE < 4:
                allq = [qS, qP, qG, qT2, qX1, qK] + qU + qX23 + qM + qO + qL
                em.op("sync", None, waits=[(q, q.v) for q in allq] + [(cS, cS.v)])
                block = stk.enter_context(nc.Block())
                em.replay(block)
                return nc
            diag = SBT("diag", [128, 96 * 128], BF16)
            hbuf = [SBT(f"hbuf{i}", [128, 32 * (TCH + 2)], BF16) for i in range(3)]
            m1T = [SBT(f"m1T{i}", [128, 8 * (TCH + 2)], BF16) for i in range(3)]
            hs_c = SBT("hs_c", [128, 32 * TCH], BF16)
            m2_c = SBT("m2_c", [128, 32 * TCH], BF16)
            m2i2 = SBT("m2i2", [128, 4 * TCH])
            gtmp = SBT("gtmp", [128, TCH], BF16)
            sred = SBT("sred", [128, TCH], BF16)
            gtr = SBT("gtr", [128, TCH], BF16)
            gtok = SBT("gtok", [128, D], BF16)
            stok = [SBT(f"stok{i}", [128, D], BF16) for i in range(2)]
            gs_t = SBT("gs_t", [128, D], BF16)
            ucmb = [SBT(f"ucmb{i}", [128, D]) for i in range(2)]
            otile = [SBT(f"otile{i}", [128, D]) for i in range(2)]
            bcrow = SBT("bcrow", [1, 512])
            bcrow_b = SBT("bcrow_b", [1, 512], BF16)
            colsc = SBT("colsc", [128, 8])
            hscalebc = SBT("hscalebc", [128, TCH + 2])
            s2bc = SBT("s2bc", [128, TCH])
            s2rbc = SBT("s2rbc", [128, TCH])

            hbuf_v = [x.rearrange("p (a c) -> p a c", c=TCH + 2) for x in hbuf]
            m1T_v = [x.rearrange("p (a c) -> p a c", c=TCH + 2) for x in m1T]
            hs_v = hs_c.rearrange("p (a c) -> p a c", c=TCH)
            m2_v = m2_c.rearrange("p (a c) -> p a c", c=TCH)

            ev_diag = ev_dmask
            for hb in range(32):
                for tap in range(3):
                    ev_diag = vop(lambda e, h=hb, t=tap: e.tensor_scalar(
                        diag[:, (h * 3 + t) * 128:(h * 3 + t + 1) * 128],
                        diagmask[:], cwcols[:, h * 3 + t:h * 3 + t + 1], None,
                        ALU.mult), waits=[ev_dmask, ev_tab, ev_s_ready])

            sload_val = [(qL[kk % 2], (kk // 2 + 1) * 128) for kk in range(NCHUNK)]

            ev_m1 = [None] * (NCHUNK + 2)
            ev_bl1 = [None] * (NCHUNK + 2)
            ev_conv3 = [None] * (NCHUNK + 2)
            ev_comb = [None] * (NCHUNK + 4)
            ev_sload = [None] * (NCHUNK + 2)
            ev_uc = [None] * (NCHUNK + 2)
            ev_st = [None] * (NCHUNK + 4)
            ev_mm_m1 = [None] * (NCHUNK + 2)
            ev_xn_ts = [None] * (NCHUNK + 2)
            ev_sil_last = [None] * (NCHUNK + 2)
            ev_fe_last = [None] * (NCHUNK + 2)
            ev_gt_free = [None] * (NCHUNK + 2)

            ev_hbc_prev = vop(lambda e: e.memset(bcrow[0:1, 0:512], 0.0),
                              waits=[ev_s_ready])
            ev_bc_free = ev_statsT
            ev_pTb_m1 = last_st["wtr"]
            ev_gs_prev = None
            ev_sred_free = None

            for it in range(NCHUNK + 2):
                k = it
                kb = it - 1
                kc = it - 2

                if 0 <= kb < NCHUNK:
                    if kb == 0:
                        ncols, off = TCH + 1, 0
                    elif kb == NCHUNK - 1:
                        ncols, off = TCH + 1, 1
                    else:
                        ncols, off = TCH, 1
                    ev_c = vop(lambda e, c=kb: e.tensor_scalar(
                        colsc[:, 0:1], amax1c_sb[:, c:c + 1], scal_bc[:, 0:1],
                        1.0 / 127.0, ALU.mult, ALU.mult),
                        waits=[ev_bc, ev_hbc_prev,
                               ev_s_ready if kb == 0 else None])
                    vdrain()
                    ev_t = top(lambda e: e.transpose(
                        psb[6][0:1, 256:384], colsc[:, 0:1], ident_f[:]),
                        waits=[ev_c, ev_bc_free])
                    ev_r = vop(lambda e: e.tensor_copy(
                        bcrow[0:1, 1:129], psb[6][0:1, 256:384]), waits=[ev_t])
                    if kb == 0:
                        ev_r = vop(lambda e: e.tensor_copy(
                            bcrow[0:1, 0:1], hrow[0:1, 0:1]), waits=[ev_hrow])
                    if kb == NCHUNK - 1:
                        ev_r = vop(lambda e: e.tensor_copy(
                            bcrow[0:1, 129:130], hrow[0:1, 1:2]),
                            waits=[ev_hrow])
                    vdrain()
                    vop(lambda e: e.tensor_copy(bcrow_b[0:1, 0:130],
                                                bcrow[0:1, 0:130]))
                    vdrain()
                    vop(lambda e: e.tensor_tensor(
                        bcrow[0:1, 256:386], bcrow[0:1, 0:130],
                        bcrow_b[0:1, 0:130], ALU.subtract))
                    vdrain()
                    ev_lo = vop(lambda e: e.tensor_copy(
                        bcrow_b[0:1, 256:386], bcrow[0:1, 256:386]))
                    vdrain()
                    top(lambda e: e.matmul(psb[6][:, 0:130], onesbf[:],
                                           bcrow_b[0:1, 0:130],
                                           start=True, stop=False),
                        waits=[ev_lo, ev_ones], inc=False)
                    ev_bcm = top(lambda e: e.matmul(
                        psb[6][:, 0:130], onesbf[:], bcrow_b[0:1, 256:386],
                        start=False, stop=True))
                    ev_hsb = vop(lambda e: e.tensor_copy(
                        hscalebc[:, 0:130], psb[6][:, 0:130]), waits=[ev_bcm])
                    vdrain()
                    ev_bc_free = ev_hsb
                    ev_hbc_prev = ev_hsb

                    ev_hevac = [None] * 32
                    ev_mm_last = None
                    for hb in range(32):
                        pH = psb[hb % 2]
                        mw = [ev_m1[kb], ev_hsb]
                        if hb >= 2:
                            mw.append(ev_hevac[hb - 2])
                        elif kb >= 1:
                            mw.append(ev_bl1[kb - 1])
                        for cc in range(8):
                            lastmm = (cc == 7)
                            ev_mm = top(lambda e, h=hb, c=cc, n=ncols, of=off,
                                        p=pH, kk=kb: e.matmul(
                                p[:, 0:n],
                                w1qT[:, c * HID + h * 128:c * HID + h * 128 + 128],
                                m1T_v[kk % 3][:, c, of:of + n],
                                start=(c == 0), stop=(c == 7)),
                                waits=mw if cc == 0 else (), inc=lastmm)
                        ev_mm_last = ev_mm
                        ww = [ev_mm]
                        if kb >= 3:
                            ww.append(ev_conv3[kb - 3])
                        ev_e = vop(lambda e, h=hb, n=ncols, of=off, p=pH, kk=kb:
                                   e.tensor_tensor(
                                       hbuf_v[kk % 3][:, h, of:of + n],
                                       p[:, 0:n], hscalebc[:, of:of + n],
                                       ALU.mult), waits=ww)
                        if kb > 0:
                            ev_e = vop(lambda e, h=hb, of=off, p=pH, kk=kb:
                                       e.tensor_tensor(
                                           hbuf_v[(kk - 1) % 3][:, h,
                                                                TCH + 1:TCH + 2],
                                           p[:, 0:1], hscalebc[:, of:of + 1],
                                           ALU.mult))
                        if kb < NCHUNK - 1:
                            ev_e = vop(lambda e, h=hb, n=ncols, of=off, p=pH,
                                       kk=kb: e.tensor_tensor(
                                           hbuf_v[(kk + 1) % 3][:, h, 0:1],
                                           p[:, n - 1:n],
                                           hscalebc[:, of + n - 1:of + n],
                                           ALU.mult))
                        ev_hevac[hb] = ev_e
                    ev_bl1[kb] = ev_hevac[31]
                    ev_mm_m1[kb] = ev_mm_last

                if 0 <= kc < NCHUNK:
                    ev_sil = [None] * 32
                    ev_c3last = None
                    for hb in range(32):
                        pC = psb[2 + hb % 2]
                        ww = [ev_bl1[kc], ev_diag]
                        if kc + 1 < NCHUNK:
                            ww.append(ev_bl1[kc + 1])
                        if hb >= 2:
                            ww.append(ev_sil[hb - 2])
                        elif kc >= 1:
                            ww.append(ev_sil_last[kc - 1])
                        for tap in range(3):
                            ev_c3 = top(lambda e, h=hb, t=tap, kk=kc, p=pC:
                                        e.matmul(
                                p[:, 0:TCH],
                                diag[:, (h * 3 + t) * 128:(h * 3 + t + 1) * 128],
                                hbuf_v[kk % 3][:, h, t:t + TCH],
                                start=(t == 0), stop=(t == 2)),
                                waits=ww if tap == 0 else (), inc=(tap == 2))
                        ev_c3last = ev_c3
                        ev_sil[hb] = sop(lambda e, h=hb, p=pC: e.activation(
                            hs_v[:, h, :], p[:, 0:TCH], AF.Silu,
                            bias=cbcols[:, h:h + 1]), waits=[ev_c3])
                    ev_conv3[kc] = ev_c3last
                    ev_sil_last[kc] = ev_sil[31]

                    ev_red = vop(lambda e: e.tensor_reduce(
                        sred[:], hs_c.rearrange("p (h t) -> p t h", t=TCH),
                        AX.X, ALU.max, apply_absolute_value=True),
                        waits=[ev_sil[31], ev_sred_free])
                    vdrain()
                    ev_tp2a = top(lambda e: e.transpose(
                        psTb[:, 768:896], sred[:], ident_bf[:]),
                        waits=[ev_red, ev_gt_free[kc - 1], ev_pTb_m1])
                    ev_sred_free = ev_tp2a
                    ev_te = sop(lambda e: e.activation(
                        gtr[:, 0:TCH], psTb[:, 768:896], AF.Copy),
                        waits=[ev_tp2a])
                    vop(lambda e: e.tensor_reduce(
                        colsc[:, 1:2], gtr[:, 0:TCH], AX.X, ALU.max,
                        apply_absolute_value=True), waits=[ev_te])
                    vdrain()
                    vop(lambda e: e.tensor_scalar(
                        colsc[:, 2:3], colsc[:, 1:2], 1e-5, None, ALU.max))
                    vdrain()
                    vop(lambda e: e.reciprocal(colsc[:, 3:4], colsc[:, 2:3]))
                    vop(lambda e: e.tensor_scalar(
                        colsc[:, 4:5], colsc[:, 2:3], scal_bc[:, 2:3],
                        1.0 / 127.0, ALU.mult, ALU.mult))
                    vdrain()
                    ev_s2c = vop(lambda e: e.tensor_scalar(
                        colsc[:, 3:4], colsc[:, 3:4], 127.0, None, ALU.mult))
                    vdrain()
                    ev_bcs = []
                    for ci, dst in ((3, s2bc), (4, s2rbc)):
                        ev_t2 = top(lambda e, c=ci: e.transpose(
                            psb[6][0:1, 256:384], colsc[:, c:c + 1], ident_f[:]),
                            waits=[ev_s2c, ev_bc_free])
                        ev_r2 = vop(lambda e: e.tensor_copy(
                            bcrow[0:1, 0:128], psb[6][0:1, 256:384]),
                            waits=[ev_t2])
                        vdrain()
                        vop(lambda e: e.tensor_copy(
                            bcrow_b[0:1, 0:128], bcrow[0:1, 0:128]))
                        vdrain()
                        vop(lambda e: e.tensor_tensor(
                            bcrow[0:1, 256:384], bcrow[0:1, 0:128],
                            bcrow_b[0:1, 0:128], ALU.subtract))
                        vdrain()
                        ev_lo2 = vop(lambda e: e.tensor_copy(
                            bcrow_b[0:1, 256:384], bcrow[0:1, 256:384]))
                        vdrain()
                        top(lambda e: e.matmul(psb[6][:, 0:128], onesbf[:],
                                               bcrow_b[0:1, 0:128],
                                               start=True, stop=False),
                            waits=[ev_lo2, ev_ones], inc=False)
                        ev_m2m = top(lambda e: e.matmul(
                            psb[6][:, 0:128], onesbf[:], bcrow_b[0:1, 256:384],
                            start=False, stop=True))
                        ev_b = vop(lambda e, d=dst: e.tensor_copy(
                            d[:, 0:TCH], psb[6][:, 0:128]), waits=[ev_m2m])
                        vdrain()
                        ev_bc_free = ev_b
                        ev_bcs.append(ev_b)

                    ev_m2done = [None] * 32
                    s2bc_b = s2bc[:].rearrange(
                        "p (a t) -> p a t", a=1).broadcast_to((128, 4, TCH))
                    for tile in range(8):
                        o = tile * 4 * TCH
                        vop(lambda e, oo=o: e.tensor_tensor(
                            m2i2[:].rearrange("p (a t) -> p a t", t=TCH),
                            hs_c[:, oo:oo + 4 * TCH].rearrange(
                                "p (a t) -> p a t", t=TCH),
                            s2bc_b, ALU.mult),
                            waits=[ev_bcs[0], ev_sil[4 * tile + 3]] +
                                  ([ev_fe_last[kc - 1]]
                                   if tile == 0 and kc >= 1 else []))
                        vdrain()
                        ev_m2done[4 * tile + 3] = vop(
                            lambda e, oo=o: e.tensor_scalar(
                                m2_c[:, oo:oo + 4 * TCH], m2i2[:],
                                MAGIC, MAGIC, ALU.add, ALU.subtract))
                        vdrain()
                    ev_m2done[31] = ev_m2done[31]

                    ev_g = [None] * 8
                    ev_gtoke = [None] * 8
                    ev_gt_prev = None
                    for oc in range(8):
                        pB = psb[4 + oc % 2]
                        ww2 = [ev_m2done[31]]
                        if oc >= 2:
                            ww2.append(ev_g[oc - 2])
                        for hb in range(32):
                            ev_f2 = top(lambda e, o=oc, h=hb, p=pB: e.matmul(
                                p[:, 0:TCH],
                                w2qT[:, h * D + o * 128:h * D + o * 128 + 128],
                                m2_v[:, h, :], start=(h == 0), stop=(h == 31)),
                                waits=ww2 if hb == 0 else (), inc=(hb == 31))
                        ev_fe = vop(lambda e, o=oc, p=pB: e.tensor_tensor(
                            ftmp[:, 0:TCH], p[:, 0:TCH], s2rbc[:, 0:TCH],
                            ALU.mult), waits=[ev_f2, ev_bcs[1]])
                        ev_g[oc] = ev_fe
                        ev_sg = sop(lambda e: e.activation(
                            gtmp[:, 0:TCH], ftmp[:, 0:TCH], AF.Silu),
                            waits=[ev_fe,
                                   ev_gt_prev if oc >= 1 else None])
                        em.op("scalar", lambda e: e.drain())
                        po2 = 768 + (oc % 2) * 128
                        ev_gt = ev_gt_prev = top(lambda e, p=po2: e.transpose(
                            psTb[:, p:p + 128], gtmp[:, 0:TCH], ident_bf[:]),
                            waits=[ev_sg, ev_te if oc < 2 else None,
                                   ev_gtoke[oc - 2] if oc >= 2 else None,
                                   ev_gt_free[kc - 1] if oc < 2 else None])
                        ev_gtoke[oc] = sop(lambda e, o=oc, p=po2: e.activation(
                            gtok[:, o * 128:(o + 1) * 128], psTb[:, p:p + 128],
                            AF.Copy), waits=[ev_gt, ev_gs_prev])
                    ev_fe_last[kc] = ev_g[7]
                    ev_gt_free[kc] = ev_gtoke[7]

                    ev_gs = vop(lambda e, kk=kc: e.tensor_tensor(
                        gs_t[:], gtok[:], stok[kk % 2][:], ALU.mult),
                        waits=[ev_gtoke[7], ev_sload[kc]])
                    vdrain()
                    ev_gs_prev = ev_gs
                    ev_o = vop(lambda e, kk=kc: e.tensor_tensor(
                        otile[kk % 2][:], gs_t[:], ucmb[kk % 2][:], ALU.add),
                        waits=[ev_uc[kc], ev_gs,
                               ev_st[kc - 2] if kc >= 2 else None])
                    vdrain()
                    ev_comb[kc] = ev_o
                    ev_st[kc] = dma("scalar", qO[kc % 2],
                                    o_out[kc * TCH:(kc + 1) * TCH, :],
                                    otile[kc % 2][:], waits=[ev_o])

                if k < NCHUNK:
                    lw = [ev_xn_ts[k - 2]] if k >= 2 else \
                        [ev_xnb_last[0], ev_xnb_last[1], ev_ln[-1], ev_sql[-1]]
                    ev_l = dma("scalar", qU[k % 2], utile[k % 2][:, 0:D],
                               ufr[k], waits=lw)
                    gop(lambda e, j=k % 2, c=k: e.tensor_scalar(
                        xnf[:], utile[j][:, 0:D], stats_sb[:, c:c + 1],
                        stats_sb[:, 32 + c:33 + c], ALU.subtract, ALU.mult),
                        waits=[ev_l])
                    gdrain()
                    ev_xn_ts[k] = gop(lambda e, c=k: e.tensor_scalar(
                        m1f[:], xnf[:], s1_sb[:, c:c + 1], MAGIC,
                        ALU.mult, ALU.add))
                    gdrain()
                    m1b = m1bb[k % 2]
                    ev_m1b = gop(lambda e, d=m1b: e.tensor_scalar(
                        d[:], m1f[:], MAGIC, None, ALU.subtract))
                    gdrain()
                    ev_last = None
                    for cc in range(8):
                        po = 512 + (cc % 2) * 128
                        xw = [ev_gt_free[k - 2]] if (cc == 0 and k >= 2) else []
                        ev_tp = top(lambda e, c=cc, p=po, b=m1b: e.transpose(
                            psTb[:, p:p + 128], b[:, c * 128:(c + 1) * 128],
                            ident_bf[:]), waits=[ev_m1b, ev_pTb_m1] + xw)
                        ev_e = sop(lambda e, c=cc, p=po, kk=k: e.activation(
                            m1T_v[kk % 3][:, c, 1:1 + TCH], psTb[:, p:p + 128],
                            AF.Copy), waits=[ev_tp, ev_mm_m1[k - 3],
                                             ev_s_ready if k < 3 else None])
                        ev_pTb_m1 = ev_e
                        ev_last = ev_e
                    if k == 0:
                        for cc in range(8):
                            ev_last = sop(lambda e, c=cc: e.activation(
                                m1T_v[0][:, c, 0:1], halocol[:, c * 2:c * 2 + 1],
                                AF.Copy), waits=[ev_hcol])
                    if k == NCHUNK - 1:
                        for cc in range(8):
                            ev_last = sop(lambda e, c=cc: e.activation(
                                m1T_v[(NCHUNK - 1) % 3][:, c, TCH + 1:TCH + 2],
                                halocol[:, c * 2 + 1:c * 2 + 2], AF.Copy),
                                waits=[ev_hcol])
                    ev_m1[k] = ev_last
                    ev_sload[k] = sload_val[k]
                    ev_uc[k] = dma("scalar", qM[k % 2], ucmb[k % 2][:], ufr[k],
                                   waits=[ev_comb[k - 2],
                                          ev_s_ready if k < 2 else None])

            comb_vals = [ev[1] for ev in ev_comb[:NCHUNK]]
            sready_val = ev_s_ready[1]

            def all_sloads(e):
                r0 = e.partition_id()
                ins = None
                for rr in nc.Switch(engines=[nc.sync], index=[r0], n=8):
                    j = rr % 4
                    bl, th = j >> 1, j & 1
                    e.wait_ge(cS.h, sready_val)
                    for kappa in range(NCHUNK):
                        if kappa >= 2:
                            e.wait_ge(vS.h, comb_vals[kappa - 2])
                        j1v = th * 32 + kappa
                        for hh in range(2):
                            for q in range(4):
                                ins = e.dma_start(
                                    out=stok[kappa % 2][:, (q * 2 + hh) * CHH:
                                                        (q * 2 + hh + 1) * CHH],
                                    in_=cag_s[hh, q, bl, :, j1v, :])
                                ins.then_inc(qL[kappa % 2].h, 16)
                return None

            em.op("sync", all_sloads)
            qL[0].v += (NCHUNK // 2) * 128
            qL[1].v += (NCHUNK // 2) * 128

            allq = [qS, qP, qG, qT2, qX1, qK] + qU + qX23 + qM + qO + qL
            em.op("sync", None, waits=[(q, q.v) for q in allq] + [(cS, cS.v)])

            block = stk.enter_context(nc.Block())
            em.replay(block)
        return nc


_NC_CACHE = {}


def kernel(L0_attn_u, L0_attn_k, L0_attn_skip, L0_ffn_w1, L0_ffn_conv_w,
           L0_ffn_conv_b, L0_ffn_w2):
    in_maps = shard_inputs(L0_attn_u, L0_attn_k, L0_attn_skip, L0_ffn_w1,
                           L0_ffn_conv_w, L0_ffn_conv_b, L0_ffn_w2)
    if "nc" not in _NC_CACHE:
        _NC_CACHE["nc"] = Builder().build()
    nc = _NC_CACHE["nc"]
    res = run_bass_kernel_spmd(nc, in_maps, core_ids=list(range(8)))
    out = np.empty((B, L, D), dtype=np.float32)
    for r in range(8):
        bq, th = r // 2, r % 2
        out[bq, th * TOK:(th + 1) * TOK, :] = res.results[r]["o_out"]
    return out


# version 09
# speedup vs baseline: 1.3860x; 1.1189x over previous
import numpy as np
import ml_dtypes
from contextlib import ExitStack

import concourse.bass as bass
import concourse.mybir as mybir
from concourse.bass_utils import run_bass_kernel_spmd

F32 = mybir.dt.float32
BF16 = mybir.dt.bfloat16
FP8 = mybir.dt.float8e4
I16 = mybir.dt.int16
AF = mybir.ActivationFunctionType
ALU = mybir.AluOpType
AX = mybir.AxisListType
BF = ml_dtypes.bfloat16

B, L, D, HID = 4, 8192, 1024, 4096
N1 = N2 = 128
NFFT = N1 * N2
J1 = 64
CHQ, CHH = 256, 128
TOK, TCH = 4096, 128
NCHUNK = TOK // TCH
MAGIC = float(np.float32(1.5 * 2 ** 23))
PHASE = 4
RG_QUAD = [[0, 1, 2, 3], [4, 5, 6, 7]]
ENGS = ("sync", "scalar", "vector", "tensor", "gpsimd")


def _tables():
    w = np.exp(-2j * np.pi / NFFT)
    w1 = np.exp(-2j * np.pi / N1)
    j1 = np.arange(J1); k1 = np.arange(N1); j2 = np.arange(N2); k2 = np.arange(N2)
    D1 = w1 ** (j1[:, None] * k1[None, :])
    D2 = w1 ** (j2[:, None] * k2[None, :])
    T = w ** (j2[:, None] * k1[None, :])
    cD1 = np.conj(w1 ** (k1[:, None] * j1[None, :]))
    t = {
        "tD1re": D1.real, "tD1im": D1.imag, "tD1imN": -D1.imag,
        "tD2re": D2.real, "tD2im": D2.imag, "tD2imN": -D2.imag,
        "tTre": T.real, "tTim": T.imag,
        "tTI1": cD1.real, "tTI2": cD1.imag, "tTI2N": -cD1.imag,
    }
    out = {kk: np.ascontiguousarray(v.astype(BF)) for kk, v in t.items()}
    out["ident_bf"] = np.eye(128, dtype=BF)
    out["ident_f32"] = np.eye(128, dtype=np.float32)
    return out


def shard_inputs(u, k, skip, w1, cw, cb, w2):
    tabs = _tables()
    cw2 = np.ascontiguousarray(np.asarray(cw, dtype=np.float32).reshape(HID, 3))
    cw_cols = np.ascontiguousarray(cw2.reshape(32, 128, 3).transpose(1, 0, 2))
    cb_cols = np.ascontiguousarray(
        np.asarray(cb, dtype=np.float32).reshape(32, 128).T)
    u = np.asarray(u, dtype=np.float32)
    k = np.asarray(k, dtype=np.float32)
    skip = np.asarray(skip, dtype=np.float32)
    w1 = np.ascontiguousarray(np.asarray(w1, dtype=np.float32))
    w2 = np.ascontiguousarray(np.asarray(w2, dtype=np.float32))
    maps = []
    for r in range(8):
        Q, dq = r // 4, r % 4
        bq, th = r // 2, r % 2
        c0 = CHQ * dq
        m = {
            "u_ffn": np.ascontiguousarray(u[bq, th * TOK:(th + 1) * TOK, :]),
            "u_halo": np.ascontiguousarray(u[bq, TOK - 1:TOK + 1, :]),
            "halo_mask": np.array([[1.0 if th == 1 else 0.0,
                                    1.0 if th == 0 else 0.0]], dtype=np.float32),
            "u_conv": np.ascontiguousarray(u[2 * Q:2 * Q + 2, :, c0:c0 + CHQ]),
            "k_fold": np.ascontiguousarray(
                k[c0:c0 + CHQ, :].reshape(CHQ, J1, N2).transpose(1, 0, 2)),
            "skip_s": np.ascontiguousarray(skip[c0:c0 + CHQ].reshape(1, CHQ)),
            "w1": w1, "w2": w2, "cw_cols": cw_cols, "cb_cols": cb_cols,
        }
        m.update(tabs)
        maps.append(m)
    return maps


class Sem:
    def __init__(self, stk, nc, name):
        self.h = stk.enter_context(nc.semaphore(name))
        self.v = 0
        self.name = name


class Emit:
    def __init__(self):
        self.prog = {e: [] for e in ENGS}

    def op(self, eng, fn, waits=(), inc=None):
        ev = None
        hinc = None
        if inc is not None:
            sem, amt = inc
            sem.v += amt
            ev = (sem, sem.v)
            hinc = (sem.h, amt)
        ws = []
        for wv in waits:
            if wv is None:
                continue
            s, v = wv
            ws.append((s.h, v))
        self.prog[eng].append((ws, fn, hinc))
        return ev

    def replay(self, block):
        def make(name):
            items = self.prog[name]

            def body(eng):
                for waits, fn, inc in items:
                    for sem, val in waits:
                        eng.wait_ge(sem, val)
                    if fn is not None:
                        ins = fn(eng)
                        if inc is not None:
                            ins.then_inc(inc[0], inc[1])
            return body

        for name in ENGS:
            getattr(block, name)(make(name))


class Builder:
    def build(self):
        nc = bass.Bass("TRN2", target_bir_lowering=False)
        self.nc = nc
        em = Emit()
        self.em = em

        u_ffn = nc.dram_tensor("u_ffn", [TOK, D], F32, kind="ExternalInput").ap()
        u_halo = nc.dram_tensor("u_halo", [2, D], F32, kind="ExternalInput").ap()
        halo_mask = nc.dram_tensor("halo_mask", [1, 2], F32, kind="ExternalInput").ap()
        u_conv = nc.dram_tensor("u_conv", [2, L, CHQ], F32, kind="ExternalInput").ap()
        k_fold = nc.dram_tensor("k_fold", [J1, CHQ, N2], F32, kind="ExternalInput").ap()
        skip_s = nc.dram_tensor("skip_s", [1, CHQ], F32, kind="ExternalInput").ap()
        w1_d = nc.dram_tensor("w1", [HID, D], F32, kind="ExternalInput").ap()
        w2_d = nc.dram_tensor("w2", [D, HID], F32, kind="ExternalInput").ap()
        cwc_d = nc.dram_tensor("cw_cols", [128, 32, 3], F32, kind="ExternalInput").ap()
        cbc_d = nc.dram_tensor("cb_cols", [128, 32], F32, kind="ExternalInput").ap()
        tabd = {}
        for name in ("tD1re", "tD1im", "tD1imN"):
            tabd[name] = nc.dram_tensor(name, [J1, N1], BF16, kind="ExternalInput").ap()
        for name in ("tD2re", "tD2im", "tD2imN", "tTre", "tTim"):
            tabd[name] = nc.dram_tensor(name, [N2, N2], BF16, kind="ExternalInput").ap()
        for name in ("tTI1", "tTI2", "tTI2N"):
            tabd[name] = nc.dram_tensor(name, [N1, J1], BF16, kind="ExternalInput").ap()
        ident_bf_d = nc.dram_tensor("ident_bf", [128, 128], BF16, kind="ExternalInput").ap()
        ident_f_d = nc.dram_tensor("ident_f32", [128, 128], F32, kind="ExternalInput").ap()
        o_out = nc.dram_tensor("o_out", [TOK, D], F32, kind="ExternalOutput").ap()

        cin_stats = nc.dram_tensor("cin_stats", [J1, 128], F32, kind="Internal").ap()
        cag_stats = nc.dram_tensor("cag_stats", [4, J1, 128], F32, kind="Internal").ap()
        cin_s = nc.dram_tensor("cin_s", [2, 2, 128, J1, CHH], BF16,
                               kind="Internal").ap()
        cag_s = nc.dram_tensor("cag_s", [2, 4, 2, 128, J1, CHH], BF16,
                               kind="Internal").ap()
        d_scr1 = nc.dram_tensor("d_scr1", [128, 32], F32, kind="Internal").ap()
        d_scr2 = nc.dram_tensor("d_scr2", [128, 8], F32, kind="Internal").ap()
        d_row = nc.dram_tensor("d_row", [1, 16], F32, kind="Internal").ap()
        d_row2 = nc.dram_tensor("d_row2", [2, 2], F32, kind="Internal").ap()

        stk = ExitStack()
        with stk:
            SBT = lambda name, shape, dt=F32: stk.enter_context(
                nc.sbuf_tensor(name, shape, dt))
            utile = [SBT(f"utile{i}", [128, D]) for i in range(2)]
            xnf = SBT("xnf", [128, D])
            ftmp = SBT("ftmp", [128, TCH])
            m1f = xnf
            m1bb = [SBT(f"m1b{i}", [128, D], BF16) for i in range(2)]
            stats_sb = SBT("stats_sb", [128, J1])
            amax1c_sb = SBT("amax1c_sb", [128, 32])
            s1_sb = SBT("s1_sb", [128, 32])
            tb = {}
            for name, shape in (
                ("tD1re", [J1, N1]), ("tD1im", [J1, N1]), ("tD1imN", [J1, N1]),
                ("tD2re", [N2, N2]), ("tD2im", [N2, N2]), ("tD2imN", [N2, N2]),
                ("tTre", [N2, N2]), ("tTim", [N2, N2]),
                ("tTI1", [N1, J1]), ("tTI2", [N1, J1]), ("tTI2N", [N1, J1]),
            ):
                tb[name] = SBT("s" + name, shape, BF16)
            ident_bf = SBT("sident_bf", [128, 128], BF16)
            ident_f = SBT("sident_f", [128, 128])
            w1qT = SBT("w1qT", [128, 8 * HID], FP8)
            w2qT = SBT("w2qT", [128, 32 * D], FP8)
            scal_bc = SBT("scal_bc", [128, 4])
            cwcols = SBT("cwcols", [128, 96])
            cbcols = SBT("cbcols", [128, 32])
            iomask = SBT("iomask", [128, 128], I16)
            diagmask = SBT("diagmask", [128, 128], BF16)
            onesbf = SBT("onesbf", [1, 128], BF16)
            epscol = SBT("epscol", [128, 1])
            halocol = SBT("halocol", [128, 16], BF16)
            halosc = SBT("halosc", [2, 8])
            hrow = SBT("hrow", [1, 2])
            hmask = SBT("hmask", [2, 1])
            smallf = SBT("smallf", [128, 8])
            srow = SBT("srow", [2, 256])
            statsT = [SBT(f"statsT{i}", [128, J1]) for i in range(4)]
            statg = SBT("statg", [J1, 128])
            skipbc = SBT("skipbc", [128, CHQ])
            haloxn = SBT("haloxn", [2, D])
            halom = SBT("halom", [2, D], BF16)
            halotmp = SBT("halotmp", [2, D])
            ctmp = SBT("ctmp", [N2, 8 * N1])

            PSB = lambda name, dt=F32, n=512: stk.enter_context(
                nc.psum_tensor(name, [128, n], dt))
            psb = [PSB(f"psb{i}") for i in range(7)]
            psTb = PSB("psTb", BF16, 1024)

            qS = Sem(stk, nc, "qS")
            qP = Sem(stk, nc, "qP")
            qG = Sem(stk, nc, "qG")
            qU = [Sem(stk, nc, "qU0"), Sem(stk, nc, "qU1")]
            qT2 = Sem(stk, nc, "qT2")
            qX1 = Sem(stk, nc, "qX1")
            qX23 = [Sem(stk, nc, "qX2"), Sem(stk, nc, "qX3")]
            qM = [Sem(stk, nc, "qM0"), Sem(stk, nc, "qM1")]
            qO = [Sem(stk, nc, "qO0"), Sem(stk, nc, "qO1")]
            qK = Sem(stk, nc, "qK")
            qL = [Sem(stk, nc, "qL0"), Sem(stk, nc, "qL1")]
            tS = Sem(stk, nc, "tS")
            vS = Sem(stk, nc, "vS")
            sS = Sem(stk, nc, "sS")
            gS = Sem(stk, nc, "gS")
            cS = Sem(stk, nc, "cS")

            def dma(eng, q, out, in_, waits=()):
                return em.op(eng, lambda e, o=out, i=in_: e.dma_start(out=o, in_=i),
                             waits=waits, inc=(q, 16))

            def vop(fn, waits=()):
                return em.op("vector", fn, waits=waits, inc=(vS, 1))

            def sop(fn, waits=()):
                return em.op("scalar", fn, waits=waits, inc=(sS, 1))

            def top(fn, waits=(), inc=True):
                return em.op("tensor", fn, waits=waits,
                             inc=(tS, 1) if inc else None)

            def gop(fn, waits=()):
                return em.op("gpsimd", fn, waits=waits, inc=(gS, 1))

            def vdrain():
                em.op("vector", lambda e: e.drain())

            def gdrain():
                em.op("gpsimd", lambda e: e.drain())

            def cmul_psum(ps, out_re, out_im, par, waits, conj=False):
                o = par * 4 * N1
                v1 = vop(lambda e: e.tensor_tensor(
                    ctmp[:, o:o + N1], ps[:, 0:128], tb["tTre"][:], ALU.mult),
                    waits=waits)
                vop(lambda e: e.tensor_tensor(
                    ctmp[:, o + N1:o + 2 * N1], ps[:, 128:256], tb["tTim"][:],
                    ALU.mult))
                vop(lambda e: e.tensor_tensor(
                    ctmp[:, o + 2 * N1:o + 3 * N1], ps[:, 0:128], tb["tTim"][:],
                    ALU.mult))
                v4 = vop(lambda e: e.tensor_tensor(
                    ctmp[:, o + 3 * N1:o + 4 * N1], ps[:, 128:256], tb["tTre"][:],
                    ALU.mult))
                vdrain()
                if not conj:
                    gop(lambda e: e.tensor_tensor(
                        out_re, ctmp[:, o:o + N1], ctmp[:, o + N1:o + 2 * N1],
                        ALU.subtract), waits=[v4])
                    ev = gop(lambda e: e.tensor_tensor(
                        out_im, ctmp[:, o + 2 * N1:o + 3 * N1],
                        ctmp[:, o + 3 * N1:o + 4 * N1], ALU.add))
                else:
                    gop(lambda e: e.tensor_tensor(
                        out_re, ctmp[:, o:o + N1], ctmp[:, o + N1:o + 2 * N1],
                        ALU.add), waits=[v4])
                    ev = gop(lambda e: e.tensor_tensor(
                        out_im, ctmp[:, o + 3 * N1:o + 4 * N1],
                        ctmp[:, o + 2 * N1:o + 3 * N1], ALU.subtract))
                gdrain()
                return v4, ev

            for name in tb:
                dma("sync", qS, tb[name][:], tabd[name])
            dma("sync", qS, ident_bf[:], ident_bf_d)
            dma("sync", qS, ident_f[:], ident_f_d)
            dma("sync", qS, cwcols[:], cwc_d.rearrange("p a b -> p (a b)"))
            dma("sync", qS, cbcols[:], cbc_d)
            dma("sync", qS, skipbc[:], skip_s.broadcast_to((128, CHQ)))
            dma("sync", qS, hmask[:], halo_mask.rearrange("o t -> t o"))
            ev_tab = dma("sync", qS, haloxn[:], u_halo)
            ev_ones = vop(lambda e: e.memset(onesbf[:], 1.0), waits=[ev_tab])
            ev_eps = vop(lambda e: e.memset(epscol[:], 1e-5))
            ev_iota = gop(lambda e: e.iota(iomask[:], pattern=[[1, 128]], base=0,
                                           channel_multiplier=-1))
            ev_dmask = vop(lambda e: e.tensor_scalar(
                diagmask[:], iomask[:], 0.0, None, ALU.is_equal), waits=[ev_iota])
            vdrain()

            if PHASE < 1:
                allq = [qS, qP, qG, qT2, qX1, qK] + qU + qX23 + qM + qO + qL
                em.op("sync", None, waits=[(q, q.v) for q in allq] + [(cS, cS.v)])
                block = stk.enter_context(nc.Block())
                em.replay(block)
                return nc
            wstk = ExitStack()
            WSBT = lambda name, shape, dt=F32: wstk.enter_context(
                nc.sbuf_tensor(name, shape, dt))
            wt = [WSBT(f"wt{i}", [128, HID]) for i in range(2)]
            wabs = WSBT("wabs", [128, HID])
            wq_b = WSBT("wq_b", [128, HID], BF16)
            absacc = WSBT("absacc", [128, 48])

            specs = [(w1_d, 32, 0, D), (w2_d, 8, 32, HID)]
            ev_abs = []
            for wd, ntile, coff, width in specs:
                wr = wd.rearrange("(n p) d -> n p d", p=128)
                for i in range(ntile):
                    waits = [ev_abs[-2]] if len(ev_abs) >= 2 else [ev_tab]
                    ev_l = dma("sync", qU[(coff + i) % 2],
                               wt[(coff + i) % 2][:, 0:width], wr[i],
                               waits=waits)
                    ev_a = sop(lambda e, j=(coff + i) % 2, c=coff + i, w=width:
                               e.activation(wabs[:, 0:w], wt[j][:, 0:w], AF.Abs,
                                            accum_out=absacc[:, c:c + 1]),
                               waits=[ev_l])
                    em.op("scalar", lambda e: e.drain())
                    ev_abs.append(ev_a)
            if PHASE < 1.2:
                allq = [qS, qP, qG, qT2, qX1, qK] + qU + qX23 + qM + qO + qL
                em.op("sync", None, waits=[(q, q.v) for q in allq] + [(cS, cS.v)])
                wstk.close()
                block = stk.enter_context(nc.Block())
                em.replay(block)
                return nc
            ev_c1 = vop(lambda e: e.tensor_reduce(
                smallf[:, 0:1], absacc[:, 0:32], AX.X, ALU.add),
                waits=[ev_abs[-1]])
            ev_c2 = vop(lambda e: e.tensor_reduce(
                smallf[:, 1:2], absacc[:, 32:40], AX.X, ALU.add))
            vdrain()
            ev_ct = top(lambda e: e.transpose(
                psb[6][0:2, 0:128], smallf[:, 0:2], ident_f[:]),
                waits=[ev_c2])
            ev_cr = vop(lambda e: e.tensor_copy(
                halotmp[0:2, 0:128], psb[6][0:2, 0:128]), waits=[ev_ct])
            vdrain()
            ev_s1 = vop(lambda e: e.tensor_reduce(
                smallf[0:2, 0:1], halotmp[0:2, 0:128], AX.X, ALU.add))
            vdrain()
            vop(lambda e: e.tensor_scalar(
                smallf[0:2, 2:3], smallf[0:2, 0:1], 1.0 / (HID * D), 1e-5,
                ALU.mult, ALU.max))
            vdrain()
            vop(lambda e: e.reciprocal(smallf[0:2, 3:4], smallf[0:2, 2:3]))
            vdrain()
            ev_pk = vop(lambda e: e.tensor_copy(srow[0:2, 0:2],
                                                smallf[0:2, 2:4]))
            ev_sd = dma("scalar", qX1, d_row2, srow[0:2, 0:2], waits=[ev_pk])
            ev_bc = dma("scalar", qX1, scal_bc[:],
                        d_row2.rearrange("a b -> (a b)").unsqueeze(0)
                        .broadcast_to((128, 4)), waits=[ev_sd])

            if PHASE < 1.5:
                allq = [qS, qP, qG, qT2, qX1, qK] + qU + qX23 + qM + qO + qL
                em.op("sync", None, waits=[(q, q.v) for q in allq] + [(cS, cS.v)])
                wstk.close()
                block = stk.enter_context(nc.Block())
                em.replay(block)
                return nc
            ev_q = []
            ev_half = [None, None]
            ev_wq_last = ev_bc
            for wi, (wd, ntile, coff, width) in enumerate(specs):
                wr = wd.rearrange("(n p) d -> n p d", p=128)
                for i in range(ntile):
                    waits = [ev_q[-2]] if len(ev_q) >= 2 else [ev_bc]
                    ev_l = dma("sync", qU[i % 2], wt[i % 2][:, 0:width], wr[i],
                               waits=waits)
                    vop(lambda e, j=i % 2, w=width, c=1 + 2 * wi:
                        e.tensor_scalar(wabs[:, 0:w], wt[j][:, 0:w],
                                        scal_bc[:, c:c + 1], MAGIC,
                                        ALU.mult, ALU.add), waits=[ev_l, ev_bc])
                    vdrain()
                    vop(lambda e, w=width: e.tensor_scalar(
                        wabs[:, 0:w], wabs[:, 0:w], MAGIC, None, ALU.subtract))
                    vdrain()
                    ev_t3 = vop(lambda e, w=width: e.tensor_scalar(
                        wq_b[:, 0:w], wabs[:, 0:w], 1.0, -1.0, ALU.min, ALU.max))
                    vdrain()
                    ev_q.append(ev_t3)
                    for cc in range(width // 128 if PHASE >= 1.8 else 0):
                        h = cc % 2
                        if h == 0:
                            pq = psTb[:, 0:128]
                            ev_tp = top(lambda e, c=cc, d=pq: e.transpose(
                                d, wq_b[:, c * 128:(c + 1) * 128], ident_bf[:]),
                                waits=[ev_t3, ev_half[h]])
                        else:
                            pq = psb[5][:, 0:128]
                            ev_tp = top(lambda e, c=cc, d=pq: e.matmul(
                                d, wq_b[:, c * 128:(c + 1) * 128], ident_bf[:],
                                start=True, stop=True),
                                waits=[ev_t3, ev_half[h]])
                        if wi == 0:
                            fn = lambda e, c=cc, ii=i, d=pq: e.tensor_copy(
                                w1qT[:, c * HID + ii * 128:c * HID + ii * 128 + 128],
                                d)
                        else:
                            fn = lambda e, c=cc, ii=i, d=pq: e.tensor_copy(
                                w2qT[:, c * D + ii * 128:c * D + ii * 128 + 128],
                                d)
                        ev_e = vop(fn, waits=[ev_tp])
                        ev_half[h] = ev_e
                        ev_wq_last = ev_e
            ev_wprep = ev_wq_last
            wstk.close()
            if PHASE < 2:
                pass

            if PHASE < 2:
                allq = [qS, qP, qG, qT2, qX1, qK] + qU + qX23 + qM + qO + qL
                em.op("sync", None, waits=[(q, q.v) for q in allq] + [(cS, cS.v)])
                block = stk.enter_context(nc.Block())
                em.replay(block)
                return nc
            ev_ln = []
            ev_sql = []
            ufr = u_ffn.rearrange("(n p) d -> n p d", p=128)
            for i in range(32):
                waits = [ev_wprep] if i < 2 else [ev_ln[i - 2], ev_sql[i - 2]]
                ev_l = dma("sync", qU[i % 2], utile[i % 2][:, 0:D], ufr[i],
                           waits=waits)
                ev_sq = sop(lambda e, j=i % 2: e.activation(
                    xnf[:, 0:D], utile[j][:, 0:D], AF.Square,
                    accum_out=smallf[:, 0:1]),
                    waits=[ev_l] + ([ev_sql[i - 1], ev_ln[i - 1]] if i >= 1
                                    else [ev_wprep]))
                em.op("scalar", lambda e: e.drain())
                ev_sql.append(ev_sq)
                vop(lambda e, j=i % 2: e.tensor_reduce(
                    smallf[:, 1:2], utile[j][:, 0:D], AX.X, ALU.add),
                    waits=[ev_l, ev_sq])
                vop(lambda e, j=i % 2: e.tensor_reduce(
                    smallf[:, 2:3], utile[j][:, 0:D], AX.X, ALU.max))
                vop(lambda e, j=i % 2: e.tensor_reduce(
                    smallf[:, 3:4], utile[j][:, 0:D], AX.X, ALU.min))
                vdrain()
                vop(lambda e, c=i: e.tensor_scalar(
                    stats_sb[:, c:c + 1], smallf[:, 1:2], 1.0 / D, None, ALU.mult))
                vop(lambda e: e.tensor_scalar(
                    smallf[:, 4:5], smallf[:, 0:1], 1.0 / D, None, ALU.mult))
                vdrain()
                vop(lambda e, c=i: e.tensor_tensor(
                    smallf[:, 5:6], stats_sb[:, c:c + 1], stats_sb[:, c:c + 1],
                    ALU.mult))
                vdrain()
                ev_v2 = vop(lambda e: e.tensor_tensor(
                    smallf[:, 4:5], smallf[:, 4:5], smallf[:, 5:6], ALU.subtract))
                ev_sqx = sop(lambda e: e.activation(
                    smallf[:, 5:6], smallf[:, 4:5], AF.Sqrt,
                    bias=epscol[:, 0:1]), waits=[ev_v2, ev_eps])
                ev_rs = vop(lambda e, c=i: e.reciprocal(
                    stats_sb[:, 32 + c:33 + c], smallf[:, 5:6]), waits=[ev_sqx])
                vdrain()
                vop(lambda e, c=i: e.tensor_tensor(
                    smallf[:, 6:7], smallf[:, 2:3], stats_sb[:, c:c + 1],
                    ALU.subtract))
                vop(lambda e, c=i: e.tensor_tensor(
                    smallf[:, 7:8], stats_sb[:, c:c + 1], smallf[:, 3:4],
                    ALU.subtract))
                vdrain()
                vop(lambda e: e.tensor_tensor(
                    smallf[:, 6:7], smallf[:, 6:7], smallf[:, 7:8], ALU.max))
                vdrain()
                vop(lambda e, c=i: e.tensor_tensor(
                    smallf[:, 6:7], smallf[:, 6:7], stats_sb[:, 32 + c:33 + c],
                    ALU.mult))
                vdrain()
                vop(lambda e, c=i: e.tensor_scalar(
                    amax1c_sb[:, c:c + 1], smallf[:, 6:7], 1e-5, None, ALU.max))
                vdrain()
                vop(lambda e, c=i: e.reciprocal(
                    smallf[:, 7:8], amax1c_sb[:, c:c + 1]))
                vdrain()
                ev_f = vop(lambda e, c=i: e.tensor_scalar(
                    s1_sb[:, c:c + 1], smallf[:, 7:8], 127.0, None, ALU.mult))
                vdrain()
                ev_ln.append(ev_f)

            ev_st_tp = top(lambda e: e.transpose(
                psb[6][0:64, 0:128], stats_sb[:, 0:J1], ident_f[:]),
                waits=[ev_ln[-1]])
            ev_st_ev = vop(lambda e: e.tensor_copy(
                statg[:, 0:128], psb[6][0:J1, 0:128]), waits=[ev_st_tp])
            ev_st_sp = dma("scalar", qX1, cin_stats, statg[:], waits=[ev_st_ev])
            ev_st_cc = em.op("gpsimd", lambda e: e.collective_compute(
                "AllGather", ALU.bypass, replica_groups=RG_QUAD,
                ins=[cin_stats], outs=[cag_stats]),
                waits=[ev_st_sp], inc=(cS, 1))

            ev_h_sq = sop(lambda e: e.activation(
                halotmp[:], haloxn[:], AF.Square, accum_out=halosc[:, 0:1]),
                waits=[ev_tab, ev_ln[-1], ev_sql[-1]])
            em.op("scalar", lambda e: e.drain())
            vop(lambda e: e.tensor_reduce(
                halosc[:, 1:2], haloxn[:], AX.X, ALU.add),
                waits=[ev_tab, ev_ln[-1], ev_h_sq])
            vop(lambda e: e.tensor_reduce(halosc[:, 2:3], haloxn[:], AX.X, ALU.max))
            vop(lambda e: e.tensor_reduce(halosc[:, 3:4], haloxn[:], AX.X, ALU.min))
            vdrain()
            vop(lambda e: e.tensor_scalar(
                halosc[:, 0:2], halosc[:, 0:2], 1.0 / D, None, ALU.mult))
            vdrain()
            vop(lambda e: e.tensor_tensor(
                halotmp[:, 0:1], halosc[:, 1:2], halosc[:, 1:2], ALU.mult))
            vdrain()
            ev_hv = vop(lambda e: e.tensor_tensor(
                halosc[:, 0:1], halosc[:, 0:1], halotmp[:, 0:1], ALU.subtract))
            ev_hs = sop(lambda e: e.activation(
                halotmp[:, 0:1], halosc[:, 0:1], AF.Sqrt,
                bias=epscol[0:2, 0:1]), waits=[ev_hv, ev_eps])
            vop(lambda e: e.reciprocal(halosc[:, 0:1], halotmp[:, 0:1]),
                waits=[ev_hs])
            vdrain()
            vop(lambda e: e.tensor_scalar(
                haloxn[:], haloxn[:], halosc[:, 1:2], halosc[:, 0:1],
                ALU.subtract, ALU.mult))
            vdrain()
            vop(lambda e: e.tensor_tensor(
                halotmp[:, 1:2], halosc[:, 2:3], halosc[:, 1:2], ALU.subtract))
            vop(lambda e: e.tensor_tensor(
                halotmp[:, 2:3], halosc[:, 1:2], halosc[:, 3:4], ALU.subtract))
            vdrain()
            vop(lambda e: e.tensor_tensor(
                halotmp[:, 1:2], halotmp[:, 1:2], halotmp[:, 2:3], ALU.max))
            vdrain()
            vop(lambda e: e.tensor_tensor(
                halotmp[:, 1:2], halotmp[:, 1:2], halosc[:, 0:1], ALU.mult))
            vdrain()
            vop(lambda e: e.tensor_scalar(
                halosc[:, 2:3], halotmp[:, 1:2], 1e-5, None, ALU.max))
            vdrain()
            vop(lambda e: e.reciprocal(halotmp[:, 1:2], halosc[:, 2:3]))
            vdrain()
            vop(lambda e: e.tensor_scalar(
                halotmp[:, 1:2], halotmp[:, 1:2], 127.0, None, ALU.mult))
            vdrain()
            vop(lambda e: e.tensor_scalar(
                haloxn[:], haloxn[:], halotmp[:, 1:2], MAGIC,
                ALU.mult, ALU.add))
            vdrain()
            vop(lambda e: e.tensor_scalar(halom[:], haloxn[:], MAGIC, None,
                                          ALU.subtract))
            vdrain()
            ev_hm = vop(lambda e: e.tensor_scalar(
                halom[:], halom[:], hmask[:, 0:1], None, ALU.mult))
            vdrain()
            ev_hh = vop(lambda e: e.tensor_scalar(
                halosc[:, 3:4], halosc[:, 2:3], scal_bc[0:2, 0:1], 1.0 / 127.0,
                ALU.mult, ALU.mult), waits=[ev_bc])
            vdrain()
            ev_hrow = dma("sync", qT2, hrow[0:1, 0:2], halosc[0:2, 3:4],
                          waits=[ev_hh])
            ev_hcol = None
            for cc in range(8):
                po = 512 + (cc % 2) * 2
                ev_tp = top(lambda e, c=cc, p=po: e.transpose(
                    psTb[:, p:p + 2], halom[:, c * 128:(c + 1) * 128],
                    ident_bf[0:2, 0:2]), waits=[ev_hm, ev_hcol])
                ev_hcol = vop(lambda e, c=cc, p=po: e.tensor_copy(
                    halocol[:, c * 2:c * 2 + 2], psTb[:, p:p + 2]), waits=[ev_tp])

            if PHASE < 3:
                allq = [qS, qP, qG, qT2, qX1, qK] + qU + qX23 + qM + qO + qL
                em.op("sync", None, waits=[(q, q.v) for q in allq] + [(cS, cS.v)])
                block = stk.enter_context(nc.Block())
                em.replay(block)
                return nc
            ev_statsT = None
            for blk in range(4):
                ev_lg = dma("sync", qT2, statg[:], cag_stats[blk],
                            waits=[ev_st_cc, ev_statsT, ev_hrow])
                ev_tp = top(lambda e: e.transpose(
                    psb[6][:, 0:64], statg[:, 0:128], ident_f[0:J1, 0:J1]),
                    waits=[ev_lg, ev_statsT])
                ev_statsT = vop(lambda e, b=blk: e.tensor_copy(
                    statsT[b][:], psb[6][:, 0:64]), waits=[ev_tp])

            cstk = ExitStack()
            CSBT = lambda name, shape, dt=F32: cstk.enter_context(
                nc.sbuf_tensor(name, shape, dt))
            xnb = [CSBT(f"xnb{i}", [128, J1 * CHH], BF16) for i in range(2)]
            ssb = [CSBT(f"ssb{i}", [128, J1 * CHH], BF16) for i in range(2)]
            kbf = CSBT("kbf", [J1, 64 * N2], BF16)
            cXre = CSBT("cXre", [J1, N2 * 2], BF16)
            cXim = CSBT("cXim", [J1, N2 * 2], BF16)
            cBre = CSBT("cBre", [N2, N1 * 2], BF16)
            cBim = CSBT("cBim", [N2, N1 * 2], BF16)
            cKhre = CSBT("cKhre", [N2, N1 * 2], BF16)
            cKhim = CSBT("cKhim", [N2, N1 * 2], BF16)
            cVre = CSBT("cVre", [N2, N1 * 2], BF16)
            cVim = CSBT("cVim", [N2, N1 * 2], BF16)
            cVreN = CSBT("cVreN", [N2, N1 * 2], BF16)
            cWpre = CSBT("cWpre", [N2, N1 * 2], BF16)
            cWpim = CSBT("cWpim", [N2, N1 * 2], BF16)
            cWtre = CSBT("cWtre", [N1, N2 * 2], BF16)
            cWtim = CSBT("cWtim", [N1, N2 * 2], BF16)
            xnb_v = [x.rearrange("p (a c) -> p a c", c=CHH) for x in xnb]
            ssb_v = [x.rearrange("p (a c) -> p a c", c=CHH) for x in ssb]

            ev_gather = []
            ev_ssb_free = None
            ev_k_last = None
            ev_xnb_last = None
            ev_ep_prev = None
            cc_pending = []
            for half in range(2):
                c0 = half * CHH
                ev_fill = []
                for bl in range(2):
                    for tau in range(J1):
                        idx = bl * J1 + tau
                        waits = [ev_fill[idx - 2]] if idx >= 2 else \
                            [ev_statsT, ev_ep_prev]
                        ev_l = dma("sync", qU[idx % 2], utile[idx % 2][:, 0:CHH],
                                   u_conv[bl].rearrange("(n p) c -> n p c", p=128)
                                   [tau, :, c0:c0 + CHH], waits=waits)
                        blk = 2 * bl + (tau // 32)
                        tl = tau % 32
                        eng = vop if idx % 2 == 0 else gop
                        ev_x = eng(lambda e, j=idx % 2, b=bl, t=tau, bk=blk, tt=tl:
                                   e.tensor_scalar(
                                       xnb_v[b][:, t, :], utile[j][:, 0:CHH],
                                       statsT[bk][:, tt:tt + 1],
                                       statsT[bk][:, 32 + tt:33 + tt],
                                       ALU.subtract, ALU.mult), waits=[ev_l])
                        ev_fill.append(ev_x)
                ev_xnb = (ev_fill[-1], ev_fill[-2])
                ev_xnb_last = ev_xnb
                if cc_pending:
                    ev_cc0 = em.op("gpsimd", lambda e: e.collective_compute(
                        "AllGather", ALU.bypass, replica_groups=RG_QUAD,
                        ins=[cin_s[0]], outs=[cag_s[0]]),
                        waits=[cc_pending.pop()], inc=(cS, 1))
                    ev_gather.append(ev_cc0)
                ev_kb_q = []

                P = [dict(psA=None, psF=None, psW=None, B_free=None, V_free=None,
                          Wp_free=None, Kh_free=None) for _ in range(2)]
                ev_psTb = ev_hcol
                ev_ep_last = None
                kcell = {"ev": None, "last_k1": ev_k_last}

                def load_kq(sub):
                    kcell["ev"] = dma(
                        "gpsimd", qK, kbf[:],
                        k_fold[:, c0 + sub * 64:c0 + sub * 64 + 64, :]
                        .rearrange("a c b -> a (c b)"),
                        waits=[ev_st_cc, kcell["last_k1"]])

                def stages_for(c, half=half, c0=c0, P=P, ev_xnb=ev_xnb):
                    par = c % 2
                    S = P[par]
                    pA, pF, pW = psb[par], psb[2 + par], psb[4 + par]
                    o1 = par * N1
                    o2 = par * N2
                    ev_kb = kcell["ev"]
                    klhs = kbf[0:J1, (c % 64) * 128:((c % 64) + 1) * 128]
                    st = {}

                    def s_k1():
                        top(lambda e: e.matmul(pA[:, 0:128], klhs, tb["tD1re"][:],
                                               start=True, stop=True),
                            waits=[ev_kb, S["psA"]], inc=False)
                        st["k1"] = top(lambda e: e.matmul(
                            pA[:, 128:256], klhs, tb["tD1im"][:],
                            start=True, stop=True))
                        kcell["last_k1"] = st["k1"]

                    def s_ktw():
                        ev_free, ev_res = cmul_psum(
                            pA, cBre[:, o1:o1 + N1], cBim[:, o1:o1 + N1], par,
                            waits=[st["k1"], S["B_free"]])
                        S["psA"] = ev_free
                        st["ktw"] = ev_res

                    def s_k2():
                        top(lambda e: e.matmul(pF[:, 0:128], tb["tD2re"][:],
                                               cBre[:, o1:o1 + N1],
                                               start=True, stop=False),
                            waits=[st["ktw"], S["psF"]], inc=False)
                        top(lambda e: e.matmul(pF[:, 0:128], tb["tD2imN"][:],
                                               cBim[:, o1:o1 + N1],
                                               start=False, stop=True), inc=False)
                        top(lambda e: e.matmul(pF[:, 128:256], tb["tD2im"][:],
                                               cBre[:, o1:o1 + N1],
                                               start=True, stop=False), inc=False)
                        st["k2"] = top(lambda e: e.matmul(
                            pF[:, 128:256], tb["tD2re"][:], cBim[:, o1:o1 + N1],
                            start=False, stop=True))
                        st["B_free"] = st["k2"]

                    def s_kev():
                        sop(lambda e: e.activation(
                            cKhre[:, o1:o1 + N1], pF[:, 0:128], AF.Copy,
                            scale=1.0 / NFFT), waits=[st["k2"], S["Kh_free"]])
                        ev = sop(lambda e: e.activation(
                            cKhim[:, o1:o1 + N1], pF[:, 128:256], AF.Copy,
                            scale=1.0 / NFFT))
                        st["kev"] = ev
                        S["psF"] = ev

                    def s_xtr():
                        nonlocal ev_psTb
                        po = par * 256
                        top(lambda e: e.transpose(
                            psTb[0:J1, po:po + 128], xnb_v[0][:, :, c],
                            ident_bf[:]),
                            waits=[ev_xnb[0], ev_xnb[1], ev_psTb], inc=False)
                        ev_t = top(lambda e: e.transpose(
                            psTb[0:J1, po + 128:po + 256], xnb_v[1][:, :, c],
                            ident_bf[:]))
                        sop(lambda e: e.activation(
                            cXre[0:J1, o2:o2 + N2], psTb[0:J1, po:po + 128],
                            AF.Copy), waits=[ev_t])
                        ev = sop(lambda e: e.activation(
                            cXim[0:J1, o2:o2 + N2],
                            psTb[0:J1, po + 128:po + 256], AF.Copy))
                        st["xtr"] = ev
                        ev_psTb = ev

                    def s_u1():
                        xre = cXre[0:J1, o2:o2 + N2]
                        xim = cXim[0:J1, o2:o2 + N2]
                        top(lambda e: e.matmul(pA[:, 0:128], xre, tb["tD1re"][:],
                                               start=True, stop=False),
                            waits=[st["xtr"], S["psA"]], inc=False)
                        top(lambda e: e.matmul(pA[:, 0:128], xim, tb["tD1imN"][:],
                                               start=False, stop=True), inc=False)
                        top(lambda e: e.matmul(pA[:, 128:256], xre, tb["tD1im"][:],
                                               start=True, stop=False), inc=False)
                        st["u1"] = top(lambda e: e.matmul(
                            pA[:, 128:256], xim, tb["tD1re"][:],
                            start=False, stop=True))

                    def s_utw():
                        ev_free, ev_res = cmul_psum(
                            pA, cBre[:, o1:o1 + N1], cBim[:, o1:o1 + N1], par,
                            waits=[st["u1"], st["B_free"]])
                        S["psA"] = ev_free
                        st["utw"] = ev_res

                    def s_u2():
                        top(lambda e: e.matmul(pF[:, 0:128], tb["tD2re"][:],
                                               cBre[:, o1:o1 + N1],
                                               start=True, stop=False),
                            waits=[st["utw"], S["psF"]], inc=False)
                        top(lambda e: e.matmul(pF[:, 0:128], tb["tD2imN"][:],
                                               cBim[:, o1:o1 + N1],
                                               start=False, stop=True), inc=False)
                        top(lambda e: e.matmul(pF[:, 128:256], tb["tD2im"][:],
                                               cBre[:, o1:o1 + N1],
                                               start=True, stop=False), inc=False)
                        st["u2"] = top(lambda e: e.matmul(
                            pF[:, 128:256], tb["tD2re"][:], cBim[:, o1:o1 + N1],
                            start=False, stop=True))
                        st["uB_free"] = st["u2"]

                    def s_pw():
                        o = par * 4 * N1
                        vop(lambda e: e.tensor_tensor(
                            ctmp[:, o:o + N1], pF[:, 0:128],
                            cKhre[:, o1:o1 + N1], ALU.mult),
                            waits=[st["u2"], st["kev"], S["V_free"]])
                        vop(lambda e: e.tensor_tensor(
                            ctmp[:, o + N1:o + 2 * N1], pF[:, 128:256],
                            cKhim[:, o1:o1 + N1], ALU.mult))
                        vop(lambda e: e.tensor_tensor(
                            ctmp[:, o + 2 * N1:o + 3 * N1], pF[:, 0:128],
                            cKhim[:, o1:o1 + N1], ALU.mult))
                        v4 = vop(lambda e: e.tensor_tensor(
                            ctmp[:, o + 3 * N1:o + 4 * N1], pF[:, 128:256],
                            cKhre[:, o1:o1 + N1], ALU.mult))
                        S["psF"] = v4
                        S["Kh_free"] = v4
                        vdrain()
                        gop(lambda e: e.tensor_tensor(
                            cVre[:, o1:o1 + N1], ctmp[:, o:o + N1],
                            ctmp[:, o + N1:o + 2 * N1], ALU.subtract),
                            waits=[v4])
                        gop(lambda e: e.tensor_tensor(
                            cVreN[:, o1:o1 + N1], ctmp[:, o + N1:o + 2 * N1],
                            ctmp[:, o:o + N1], ALU.subtract))
                        ev = gop(lambda e: e.tensor_tensor(
                            cVim[:, o1:o1 + N1], ctmp[:, o + 2 * N1:o + 3 * N1],
                            ctmp[:, o + 3 * N1:o + 4 * N1], ALU.add))
                        gdrain()
                        st["pw"] = ev

                    def s_iA():
                        top(lambda e: e.matmul(pW[:, 0:128], tb["tD2re"][:],
                                               cVre[:, o1:o1 + N1],
                                               start=True, stop=False),
                            waits=[st["pw"], S["psW"]], inc=False)
                        top(lambda e: e.matmul(pW[:, 0:128], tb["tD2im"][:],
                                               cVim[:, o1:o1 + N1],
                                               start=False, stop=True), inc=False)
                        top(lambda e: e.matmul(pW[:, 128:256], tb["tD2re"][:],
                                               cVim[:, o1:o1 + N1],
                                               start=True, stop=False), inc=False)
                        st["iA"] = top(lambda e: e.matmul(
                            pW[:, 128:256], tb["tD2im"][:], cVreN[:, o1:o1 + N1],
                            start=False, stop=True))
                        S["V_free"] = st["iA"]

                    def s_itw():
                        ev_free, ev_res = cmul_psum(
                            pW, cWpre[:, o1:o1 + N1], cWpim[:, o1:o1 + N1], par,
                            waits=[st["iA"], S["Wp_free"]], conj=True)
                        st["W_mm_free"] = ev_free
                        st["itw"] = ev_res

                    def s_wtr():
                        nonlocal ev_psTb
                        po = par * 256
                        top(lambda e: e.transpose(
                            psTb[:, po:po + 128], cWpre[:, o1:o1 + N1],
                            ident_bf[:]), waits=[st["itw"], ev_psTb], inc=False)
                        ev_t = top(lambda e: e.transpose(
                            psTb[:, po + 128:po + 256], cWpim[:, o1:o1 + N1],
                            ident_bf[:]))
                        sop(lambda e: e.activation(
                            cWtre[:, o2:o2 + N2], psTb[:, po:po + 128],
                            AF.Copy), waits=[ev_t])
                        ev = sop(lambda e: e.activation(
                            cWtim[:, o2:o2 + N2], psTb[:, po + 128:po + 256],
                            AF.Copy))
                        st["wtr"] = ev
                        S["Wp_free"] = ev
                        ev_psTb = ev

                    def s_iB():
                        top(lambda e: e.matmul(pW[:, 0:64],
                                               cWtre[:, o2:o2 + N2], tb["tTI1"][:],
                                               start=True, stop=False),
                            waits=[st["wtr"], st["W_mm_free"]], inc=False)
                        top(lambda e: e.matmul(pW[:, 0:64],
                                               cWtim[:, o2:o2 + N2], tb["tTI2N"][:],
                                               start=False, stop=True), inc=False)
                        top(lambda e: e.matmul(pW[:, 64:128],
                                               cWtim[:, o2:o2 + N2], tb["tTI1"][:],
                                               start=True, stop=False), inc=False)
                        st["iB"] = top(lambda e: e.matmul(
                            pW[:, 64:128], cWtre[:, o2:o2 + N2], tb["tTI2"][:],
                            start=False, stop=True))

                    def s_ep():
                        gc = c0 + c
                        vop(lambda e: e.scalar_tensor_tensor(
                            ssb_v[0][:, :, c], xnb_v[0][:, :, c],
                            skipbc[:, gc:gc + 1], pW[:, 0:64],
                            ALU.mult, ALU.add),
                            waits=[st["iB"], ev_ssb_free])
                        ev = vop(lambda e: e.scalar_tensor_tensor(
                            ssb_v[1][:, :, c], xnb_v[1][:, :, c],
                            skipbc[:, gc:gc + 1], pW[:, 64:128],
                            ALU.mult, ALU.add))
                        S["psW"] = ev
                        st["ep"] = ev

                    return st, [s_k1, s_ktw, s_k2, s_kev, s_xtr, s_u1, s_utw,
                                s_u2, s_pw, s_iA, s_itw, s_wtr, s_iB, s_ep]

                last_st = None
                for cp in range(CHH // 2):
                    if cp % 32 == 0:
                        load_kq(cp // 32)
                    sta, fa = stages_for(2 * cp)
                    stb, fb = stages_for(2 * cp + 1)
                    for f1, f2 in zip(fa, fb):
                        f1()
                        f2()
                    last_st = stb
                ev_ep_last = last_st["ep"]
                ev_ep_prev = ev_ep_last
                ev_k_last = kcell["last_k1"]

                ev_sp0 = dma("scalar", qX23[half],
                             cin_s[half, 0].rearrange("p a c -> p (a c)"),
                             ssb[0][:], waits=[ev_ep_last])
                ev_sp1 = dma("scalar", qX23[half],
                             cin_s[half, 1].rearrange("p a c -> p (a c)"),
                             ssb[1][:], waits=[ev_ep_last])
                ev_ssb_free = ev_sp1
                if half == 0:
                    cc_pending.append(ev_sp1)
                else:
                    ev_cc = em.op("gpsimd", lambda e: e.collective_compute(
                        "AllGather", ALU.bypass, replica_groups=RG_QUAD,
                        ins=[cin_s[1]], outs=[cag_s[1]]),
                        waits=[ev_sp1], inc=(cS, 1))
                    ev_gather.append(ev_cc)

            ev_s_ready = ev_gather[-1]
            cstk.close()

            if PHASE < 4:
                allq = [qS, qP, qG, qT2, qX1, qK] + qU + qX23 + qM + qO + qL
                em.op("sync", None, waits=[(q, q.v) for q in allq] + [(cS, cS.v)])
                block = stk.enter_context(nc.Block())
                em.replay(block)
                return nc
            diag = SBT("diag", [128, 96 * 128], BF16)
            hbuf = [SBT(f"hbuf{i}", [128, 32 * (TCH + 2)], BF16) for i in range(3)]
            m1T = [SBT(f"m1T{i}", [128, 8 * (TCH + 2)], BF16) for i in range(3)]
            hs_c = SBT("hs_c", [128, 32 * TCH], BF16)
            m2_c = SBT("m2_c", [128, 32 * TCH], BF16)
            m2i2 = SBT("m2i2", [128, 4 * TCH])
            gtmp = SBT("gtmp", [128, TCH], BF16)
            sred = SBT("sred", [128, TCH], BF16)
            gtr = SBT("gtr", [128, TCH], BF16)
            gtok = SBT("gtok", [128, D], BF16)
            stok = [SBT(f"stok{i}", [128, D], BF16) for i in range(2)]
            gs_t = SBT("gs_t", [128, D], BF16)
            ucmb = [SBT(f"ucmb{i}", [128, D]) for i in range(2)]
            otile = [SBT(f"otile{i}", [128, D]) for i in range(2)]
            bcrow = SBT("bcrow", [1, 512])
            bcrow_b = SBT("bcrow_b", [1, 512], BF16)
            colsc = SBT("colsc", [128, 8])
            hscalebc = SBT("hscalebc", [128, TCH + 2])
            s2bc = SBT("s2bc", [128, TCH])
            s2rbc = SBT("s2rbc", [128, TCH])

            hbuf_v = [x.rearrange("p (a c) -> p a c", c=TCH + 2) for x in hbuf]
            m1T_v = [x.rearrange("p (a c) -> p a c", c=TCH + 2) for x in m1T]
            hs_v = hs_c.rearrange("p (a c) -> p a c", c=TCH)
            m2_v = m2_c.rearrange("p (a c) -> p a c", c=TCH)

            ev_diag = ev_dmask
            for hb in range(32):
                for tap in range(3):
                    ev_diag = vop(lambda e, h=hb, t=tap: e.tensor_scalar(
                        diag[:, (h * 3 + t) * 128:(h * 3 + t + 1) * 128],
                        diagmask[:], cwcols[:, h * 3 + t:h * 3 + t + 1], None,
                        ALU.mult), waits=[ev_dmask, ev_tab, ev_s_ready])

            sload_val = [(qL[kk % 2], (kk // 2 + 1) * 128) for kk in range(NCHUNK)]

            ev_m1 = [None] * (NCHUNK + 2)
            ev_bl1 = [None] * (NCHUNK + 2)
            ev_conv3 = [None] * (NCHUNK + 2)
            ev_comb = [None] * (NCHUNK + 4)
            ev_sload = [None] * (NCHUNK + 2)
            ev_uc = [None] * (NCHUNK + 2)
            ev_st = [None] * (NCHUNK + 4)
            ev_mm_m1 = [None] * (NCHUNK + 2)
            ev_xn_ts = [None] * (NCHUNK + 2)
            ev_sil_last = [None] * (NCHUNK + 2)
            ev_fe_last = [None] * (NCHUNK + 2)
            ev_gt_free = [None] * (NCHUNK + 2)

            ev_hbc_prev = vop(lambda e: e.memset(bcrow[0:1, 0:512], 0.0),
                              waits=[ev_s_ready])
            ev_bc_free = ev_statsT
            ev_pTb_m1 = last_st["wtr"]
            ev_gs_prev = None
            ev_sred_free = None

            for it in range(NCHUNK + 2):
                k = it
                kb = it - 1
                kc = it - 2

                if 0 <= kb < NCHUNK:
                    if kb == 0:
                        ncols, off = TCH + 1, 0
                    elif kb == NCHUNK - 1:
                        ncols, off = TCH + 1, 1
                    else:
                        ncols, off = TCH, 1
                    ev_c = vop(lambda e, c=kb: e.tensor_scalar(
                        colsc[:, 0:1], amax1c_sb[:, c:c + 1], scal_bc[:, 0:1],
                        1.0 / 127.0, ALU.mult, ALU.mult),
                        waits=[ev_bc, ev_hbc_prev,
                               ev_s_ready if kb == 0 else None])
                    vdrain()
                    ev_t = top(lambda e: e.transpose(
                        psb[6][0:1, 256:384], colsc[:, 0:1], ident_f[:]),
                        waits=[ev_c, ev_bc_free])
                    ev_r = vop(lambda e: e.tensor_copy(
                        bcrow[0:1, 1:129], psb[6][0:1, 256:384]), waits=[ev_t])
                    if kb == 0:
                        ev_r = vop(lambda e: e.tensor_copy(
                            bcrow[0:1, 0:1], hrow[0:1, 0:1]), waits=[ev_hrow])
                    if kb == NCHUNK - 1:
                        ev_r = vop(lambda e: e.tensor_copy(
                            bcrow[0:1, 129:130], hrow[0:1, 1:2]),
                            waits=[ev_hrow])
                    vdrain()
                    vop(lambda e: e.tensor_copy(bcrow_b[0:1, 0:130],
                                                bcrow[0:1, 0:130]))
                    vdrain()
                    vop(lambda e: e.tensor_tensor(
                        bcrow[0:1, 256:386], bcrow[0:1, 0:130],
                        bcrow_b[0:1, 0:130], ALU.subtract))
                    vdrain()
                    ev_lo = vop(lambda e: e.tensor_copy(
                        bcrow_b[0:1, 256:386], bcrow[0:1, 256:386]))
                    vdrain()
                    top(lambda e: e.matmul(psb[6][:, 0:130], onesbf[:],
                                           bcrow_b[0:1, 0:130],
                                           start=True, stop=False),
                        waits=[ev_lo, ev_ones], inc=False)
                    ev_bcm = top(lambda e: e.matmul(
                        psb[6][:, 0:130], onesbf[:], bcrow_b[0:1, 256:386],
                        start=False, stop=True))
                    ev_hsb = vop(lambda e: e.tensor_copy(
                        hscalebc[:, 0:130], psb[6][:, 0:130]), waits=[ev_bcm])
                    vdrain()
                    ev_bc_free = ev_hsb
                    ev_hbc_prev = ev_hsb

                    ev_hevac = [None] * 32
                    ev_mm_last = None
                    for hb in range(32):
                        pH = psb[hb % 2]
                        mw = [ev_m1[kb], ev_hsb]
                        if hb >= 2:
                            mw.append(ev_hevac[hb - 2])
                        elif kb >= 1:
                            mw.append(ev_bl1[kb - 1])
                        for cc in range(8):
                            lastmm = (cc == 7)
                            ev_mm = top(lambda e, h=hb, c=cc, n=ncols, of=off,
                                        p=pH, kk=kb: e.matmul(
                                p[:, 0:n],
                                w1qT[:, c * HID + h * 128:c * HID + h * 128 + 128],
                                m1T_v[kk % 3][:, c, of:of + n],
                                start=(c == 0), stop=(c == 7)),
                                waits=mw if cc == 0 else (), inc=lastmm)
                        ev_mm_last = ev_mm
                        ww = [ev_mm]
                        if kb >= 3:
                            ww.append(ev_conv3[kb - 3])
                        ev_e = vop(lambda e, h=hb, n=ncols, of=off, p=pH, kk=kb:
                                   e.tensor_tensor(
                                       hbuf_v[kk % 3][:, h, of:of + n],
                                       p[:, 0:n], hscalebc[:, of:of + n],
                                       ALU.mult), waits=ww)
                        if kb > 0:
                            ev_e = vop(lambda e, h=hb, of=off, p=pH, kk=kb:
                                       e.tensor_tensor(
                                           hbuf_v[(kk - 1) % 3][:, h,
                                                                TCH + 1:TCH + 2],
                                           p[:, 0:1], hscalebc[:, of:of + 1],
                                           ALU.mult))
                        if kb < NCHUNK - 1:
                            ev_e = vop(lambda e, h=hb, n=ncols, of=off, p=pH,
                                       kk=kb: e.tensor_tensor(
                                           hbuf_v[(kk + 1) % 3][:, h, 0:1],
                                           p[:, n - 1:n],
                                           hscalebc[:, of + n - 1:of + n],
                                           ALU.mult))
                        ev_hevac[hb] = ev_e
                    ev_bl1[kb] = ev_hevac[31]
                    ev_mm_m1[kb] = ev_mm_last

                if 0 <= kc < NCHUNK:
                    ev_sil = [None] * 32
                    ev_c3last = None
                    for hb in range(32):
                        pC = psb[2 + hb % 2]
                        ww = [ev_bl1[kc], ev_diag]
                        if kc + 1 < NCHUNK:
                            ww.append(ev_bl1[kc + 1])
                        if hb >= 2:
                            ww.append(ev_sil[hb - 2])
                        elif kc >= 1:
                            ww.append(ev_sil_last[kc - 1])
                        for tap in range(3):
                            ev_c3 = top(lambda e, h=hb, t=tap, kk=kc, p=pC:
                                        e.matmul(
                                p[:, 0:TCH],
                                diag[:, (h * 3 + t) * 128:(h * 3 + t + 1) * 128],
                                hbuf_v[kk % 3][:, h, t:t + TCH],
                                start=(t == 0), stop=(t == 2)),
                                waits=ww if tap == 0 else (), inc=(tap == 2))
                        ev_c3last = ev_c3
                        ev_sil[hb] = sop(lambda e, h=hb, p=pC: e.activation(
                            hs_v[:, h, :], p[:, 0:TCH], AF.Silu,
                            bias=cbcols[:, h:h + 1]), waits=[ev_c3])
                    ev_conv3[kc] = ev_c3last
                    ev_sil_last[kc] = ev_sil[31]

                    ev_red = vop(lambda e: e.tensor_reduce(
                        sred[:], hs_c.rearrange("p (h t) -> p t h", t=TCH),
                        AX.X, ALU.max, apply_absolute_value=True),
                        waits=[ev_sil[31], ev_sred_free])
                    vdrain()
                    ev_tp2a = top(lambda e: e.transpose(
                        psTb[:, 768:896], sred[:], ident_bf[:]),
                        waits=[ev_red, ev_gt_free[kc - 1], ev_pTb_m1])
                    ev_sred_free = ev_tp2a
                    ev_te = sop(lambda e: e.activation(
                        gtr[:, 0:TCH], psTb[:, 768:896], AF.Copy),
                        waits=[ev_tp2a])
                    vop(lambda e: e.tensor_reduce(
                        colsc[:, 1:2], gtr[:, 0:TCH], AX.X, ALU.max,
                        apply_absolute_value=True), waits=[ev_te])
                    vdrain()
                    vop(lambda e: e.tensor_scalar(
                        colsc[:, 2:3], colsc[:, 1:2], 1e-5, None, ALU.max))
                    vdrain()
                    vop(lambda e: e.reciprocal(colsc[:, 3:4], colsc[:, 2:3]))
                    vop(lambda e: e.tensor_scalar(
                        colsc[:, 4:5], colsc[:, 2:3], scal_bc[:, 2:3],
                        1.0 / 127.0, ALU.mult, ALU.mult))
                    vdrain()
                    ev_s2c = vop(lambda e: e.tensor_scalar(
                        colsc[:, 3:4], colsc[:, 3:4], 127.0, None, ALU.mult))
                    vdrain()
                    ev_bcs = []
                    for ci, dst in ((3, s2bc), (4, s2rbc)):
                        ev_t2 = top(lambda e, c=ci: e.transpose(
                            psb[6][0:1, 256:384], colsc[:, c:c + 1], ident_f[:]),
                            waits=[ev_s2c, ev_bc_free])
                        ev_r2 = vop(lambda e: e.tensor_copy(
                            bcrow[0:1, 0:128], psb[6][0:1, 256:384]),
                            waits=[ev_t2])
                        vdrain()
                        vop(lambda e: e.tensor_copy(
                            bcrow_b[0:1, 0:128], bcrow[0:1, 0:128]))
                        vdrain()
                        vop(lambda e: e.tensor_tensor(
                            bcrow[0:1, 256:384], bcrow[0:1, 0:128],
                            bcrow_b[0:1, 0:128], ALU.subtract))
                        vdrain()
                        ev_lo2 = vop(lambda e: e.tensor_copy(
                            bcrow_b[0:1, 256:384], bcrow[0:1, 256:384]))
                        vdrain()
                        top(lambda e: e.matmul(psb[6][:, 0:128], onesbf[:],
                                               bcrow_b[0:1, 0:128],
                                               start=True, stop=False),
                            waits=[ev_lo2, ev_ones], inc=False)
                        ev_m2m = top(lambda e: e.matmul(
                            psb[6][:, 0:128], onesbf[:], bcrow_b[0:1, 256:384],
                            start=False, stop=True))
                        ev_b = vop(lambda e, d=dst: e.tensor_copy(
                            d[:, 0:TCH], psb[6][:, 0:128]), waits=[ev_m2m])
                        vdrain()
                        ev_bc_free = ev_b
                        ev_bcs.append(ev_b)

                    ev_m2done = [None] * 32
                    s2bc_b = s2bc[:].rearrange(
                        "p (a t) -> p a t", a=1).broadcast_to((128, 4, TCH))
                    for tile in range(8):
                        o = tile * 4 * TCH
                        vop(lambda e, oo=o: e.tensor_tensor(
                            m2i2[:].rearrange("p (a t) -> p a t", t=TCH),
                            hs_c[:, oo:oo + 4 * TCH].rearrange(
                                "p (a t) -> p a t", t=TCH),
                            s2bc_b, ALU.mult),
                            waits=[ev_bcs[0], ev_sil[4 * tile + 3]] +
                                  ([ev_fe_last[kc - 1]]
                                   if tile == 0 and kc >= 1 else []))
                        vdrain()
                        ev_m2done[4 * tile + 3] = vop(
                            lambda e, oo=o: e.tensor_scalar(
                                m2_c[:, oo:oo + 4 * TCH], m2i2[:],
                                MAGIC, MAGIC, ALU.add, ALU.subtract))
                        vdrain()
                    ev_m2done[31] = ev_m2done[31]

                    ev_g = [None] * 8
                    ev_gtoke = [None] * 8
                    ev_gt_prev = None
                    for oc in range(8):
                        pB = psb[4 + oc % 2]
                        ww2 = [ev_m2done[3]]
                        if oc >= 2:
                            ww2.append(ev_g[oc - 2])
                        for hb in range(32):
                            if hb % 4 == 0 and hb > 0 and oc == 0:
                                hw = [ev_m2done[hb + 3]]
                            else:
                                hw = ww2 if hb == 0 else []
                            ev_f2 = top(lambda e, o=oc, h=hb, p=pB: e.matmul(
                                p[:, 0:TCH],
                                w2qT[:, h * D + o * 128:h * D + o * 128 + 128],
                                m2_v[:, h, :], start=(h == 0), stop=(h == 31)),
                                waits=hw, inc=(hb == 31))
                        ev_fe = vop(lambda e, o=oc, p=pB: e.tensor_tensor(
                            ftmp[:, 0:TCH], p[:, 0:TCH], s2rbc[:, 0:TCH],
                            ALU.mult), waits=[ev_f2, ev_bcs[1]])
                        ev_g[oc] = ev_fe
                        ev_sg = sop(lambda e: e.activation(
                            gtmp[:, 0:TCH], ftmp[:, 0:TCH], AF.Silu),
                            waits=[ev_fe,
                                   ev_gt_prev if oc >= 1 else None])
                        em.op("scalar", lambda e: e.drain())
                        po2 = 768 + (oc % 2) * 128
                        ev_gt = ev_gt_prev = top(lambda e, p=po2: e.transpose(
                            psTb[:, p:p + 128], gtmp[:, 0:TCH], ident_bf[:]),
                            waits=[ev_sg, ev_te if oc < 2 else None,
                                   ev_gtoke[oc - 2] if oc >= 2 else None,
                                   ev_gt_free[kc - 1] if oc < 2 else None])
                        ev_gtoke[oc] = sop(lambda e, o=oc, p=po2: e.activation(
                            gtok[:, o * 128:(o + 1) * 128], psTb[:, p:p + 128],
                            AF.Copy), waits=[ev_gt, ev_gs_prev])
                    ev_fe_last[kc] = ev_g[7]
                    ev_gt_free[kc] = ev_gtoke[7]

                    ev_gs = vop(lambda e, kk=kc: e.tensor_tensor(
                        gs_t[:], gtok[:], stok[kk % 2][:], ALU.mult),
                        waits=[ev_gtoke[7], ev_sload[kc]])
                    vdrain()
                    ev_gs_prev = ev_gs
                    ev_o = vop(lambda e, kk=kc: e.tensor_tensor(
                        otile[kk % 2][:], gs_t[:], ucmb[kk % 2][:], ALU.add),
                        waits=[ev_uc[kc], ev_gs,
                               ev_st[kc - 2] if kc >= 2 else None])
                    vdrain()
                    ev_comb[kc] = ev_o
                    ev_st[kc] = dma("scalar", qO[kc % 2],
                                    o_out[kc * TCH:(kc + 1) * TCH, :],
                                    otile[kc % 2][:], waits=[ev_o])

                if k < NCHUNK:
                    lw = [ev_xn_ts[k - 2]] if k >= 2 else \
                        [ev_xnb_last[0], ev_xnb_last[1], ev_ln[-1], ev_sql[-1]]
                    ev_l = dma("scalar", qU[k % 2], utile[k % 2][:, 0:D],
                               ufr[k], waits=lw)
                    vop(lambda e, j=k % 2, c=k: e.tensor_scalar(
                        xnf[:], utile[j][:, 0:D], stats_sb[:, c:c + 1],
                        stats_sb[:, 32 + c:33 + c], ALU.subtract, ALU.mult),
                        waits=[ev_l])
                    vdrain()
                    ev_xn_ts[k] = vop(lambda e, c=k: e.tensor_scalar(
                        m1f[:], xnf[:], s1_sb[:, c:c + 1], MAGIC,
                        ALU.mult, ALU.add))
                    vdrain()
                    m1b = m1bb[k % 2]
                    ev_m1b = vop(lambda e, d=m1b: e.tensor_scalar(
                        d[:], m1f[:], MAGIC, None, ALU.subtract))
                    vdrain()
                    ev_last = None
                    for cc in range(8):
                        po = 512 + (cc % 2) * 128
                        xw = [ev_gt_free[k - 2]] if (cc == 0 and k >= 2) else []
                        ev_tp = top(lambda e, c=cc, p=po, b=m1b: e.transpose(
                            psTb[:, p:p + 128], b[:, c * 128:(c + 1) * 128],
                            ident_bf[:]), waits=[ev_m1b, ev_pTb_m1] + xw)
                        ev_e = sop(lambda e, c=cc, p=po, kk=k: e.activation(
                            m1T_v[kk % 3][:, c, 1:1 + TCH], psTb[:, p:p + 128],
                            AF.Copy), waits=[ev_tp, ev_mm_m1[k - 3],
                                             ev_s_ready if k < 3 else None])
                        ev_pTb_m1 = ev_e
                        ev_last = ev_e
                    if k == 0:
                        for cc in range(8):
                            ev_last = sop(lambda e, c=cc: e.activation(
                                m1T_v[0][:, c, 0:1], halocol[:, c * 2:c * 2 + 1],
                                AF.Copy), waits=[ev_hcol])
                    if k == NCHUNK - 1:
                        for cc in range(8):
                            ev_last = sop(lambda e, c=cc: e.activation(
                                m1T_v[(NCHUNK - 1) % 3][:, c, TCH + 1:TCH + 2],
                                halocol[:, c * 2 + 1:c * 2 + 2], AF.Copy),
                                waits=[ev_hcol])
                    ev_m1[k] = ev_last
                    ev_sload[k] = sload_val[k]
                    ev_uc[k] = dma("scalar", qM[k % 2], ucmb[k % 2][:], ufr[k],
                                   waits=[ev_comb[k - 2],
                                          ev_s_ready if k < 2 else None])

            comb_vals = [ev[1] for ev in ev_comb[:NCHUNK]]
            sready_val = ev_s_ready[1]

            def all_sloads(e):
                r0 = e.partition_id()
                ins = None
                for rr in nc.Switch(engines=[nc.sync], index=[r0], n=8):
                    j = rr % 4
                    bl, th = j >> 1, j & 1
                    e.wait_ge(cS.h, sready_val)
                    for kappa in range(NCHUNK):
                        if kappa >= 2:
                            e.wait_ge(vS.h, comb_vals[kappa - 2])
                        j1v = th * 32 + kappa
                        for hh in range(2):
                            for q in range(4):
                                ins = e.dma_start(
                                    out=stok[kappa % 2][:, (q * 2 + hh) * CHH:
                                                        (q * 2 + hh + 1) * CHH],
                                    in_=cag_s[hh, q, bl, :, j1v, :])
                                ins.then_inc(qL[kappa % 2].h, 16)
                return None

            em.op("sync", all_sloads)
            qL[0].v += (NCHUNK // 2) * 128
            qL[1].v += (NCHUNK // 2) * 128

            allq = [qS, qP, qG, qT2, qX1, qK] + qU + qX23 + qM + qO + qL
            em.op("sync", None, waits=[(q, q.v) for q in allq] + [(cS, cS.v)])

            block = stk.enter_context(nc.Block())
            em.replay(block)
        return nc


_NC_CACHE = {}


def kernel(L0_attn_u, L0_attn_k, L0_attn_skip, L0_ffn_w1, L0_ffn_conv_w,
           L0_ffn_conv_b, L0_ffn_w2):
    in_maps = shard_inputs(L0_attn_u, L0_attn_k, L0_attn_skip, L0_ffn_w1,
                           L0_ffn_conv_w, L0_ffn_conv_b, L0_ffn_w2)
    if "nc" not in _NC_CACHE:
        _NC_CACHE["nc"] = Builder().build()
    nc = _NC_CACHE["nc"]
    res = run_bass_kernel_spmd(nc, in_maps, core_ids=list(range(8)))
    out = np.empty((B, L, D), dtype=np.float32)
    for r in range(8):
        bq, th = r // 2, r % 2
        out[bq, th * TOK:(th + 1) * TOK, :] = res.results[r]["o_out"]
    return out
